# version 43
# baseline (speedup 1.0000x reference)
import numpy as np
import ml_dtypes
from contextlib import ExitStack

import concourse.bass as bass
import concourse.mybir as mybir
from concourse.bass_utils import run_bass_kernel_spmd

F32 = mybir.dt.float32
BF16 = mybir.dt.bfloat16
AX = mybir.AxisListType.X
AF = mybir.ActivationFunctionType
OP = mybir.AluOpType

B, S, C = 4, 4096, 1024
NCORES = 8
TOK = 2048
R = 2177
TT = 16
DTl = 17
CCH = 8
G = 512
GCH = 4
NE = 16
HTOT = 6144
NBLK = 8
HBLK = 768
HCB = 6
NPAIR = 8
INV_C = 1.0 / 1024.0
NEGBIG = -1e30


class Res:
    __slots__ = ("w", "r")

    def __init__(self):
        self.w = None
        self.r = []


class KB:
    ENG = ("tensor", "vector", "scalar", "gpsimd", "sync")
    NDSEM = 32

    def __init__(self):
        self.ops = {e: [] for e in self.ENG}
        self.barrier_tokens = None
        self.barrier_todo = set()
        self.dma_n = 0
        self.dma_uses = [0] * self.NDSEM

    def op(self, eng, emit, r=(), w=(), dma=False, deps=()):
        tok = (eng, len(self.ops[eng]))
        dl = list(deps)
        for x in r:
            if x.w is not None:
                dl.append(x.w)
        for x in w:
            if x.w is not None:
                dl.append(x.w)
            dl.extend(x.r)
        if self.barrier_tokens is not None and eng in self.barrier_todo:
            dl.extend(self.barrier_tokens)
            self.barrier_todo.discard(eng)
        if dma:
            idx = self.dma_n % self.NDSEM
            self.dma_n += 1
            self.dma_uses[idx] += 1
            dsem = (idx, 16 * self.dma_uses[idx])
        else:
            dsem = None
        self.ops[eng].append([emit, dl, dma, dsem])
        for x in r:
            x.r.append(tok)
        for x in w:
            x.w = tok
            x.r = []
        return tok

    def barrier(self):
        toks = []
        for e in self.ENG:
            if self.ops[e]:
                toks.append((e, len(self.ops[e]) - 1))
        self.barrier_tokens = toks
        self.barrier_todo = set(self.ENG)

    def finalize(self):
        self.need = {e: [False] * len(self.ops[e]) for e in self.ENG}
        for e in self.ENG:
            for _, deps, _, _ in self.ops[e]:
                for (pe, pi) in deps:
                    po = self.ops[pe][pi]
                    if po[2]:
                        continue
                    if pe == e == "tensor":
                        continue
                    self.need[pe][pi] = True
        self.incval = {}
        for e in self.ENG:
            c = 0
            iv = []
            for i, o in enumerate(self.ops[e]):
                if self.need[e][i]:
                    c += 1
                iv.append(c)
            self.incval[e] = iv

    def emit(self, block, sems, dsems, final):
        water = {e: {pe: 0 for pe in self.ENG} for e in self.ENG}
        dwater = {e: [0] * self.NDSEM for e in self.ENG}

        def mk(e):
            plan = self.ops[e]
            need = self.need[e]
            wtr = water[e]
            dwtr = dwater[e]

            def emitter(eng):
                for i, (emit, deps, dma, dsem) in enumerate(plan):
                    req = {}
                    dreq = {}
                    for (pe, pi) in deps:
                        po = self.ops[pe][pi]
                        if po[2]:
                            di, dv = po[3]
                            if dv > dreq.get(di, 0):
                                dreq[di] = dv
                            continue
                        if pe == e == "tensor":
                            continue
                        v = self.incval[pe][pi]
                        if v > req.get(pe, 0):
                            req[pe] = v
                    for pe, v in req.items():
                        if v > wtr[pe]:
                            eng.wait_ge(sems[pe], v)
                            wtr[pe] = v
                    for di, dv in dreq.items():
                        if dv > dwtr[di]:
                            eng.wait_ge(dsems[di], dv)
                            dwtr[di] = dv
                    inst = emit(eng)
                    if dma:
                        inst.then_inc(dsems[dsem[0]], 16)
                    elif need[i]:
                        inst.then_inc(sems[e], 1)
                if final:
                    for di in range(self.NDSEM):
                        tot = 16 * self.dma_uses[di]
                        if tot > 0 and dwtr[di] < tot:
                            eng.wait_ge(dsems[di], tot)
                            dwtr[di] = tot
            return emitter

        for e in self.ENG:
            if self.ops[e]:
                getattr(block, e)(mk(e))


def build_nc():
    nc = bass.Bass("TRN2", target_bir_lowering=False)
    kb = KB()

    emb = nc.dram_tensor("emb", [R, C], F32, kind="ExternalInput").ap()
    pwa_s = nc.dram_tensor("pwa_s", [128, 128], BF16, kind="ExternalInput").ap()
    pwa_g = nc.dram_tensor("pwa_g", [128, 128], BF16, kind="ExternalInput").ap()
    pwb = nc.dram_tensor("pwb", [7, 128], BF16, kind="ExternalInput").ap()
    idb_d = nc.dram_tensor("idb", [128, 128], BF16, kind="ExternalInput").ap()
    idf_d = nc.dram_tensor("idf", [128, 128], F32, kind="ExternalInput").ap()
    guid = nc.dram_tensor("guid", [G, C], F32, kind="ExternalInput").ap()
    guidT = nc.dram_tensor("guidT", [C, G], F32, kind="ExternalInput").ap()
    rT = nc.dram_tensor("rT", [C, NE], F32, kind="ExternalInput").ap()
    biasr = nc.dram_tensor("biasr", [1, NE], F32, kind="ExternalInput").ap()
    crow = nc.dram_tensor("crow", [1, 32], F32, kind="ExternalInput").ap()
    crow2 = nc.dram_tensor("crow2", [2, 256], F32, kind="ExternalInput").ap()
    wup = nc.dram_tensor("wup", [C, 2 * HTOT], F32, kind="ExternalInput").ap()
    wdn = nc.dram_tensor("wdn", [HTOT, C], F32, kind="ExternalInput").ap()
    out_d = nc.dram_tensor("out", [TOK, C], F32, kind="ExternalOutput").ap()

    srow_d = nc.dram_tensor("srow_scr", [1, G], BF16).ap()
    cbt_d = nc.dram_tensor("cbt_scr", [NE, TOK], BF16).ap()

    o_tiles = out_d.rearrange("(t p) c -> t p c", p=128)
    emb_r = emb

    def res():
        return Res()

    with ExitStack() as octx:
        def osb(name, shape, dtype):
            return octx.enter_context(nc.sbuf_tensor(name, shape, dtype))

        sems = {}
        for e in KB.ENG:
            sems[e] = octx.enter_context(nc.semaphore("sem_" + e))
        dsems = [octx.enter_context(nc.semaphore(f"dsem{i}"))
                 for i in range(KB.NDSEM)]

        X = osb("X", [128, TT * C], F32)
        idb = osb("idb_sb", [128, 128], BF16)
        idf = osb("idf_sb", [128, 128], F32)
        rtb = osb("rtb", [128, CCH * NE], F32)
        biasb = osb("biasb", [128, NE], F32)
        pwaS = osb("pwaS", [128, 128], BF16)
        pwaG = osb("pwaG", [128, 128], BF16)
        pwbS = osb("pwbS", [7, 128], BF16)
        CS2 = {nm: osb("cs_" + nm, [128, 20], F32) for nm in ("SUM2", "SQ2")}
        SB2 = {nm: osb("c_" + nm, [128, 20], F32)
               for nm in ("MS2", "V12", "T2S2", "RSTD2", "NMR2")}

        rX = [res() for _ in range(TT)]
        rgub = [res(), res()]
        rdnb = [res(), res()]
        ridb, rrtb, rbiasb = res(), res(), res()
        ridf = res()
        rpwaS, rpwaG, rpwbS = res(), res(), res()
        rCS2 = {nm: res() for nm in CS2}
        rSB2 = {nm: res() for nm in SB2}
        rsrow_d, rcbt_d = res(), res()

        with ExitStack() as g1:
            def sb1(name, shape, dtype):
                return g1.enter_context(nc.sbuf_tensor(name, shape, dtype))

            def ps1(name, shape, dtype):
                return g1.enter_context(nc.psum_tensor(name, shape, dtype))

            guidb = sb1("guidb", [128, GCH * C], BF16)
            guidTs = sb1("guidTs", [128, CCH * G], BF16)
            Sfull = sb1("Sfull", [128, G], BF16)
            WB = sb1("WBar", [128, DTl * C], BF16)
            mwcols = sb1("mwcols", [128, DTl], BF16)
            eb = [sb1(f"eb{p}", [128, C], F32) for p in (0, 1)]
            ebB = [sb1(f"ebB{p}", [128, C], F32) for p in (0, 1)]
            xbs = [sb1(f"xbs{p}", [128, C], F32) for p in (0, 1)]
            xnb = [sb1(f"xnb{p}", [128, C], F32) for p in (0, 1)]
            dbf6 = [sb1(f"dbf6_{p}", [128, C], F32) for p in range(6)]
            nb6 = [sb1(f"nb6_{p}", [128, C], BF16) for p in range(6)]
            nsb = [sb1(f"nsb{p}", [128, C], BF16) for p in (0, 1)]
            Eb = [sb1(f"Eb{p}", [128, G], BF16) for p in (0, 1)]
            etb = [sb1(f"etb{p}", [128, G], BF16) for p in (0, 1)]
            t2b = [sb1(f"t2b{p}", [128, C], BF16) for p in (0, 1)]
            sqb = [sb1(f"sqb{p}", [128, C], BF16) for p in (0, 1)]
            colsB = [sb1(f"colsB{p}", [128, 16], F32) for p in (0, 1)]
            colsS = [sb1(f"colsS{p}", [128, 16], F32) for p in (0, 1)]
            colsD0 = sb1("colsD0", [128, 132], F32)
            s4 = sb1("s4", [128, 4], F32)
            s4b = sb1("s4b", [128, 4], BF16)
            SLAB = {}
            for nm in ("SUMS", "SQS", "MS", "V1", "T2S", "RSTD", "NMR",
                       "RSTDS", "NMRS", "XS", "XQ",
                       "MAG2", "MC", "IVR", "INVS", "MAGT", "TMW", "T1S",
                       "RQS", "MWS", "ACO"):
                SLAB[nm] = sb1("sl_" + nm, [128, 20], F32)
            TSTG = sb1("tstg", [64, 128], F32)
            rTSTG = res()

            nT_ps = ps1("nT_ps", [128, C], BF16)
            dots_ps = ps1("dots_ps", [128, G], F32)
            ET_ps = ps1("ET_ps", [128, G], BF16)
            infl_ps = ps1("infl_ps", [128, C], F32)
            num_ps = ps1("num_ps", [128, C], F32)
            den_ps = ps1("den_ps", [128, 8], F32)

            rguidb, rguidTs, rSfull = res(), res(), res()
            rWB = [res() for _ in range(DTl)]
            reb = [res(), res()]
            rebB = [res(), res()]
            rxbs = [res(), res()]
            rxnb = [res(), res()]
            rdbf6 = [res() for _ in range(6)]
            rnb6 = [res() for _ in range(6)]
            rnsb = [res(), res()]
            rEb = [res(), res()]
            retb = [res(), res()]
            rt2b = [res(), res()]
            rsqb = [res(), res()]
            rs4, rs4b = res(), res()
            rnT, rdots, rET, rinfl, rnum, rden = (res(), res(), res(), res(),
                                                  res(), res())
            rcolsB = [res(), res()]
            rcolsS = [res(), res()]
            rcolsD = [res()]
            rSLAB = {nm: res() for nm in SLAB}
            rmwG = [res(), res(), res()]

            def mkcol(tensor):
                def col(i):
                    return tensor[:, 4 * i:4 * i + 1]
                return col

            def colv(i):
                return colsD0[:, 4 * i:4 * i + 16:4]

            colD0 = mkcol(colsD0)

            kb.op("scalar", lambda e: e.dma_start(out=idb[:], in_=idb_d),
                  w=[ridb], dma=True)
            kb.op("scalar", lambda e: e.dma_start(out=idf[:], in_=idf_d),
                  w=[ridf], dma=True)
            kb.op("scalar", lambda e: e.dma_start(out=pwaS[:], in_=pwa_s),
                  w=[rpwaS], dma=True)
            kb.op("scalar", lambda e: e.dma_start(out=pwaG[:], in_=pwa_g),
                  w=[rpwaG], dma=True)
            kb.op("scalar", lambda e: e.dma_start(out=pwbS[:], in_=pwb),
                  w=[rpwbS], dma=True)
            kb.op("scalar", lambda e: e.dma_start(
                out=biasb[:], in_=biasr.to_broadcast((128, NE))),
                w=[rbiasb], dma=True)
            kb.op("sync", lambda e: e.dma_start(
                out=rtb[:].rearrange("p (cc e) -> p cc e", cc=CCH),
                in_=rT.rearrange("(cc p) e -> p cc e", p=128)),
                w=[rrtb], dma=True)
            kb.op("gpsimd", lambda e: e.dma_start(
                out=guidb[:].rearrange("p (gc c) -> p gc c", gc=GCH),
                in_=guid.rearrange("(gc p) c -> p gc c", p=128)),
                w=[rguidb], dma=True)
            kb.op("gpsimd", lambda e: e.dma_start(
                out=guidTs[:].rearrange("p (cc g) -> p cc g", cc=CCH),
                in_=guidT.rearrange("(cc p) g -> p cc g", p=128)),
                w=[rguidTs], dma=True)
            for t in range(TT):
                kb.op("sync" if t % 2 == 0 else "scalar",
                      lambda e, t=t: e.dma_start(
                          out=X[:, t * C:(t + 1) * C],
                          in_=emb_r[8 + 128 * t: 8 + 128 * (t + 1), :]),
                      w=[rX[t]], dma=True)

            def newton_rsqrt(y_ap, v_ap, rres, t_ap, rextra=()):
                rr = [rres] + list(rextra)
                kb.op("vector", lambda e: e.tensor_mul(
                    out=t_ap, in0=y_ap, in1=y_ap), r=rr, w=rr)
                kb.op("vector", lambda e: e.scalar_tensor_tensor(
                    out=t_ap, in0=v_ap, scalar=-0.5, in1=t_ap,
                    op0=OP.mult, op1=OP.mult), r=rr, w=rr)
                kb.op("vector", lambda e: e.tensor_scalar(
                    out=t_ap, in0=t_ap, scalar1=1.5, scalar2=None,
                    op0=OP.add), r=rr, w=rr)
                kb.op("vector", lambda e: e.tensor_mul(
                    out=y_ap, in0=y_ap, in1=t_ap), r=rr, w=rr)

            def SL(nm, a=0, b=DTl):
                return SLAB[nm][:, a:b]

            def normedb_one(d):
                par6 = d % 6
                kb.op("scalar", lambda e, d=d, par6=par6: e.activation(
                    out=nb6[par6][:], in_=dbf6[par6][:], func=AF.Copy,
                    scale=SLAB["INVS"][:, d:d + 1]),
                    r=[rdbf6[par6], rSLAB["INVS"]], w=[rnb6[par6]])

            def s5_alpha(d):
                par = d % 2
                par6 = d % 6
                for cc in range(CCH):
                    kb.op("tensor", lambda e, cc=cc, par6=par6: e.transpose(
                        nT_ps[:, cc * 128:(cc + 1) * 128],
                        nb6[par6][:, cc * 128:(cc + 1) * 128], idb[:]),
                        r=[rnb6[par6], ridb], w=[rnT])
                kb.op("scalar", lambda e, par=par: e.activation(
                    out=nsb[par][:], in_=nT_ps[:], func=AF.Copy),
                    r=[rnT], w=[rnsb[par]])

            def s5_mid(d):
                par = d % 2
                cS = mkcol(colsS[par])
                rcS = rcolsS[par]
                for cc in range(CCH):
                    kb.op("tensor", lambda e, cc=cc, par=par: e.matmul(
                        dots_ps[:], nsb[par][:, cc * 128:(cc + 1) * 128],
                        guidTs[:, cc * G:(cc + 1) * G],
                        start=(cc == 0), stop=(cc == CCH - 1)),
                        r=[rnsb[par], rguidTs], w=[rdots])
                kb.op("vector", lambda e, cS=cS: e.reduce_max(
                    cS(0), dots_ps[:], axis=AX, negate=True),
                    r=[rdots], w=[rcS])
                kb.op("scalar", lambda e, par=par, cS=cS: e.activation(
                    out=Eb[par][:], in_=dots_ps[:], func=AF.Exp,
                    bias=cS(0)),
                    r=[rdots, rcS], w=[rEb[par]])

            def s5_fin(d):
                par = d % 2
                par6 = d % 6
                cS = mkcol(colsS[par])
                rcS = rcolsS[par]
                kb.op("vector", lambda e, par=par, cS=cS: e.reduce_sum(
                    cS(1), Eb[par][:], axis=AX), r=[rEb[par]], w=[rcS])
                kb.op("vector", lambda e, cS=cS: e.reciprocal(
                    out=cS(2), in_=cS(1)), r=[rcS], w=[rcS])
                kb.op("vector", lambda e, d=d, cS=cS: e.scalar_tensor_tensor(
                    out=cS(3), in0=SLAB["MWS"][:, d:d + 1], scalar=0.4,
                    in1=cS(2), op0=OP.mult, op1=OP.mult),
                    r=[rcS, rSLAB["MWS"]], w=[rcS])
                for gc in range(GCH):
                    kb.op("tensor", lambda e, gc=gc, par=par: e.transpose(
                        ET_ps[:, gc * 128:(gc + 1) * 128],
                        Eb[par][:, gc * 128:(gc + 1) * 128], idb[:]),
                        r=[rEb[par], ridb], w=[rET])
                kb.op("scalar", lambda e, par=par: e.activation(
                    out=etb[par][:], in_=ET_ps[:], func=AF.Copy),
                    r=[rET], w=[retb[par]])
                for gc in range(GCH):
                    for nh in range(2):
                        kb.op("tensor", lambda e, gc=gc, nh=nh, par=par:
                              e.matmul(
                                  infl_ps[:, nh * 512:(nh + 1) * 512],
                                  etb[par][:, gc * 128:(gc + 1) * 128],
                                  guidb[:, gc * C + nh * 512:
                                        gc * C + (nh + 1) * 512],
                                  start=(gc == 0), stop=(gc == GCH - 1)),
                              r=[retb[par], rguidb], w=[rinfl])
                kb.op("scalar", lambda e, par=par, cS=cS: e.activation(
                    out=t2b[par][:], in_=infl_ps[:], func=AF.Copy,
                    scale=cS(3)), r=[rinfl, rcS], w=[rt2b[par]])

            def s5_fin_b(d):
                par = d % 2
                par6 = d % 6
                kb.op("vector", lambda e, d=d, par6=par6, par=par:
                      e.scalar_tensor_tensor(
                          out=WB[:, d * C:(d + 1) * C], in0=nb6[par6][:],
                          scalar=SLAB["ACO"][:, d:d + 1], in1=t2b[par][:],
                          op0=OP.mult, op1=OP.add),
                      r=[rnb6[par6], rSLAB["ACO"], rt2b[par]], w=[rWB[d]])

            def batch_mag_p1(g0, g1):
                n = g1 - g0
                A = lambda nm: SLAB[nm][:, g0:g1]
                R = [rSLAB[nm] for nm in SLAB]

                def bop(fn, rd=(), wr=()):
                    kb.op("vector", fn, r=list(rd), w=list(wr))

                def sop(fn, rd=(), wr=()):
                    kb.op("scalar", fn, r=list(rd), w=list(wr))

                bop(lambda e: e.tensor_scalar(
                    out=A("MC"), in0=A("MAG2"), scalar1=1e-12, scalar2=None,
                    op0=OP.max), [rSLAB["MAG2"]], [rSLAB["MC"]])
                sop(lambda e: e.activation(out=A("T1S"), in_=A("MC"),
                                           func=AF.Ln),
                    [rSLAB["MC"]], [rSLAB["T1S"]])
                sop(lambda e: e.activation(out=A("IVR"), in_=A("T1S"),
                                           func=AF.Exp, scale=-0.5),
                    [rSLAB["T1S"]], [rSLAB["IVR"]])
                newton_rsqrt(A("IVR"), A("MC"), rSLAB["IVR"], A("T2S"),
                             rextra=[rSLAB["MC"], rSLAB["T2S"]])
                bop(lambda e: e.tensor_scalar(
                    out=A("T1S"), in0=A("MAG2"), scalar1=1e-12, scalar2=None,
                    op0=OP.is_gt), [rSLAB["MAG2"]], [rSLAB["T1S"]])
                bop(lambda e: e.tensor_mul(out=A("INVS"), in0=A("IVR"),
                                           in1=A("T1S")),
                    [rSLAB["IVR"], rSLAB["T1S"]], [rSLAB["INVS"]])

            def batch_mag_p2(g0, g1):
                A = lambda nm: SLAB[nm][:, g0:g1]

                def bop(fn, rd=(), wr=()):
                    kb.op("vector", fn, r=list(rd), w=list(wr))

                def sop(fn, rd=(), wr=()):
                    kb.op("scalar", fn, r=list(rd), w=list(wr))

                bop(lambda e: e.tensor_mul(out=A("MAGT"), in0=A("MC"),
                                           in1=A("IVR")),
                    [rSLAB["MC"], rSLAB["IVR"]], [rSLAB["MAGT"]])
                sop(lambda e: e.activation(out=A("TMW"), in_=A("MAGT"),
                                           func=AF.Exp, scale=-4.0),
                    [rSLAB["MAGT"]], [rSLAB["TMW"]])
                bop(lambda e: e.tensor_scalar(
                    out=A("T1S"), in0=A("TMW"), scalar1=-1.0, scalar2=1.0,
                    op0=OP.mult, op1=OP.add), [rSLAB["TMW"]], [rSLAB["T1S"]])
                bop(lambda e: e.tensor_scalar(
                    out=A("T2S"), in0=A("TMW"), scalar1=1.0, scalar2=None,
                    op0=OP.add), [rSLAB["TMW"]], [rSLAB["T2S"]])
                bop(lambda e: e.reciprocal(out=A("RQS"), in_=A("T2S")),
                    [rSLAB["T2S"]], [rSLAB["RQS"]])
                bop(lambda e: e.tensor_mul(out=A("MWS"), in0=A("T1S"),
                                           in1=A("RQS")),
                    [rSLAB["T1S"], rSLAB["RQS"]], [rSLAB["MWS"]])
                gidx = min(g0 // 6, 2)
                kb.op("vector", lambda e: e.tensor_copy(
                    out=mwcols[:, g0:g1], in_=A("MWS")),
                    r=[rSLAB["MWS"]], w=[rmwG[gidx]])
                bop(lambda e: e.tensor_scalar(
                    out=A("ACO"), in0=A("MWS"), scalar1=0.6, scalar2=None,
                    op0=OP.mult), [rSLAB["MWS"]], [rSLAB["ACO"]])

            def token_tile(tt_):
                colB = mkcol(colsB[tt_ % 2])
                rcB = rcolsB[tt_ % 2]
                pA = pwaS if tt_ == 0 else pwaG
                rpA = rpwaS if tt_ == 0 else rpwaG
                wb0 = WB[:, tt_ * C:(tt_ + 1) * C]
                wb1 = WB[0:7, (tt_ + 1) * C:(tt_ + 2) * C]
                for nh in range(2):
                    kb.op("tensor", lambda e, nh=nh, pA=pA, wb0=wb0: e.matmul(
                        num_ps[:, nh * 512:(nh + 1) * 512], pA[:],
                        wb0[:, nh * 512:(nh + 1) * 512],
                        start=True, stop=False),
                        r=[rpA, rWB[tt_]], w=[rnum])
                    kb.op("tensor", lambda e, nh=nh, wb1=wb1: e.matmul(
                        num_ps[:, nh * 512:(nh + 1) * 512], pwbS[:],
                        wb1[:, nh * 512:(nh + 1) * 512],
                        start=False, stop=True),
                        r=[rpwbS, rWB[tt_ + 1]], w=[rnum])
                kb.op("tensor", lambda e, pA=pA, tt_=tt_: e.matmul(
                    den_ps[:, 0:1], pA[:], mwcols[:, tt_:tt_ + 1],
                    start=True, stop=False),
                    r=[rpA, rmwG[min(tt_ // 6, 2)]], w=[rden])
                kb.op("tensor", lambda e, tt_=tt_: e.matmul(
                    den_ps[:, 0:1], pwbS[:], mwcols[0:7, tt_ + 1:tt_ + 2],
                    start=False, stop=True),
                    r=[rpwbS, rmwG[min((tt_ + 1) // 6, 2)]], w=[rden])
                kb.op("vector", lambda e: e.tensor_scalar(
                    out=colB(0), in0=den_ps[:, 0:1], scalar1=1e-8,
                    scalar2=None, op0=OP.max), r=[rden], w=[rcB])
                kb.op("vector", lambda e: e.reciprocal(
                    out=colB(1), in_=colB(0)), r=[rcB], w=[rcB])
                xt = X[:, tt_ * C:(tt_ + 1) * C]
                kb.op("vector", lambda e, xt=xt: e.scalar_tensor_tensor(
                    out=xt, in0=num_ps[:], scalar=colB(1), in1=xt,
                    op0=OP.mult, op1=OP.add),
                    r=[rnum, rcB, rX[tt_]], w=[rX[tt_]])

            kb.op("sync", lambda e: e.dma_start(
                out=eb[0][0:8, :], in_=emb_r[0:8, :]),
                w=[reb[0]], dma=True)
            kb.op("gpsimd", lambda e: e.tensor_tensor(
                out=xnb[0][0:8, :], in0=eb[0][0:8, :], in1=eb[0][0:8, :],
                op=OP.mult), r=[reb[0]], w=[rxnb[0]])
            for t in range(TT):
                par = t % 2
                kb.op("vector", lambda e, t=t: e.reduce_sum(
                    SLAB["XS"][:, t:t + 1], X[:, t * C:(t + 1) * C], axis=AX),
                    r=[rX[t]], w=[rSLAB["XS"]])
                kb.op("gpsimd", lambda e, t=t, par=par: e.tensor_tensor(
                    out=sqb[par][:], in0=X[:, t * C:(t + 1) * C],
                    in1=X[:, t * C:(t + 1) * C], op=OP.mult),
                    r=[rX[t]], w=[rsqb[par]])
                kb.op("vector", lambda e, t=t, par=par: e.reduce_sum(
                    SLAB["XQ"][:, t:t + 1], sqb[par][:], axis=AX),
                    r=[rsqb[par]], w=[rSLAB["XQ"]])
            dtv = dots_ps[:]
            itv = infl_ps[:]
            ntv = num_ps[:]
            kb.op("tensor", lambda e: e.transpose(
                dtv[0:17, 0:128], SLAB["XS"][:, 0:17], idf[:]),
                r=[rSLAB["XS"], ridf], w=[rdots])
            kb.op("tensor", lambda e: e.transpose(
                ntv[0:17, 0:128], SLAB["XQ"][:, 0:17], idf[:]),
                r=[rSLAB["XQ"], ridf], w=[rnum])
            kb.op("vector", lambda e: e.tensor_copy(
                out=TSTG[0:17, 8:128], in_=dtv[0:17, 0:120]),
                r=[rdots], w=[rTSTG])
            kb.op("vector", lambda e: e.tensor_copy(
                out=TSTG[32:49, 8:128], in_=ntv[0:17, 0:120]),
                r=[rnum], w=[rTSTG])
            kb.op("tensor", lambda e: e.transpose(
                itv[0:128, 0:17], TSTG[0:17, 0:128], idf[0:17, 0:17]),
                r=[rTSTG, ridf], w=[rinfl])
            kb.op("tensor", lambda e: e.transpose(
                itv[0:128, 17:34], TSTG[32:49, 0:128],
                idf[32:49, 32:49]),
                r=[rTSTG, ridf], w=[rinfl])
            kb.op("scalar", lambda e: e.activation(
                out=SLAB["SUMS"][:, 0:17], in_=itv[:, 0:17], func=AF.Copy),
                r=[rinfl], w=[rSLAB["SUMS"]])
            kb.op("scalar", lambda e: e.activation(
                out=SLAB["SQS"][:, 0:17], in_=itv[:, 17:34], func=AF.Copy),
                r=[rinfl], w=[rSLAB["SQS"]])
            kb.op("sync", lambda e: e.dma_start(
                out=SLAB["SUMS"][0:8, 1:TT + 1],
                in_=SLAB["XS"][120:128, 0:TT]),
                r=[rSLAB["XS"]], w=[rSLAB["SUMS"]], dma=True)
            kb.op("vector", lambda e: e.reduce_sum(
                SLAB["SUMS"][0:8, 0:1], eb[0][0:8, :], axis=AX),
                r=[reb[0]], w=[rSLAB["SUMS"]])
            kb.op("vector", lambda e: e.reduce_sum(
                SLAB["SQS"][0:8, 0:1], xnb[0][0:8, :], axis=AX),
                r=[rxnb[0]], w=[rSLAB["SQS"]])
            for gc in range(GCH):
                gslice = guidb[:, gc * C:(gc + 1) * C]
                gp = gc % 2
                kb.op("gpsimd", lambda e, gs_=gslice, gp=gp: e.tensor_tensor(
                    out=sqb[gp][:], in0=gs_, in1=gs_, op=OP.mult),
                    r=[rguidb], w=[rsqb[gp]])
                kb.op("vector", lambda e, gc=gc, gp=gp: e.reduce_sum(
                    colD0(28 + gc), sqb[gp][:], axis=AX),
                    r=[rsqb[gp]], w=[rcolsD[0]])
            kb.op("vector", lambda e: e.tensor_scalar(
                out=colv(28), in0=colv(28), scalar1=1e-16,
                scalar2=None, op0=OP.max), r=[rcolsD[0]], w=[rcolsD[0]])
            kb.op("scalar", lambda e: e.activation(
                out=s4[:], in_=colv(28), func=AF.Ln),
                r=[rcolsD[0]], w=[rs4])
            kb.op("scalar", lambda e: e.activation(
                out=s4[:], in_=s4[:], func=AF.Exp, scale=-0.5),
                r=[rs4], w=[rs4])
            kb.op("vector", lambda e: e.tensor_mul(
                out=colv(24), in0=s4[:], in1=s4[:]),
                r=[rs4, rcolsD[0]], w=[rcolsD[0]])
            kb.op("vector", lambda e: e.scalar_tensor_tensor(
                out=colv(24), in0=colv(28), scalar=-0.5,
                in1=colv(24), op0=OP.mult, op1=OP.mult),
                r=[rcolsD[0]], w=[rcolsD[0]])
            kb.op("vector", lambda e: e.tensor_scalar(
                out=colv(24), in0=colv(24), scalar1=1.5,
                scalar2=None, op0=OP.add), r=[rcolsD[0]], w=[rcolsD[0]])
            kb.op("vector", lambda e: e.tensor_mul(
                out=s4[:], in0=s4[:], in1=colv(24)),
                r=[rs4, rcolsD[0]], w=[rs4])
            kb.op("vector", lambda e: e.tensor_scalar(
                out=s4b[:], in0=s4[:], scalar1=2.0, scalar2=None,
                op0=OP.mult), r=[rs4], w=[rs4b])
            kb.op("tensor", lambda e: e.transpose(
                nT_ps[0:4, 0:128], s4b[:], idb[:]),
                r=[rs4b, ridb], w=[rnT])
            kb.op("scalar", lambda e: e.activation(
                out=Sfull[0:4, 0:128], in_=nT_ps[0:4, 0:128], func=AF.Copy),
                r=[rnT], w=[rSfull])
            for gc in range(GCH):
                kb.op("scalar", lambda e, gc=gc: e.dma_start(
                    out=srow_d[0:1, gc * 128:(gc + 1) * 128],
                    in_=Sfull[gc:gc + 1, 0:128]),
                    r=[rSfull], w=[rsrow_d], dma=True)
            kb.op("scalar", lambda e: e.dma_start(
                out=Sfull[:], in_=srow_d.to_broadcast((128, G))),
                r=[rsrow_d], w=[rSfull], dma=True)
            for cc in range(CCH):
                sl = guidTs[:, cc * G:(cc + 1) * G]
                kb.op("vector", lambda e, sl=sl: e.tensor_mul(
                    out=sl, in0=sl, in1=Sfull[:]),
                    r=[rSfull, rguidTs], w=[rguidTs])

            kb.op("scalar", lambda e: e.dma_start(
                out=SLAB["SQS"][0:8, 1:TT + 1],
                in_=SLAB["XQ"][120:128, 0:TT]),
                r=[rSLAB["XQ"]], w=[rSLAB["SQS"]], dma=True)

            kb.op("vector", lambda e: e.tensor_scalar(
                out=SL("MS"), in0=SL("SUMS"), scalar1=INV_C, scalar2=None,
                op0=OP.mult), r=[rSLAB["SUMS"]], w=[rSLAB["MS"]])
            kb.op("vector", lambda e: e.tensor_mul(
                out=SL("T2S"), in0=SL("MS"), in1=SL("MS")),
                r=[rSLAB["MS"]], w=[rSLAB["T2S"]])
            kb.op("vector", lambda e: e.tensor_scalar(
                out=SL("V1"), in0=SL("SQS"), scalar1=INV_C, scalar2=None,
                op0=OP.mult), r=[rSLAB["SQS"]], w=[rSLAB["V1"]])
            kb.op("vector", lambda e: e.tensor_sub(
                out=SL("V1"), in0=SL("V1"), in1=SL("T2S")),
                r=[rSLAB["V1"], rSLAB["T2S"]], w=[rSLAB["V1"]])
            kb.op("vector", lambda e: e.tensor_scalar(
                out=SL("V1"), in0=SL("V1"), scalar1=1e-5, scalar2=None,
                op0=OP.add), r=[rSLAB["V1"]], w=[rSLAB["V1"]])
            kb.op("scalar", lambda e: e.activation(
                out=SL("RSTD"), in_=SL("V1"), func=AF.Ln),
                r=[rSLAB["V1"]], w=[rSLAB["RSTD"]])
            kb.op("scalar", lambda e: e.activation(
                out=SL("RSTD"), in_=SL("RSTD"), func=AF.Exp, scale=-0.5),
                r=[rSLAB["RSTD"]], w=[rSLAB["RSTD"]])
            newton_rsqrt(SL("RSTD"), SL("V1"), rSLAB["RSTD"], SL("T2S"),
                         rextra=[rSLAB["V1"], rSLAB["T2S"]])
            kb.op("vector", lambda e: e.scalar_tensor_tensor(
                out=SL("NMR"), in0=SL("MS"), scalar=-1.0, in1=SL("RSTD"),
                op0=OP.mult, op1=OP.mult),
                r=[rSLAB["MS"], rSLAB["RSTD"]], w=[rSLAB["NMR"]])
            kb.op("tensor", lambda e: e.transpose(
                dtv[0:17, 0:128], SLAB["RSTD"][:, 0:17], idf[:]),
                r=[rSLAB["RSTD"], ridf], w=[rdots])
            kb.op("tensor", lambda e: e.transpose(
                ntv[0:17, 0:128], SLAB["NMR"][:, 0:17], idf[:]),
                r=[rSLAB["NMR"], ridf], w=[rnum])
            kb.op("vector", lambda e: e.tensor_copy(
                out=TSTG[0:17, 0:127], in_=dtv[0:17, 1:128]),
                r=[rdots], w=[rTSTG])
            kb.op("vector", lambda e: e.tensor_copy(
                out=TSTG[32:49, 0:127], in_=ntv[0:17, 1:128]),
                r=[rnum], w=[rTSTG])
            kb.op("tensor", lambda e: e.transpose(
                itv[0:128, 0:17], TSTG[0:17, 0:128], idf[0:17, 0:17]),
                r=[rTSTG, ridf], w=[rinfl])
            kb.op("tensor", lambda e: e.transpose(
                itv[0:128, 17:34], TSTG[32:49, 0:128],
                idf[32:49, 32:49]),
                r=[rTSTG, ridf], w=[rinfl])
            kb.op("scalar", lambda e: e.activation(
                out=SLAB["RSTDS"][:, 0:17], in_=itv[:, 0:17], func=AF.Copy),
                r=[rinfl], w=[rSLAB["RSTDS"]])
            kb.op("scalar", lambda e: e.activation(
                out=SLAB["NMRS"][:, 0:17], in_=itv[:, 17:34], func=AF.Copy),
                r=[rinfl], w=[rSLAB["NMRS"]])
            for nm in ("RSTD", "NMR"):
                kb.op("scalar", lambda e, nm=nm: e.dma_start(
                    out=SLAB[nm + "S"][127:128, 0:DTl - 1],
                    in_=SLAB[nm][0:1, 1:DTl]),
                    r=[rSLAB[nm]], w=[rSLAB[nm + "S"]], dma=True)

            def stats_load(d):
                par = d % 2
                kb.op("sync", lambda e, d=d, par=par: e.dma_start(
                    out=eb[par][:], in_=emb_r[128 * d:128 * (d + 1), :]),
                    w=[reb[par]], dma=True)
                kb.op("sync", lambda e, d=d, par=par: e.dma_start(
                    out=ebB[par][:],
                    in_=emb_r[128 * d + 1:128 * (d + 1) + 1, :]),
                    w=[rebB[par]], dma=True)

            def stats_comp_a(d):
                par = d % 2
                sl6 = d % 6
                kb.op("vector", lambda e, d=d, par=par: e.tensor_scalar(
                    out=xnb[par][:], in0=eb[par][:],
                    scalar1=SLAB["RSTD"][:, d:d + 1],
                    scalar2=SLAB["NMR"][:, d:d + 1],
                    op0=OP.mult, op1=OP.add),
                    r=[reb[par], rSLAB["RSTD"], rSLAB["NMR"]],
                    w=[rxnb[par]])
                kb.op("vector", lambda e, d=d, par=par: e.tensor_scalar(
                    out=xbs[par][:], in0=ebB[par][:],
                    scalar1=SLAB["RSTDS"][:, d:d + 1],
                    scalar2=SLAB["NMRS"][:, d:d + 1],
                    op0=OP.mult, op1=OP.add),
                    r=[rebB[par], rSLAB["RSTDS"], rSLAB["NMRS"]],
                    w=[rxbs[par]])
                kb.op("gpsimd", lambda e, par=par, sl6=sl6: e.tensor_tensor(
                    out=dbf6[sl6][:], in0=xbs[par][:], in1=xnb[par][:],
                    op=OP.subtract),
                    r=[rxbs[par], rxnb[par]], w=[rdbf6[sl6]])
                kb.op("gpsimd", lambda e, sl6=sl6, par=par: e.tensor_tensor(
                    out=sqb[par][:], in0=dbf6[sl6][:], in1=dbf6[sl6][:],
                    op=OP.mult),
                    r=[rdbf6[sl6]], w=[rsqb[par]])

            def stats_comp_b(d):
                par = d % 2
                kb.op("vector", lambda e, d=d, par=par: e.reduce_sum(
                    SLAB["MAG2"][:, d:d + 1], sqb[par][:], axis=AX),
                    r=[rsqb[par]], w=[rSLAB["MAG2"]])

            def stats_comp(d):
                stats_comp_a(d)
                stats_comp_b(d)

            c1_done = [False] * TT

            def c1_stat(tt_):
                srcX = X[:, tt_ * C:(tt_ + 1) * C]
                par = (tt_ + 1) % 2
                kb.op("vector", lambda e, srcX=srcX, tt_=tt_: e.reduce_sum(
                    CS2["SUM2"][:, tt_:tt_ + 1], srcX, axis=AX),
                    r=[rX[tt_]], w=[rCS2["SUM2"]])
                kb.op("gpsimd", lambda e, srcX=srcX, par=par: e.tensor_tensor(
                    out=sqb[par][:], in0=srcX, in1=srcX, op=OP.mult),
                    r=[rX[tt_]], w=[rsqb[par]])
                kb.op("vector", lambda e, tt_=tt_, par=par: e.reduce_sum(
                    CS2["SQ2"][:, tt_:tt_ + 1], sqb[par][:], axis=AX),
                    r=[rsqb[par]], w=[rCS2["SQ2"]])
                c1_done[tt_] = True

            nb_hi = -1
            mag_hi = -1

            def ensure_nb(upto):
                nonlocal nb_hi
                for d in range(nb_hi + 1, min(upto, mag_hi) + 1):
                    normedb_one(d)
                    nb_hi = d

            stats_load(0)
            stats_load(1)
            stats_comp(0)
            stats_load(2)
            stats_comp(1)
            stats_load(3)
            batch_mag_p1(0, 2)
            batch_mag_p2(0, 2)
            mag_hi = 1
            ensure_nb(1)
            s5_alpha(0)
            p1sched = {0: (2, 4), 1: (4, 6), 3: (6, 8), 5: (8, 10),
                       7: (10, 12), 9: (12, 14), 11: (14, 16),
                       12: (16, DTl)}
            p2sched = {i + 1: v for i, v in p1sched.items()}
            compsched = {0: [2, 3], 1: [4, 5]}
            loadsched = {0: [4, 5], 1: [6, 7]}
            for i in range(2, 13):
                compsched[i] = [4 + i]
                if 6 + i <= DTl - 1:
                    loadsched[i] = [6 + i]
            for i in range(0, DTl + 1):
                for d in compsched.get(i, []):
                    stats_comp_a(d)
                for d in loadsched.get(i, []):
                    stats_load(d)
                ensure_nb(i + 2)
                if i + 1 <= nb_hi:
                    s5_alpha(i + 1)
                if i < DTl:
                    s5_mid(i)
                if 2 <= i and i - 2 < TT:
                    token_tile(i - 2)
                if i < DTl:
                    s5_fin(i)
                for d in compsched.get(i, []):
                    stats_comp_b(d)
                if 2 <= i and i - 2 < TT:
                    c1_stat(i - 2)
                if i < DTl:
                    s5_fin_b(i)
                if i in p1sched:
                    g0m, g1m = p1sched[i]
                    batch_mag_p1(g0m, g1m)
                    mag_hi = g1m - 1
                if i in p2sched:
                    batch_mag_p2(*p2sched[i])

            def SL2(nm):
                return SB2[nm][:, 0:TT]
            kb.op("vector", lambda e: e.tensor_scalar(
                out=SL2("MS2"), in0=CS2["SUM2"][:, 0:TT], scalar1=INV_C,
                scalar2=None,
                op0=OP.mult), r=[rCS2["SUM2"]], w=[rSB2["MS2"]])
            kb.op("vector", lambda e: e.tensor_mul(
                out=SL2("T2S2"), in0=SL2("MS2"), in1=SL2("MS2")),
                r=[rSB2["MS2"]], w=[rSB2["T2S2"]])
            kb.op("vector", lambda e: e.tensor_scalar(
                out=SL2("V12"), in0=CS2["SQ2"][:, 0:TT], scalar1=INV_C,
                scalar2=None,
                op0=OP.mult), r=[rCS2["SQ2"]], w=[rSB2["V12"]])
            kb.op("vector", lambda e: e.tensor_sub(
                out=SL2("V12"), in0=SL2("V12"), in1=SL2("T2S2")),
                r=[rSB2["V12"], rSB2["T2S2"]], w=[rSB2["V12"]])
            kb.op("vector", lambda e: e.tensor_scalar(
                out=SL2("V12"), in0=SL2("V12"), scalar1=1e-5, scalar2=None,
                op0=OP.add), r=[rSB2["V12"]], w=[rSB2["V12"]])
            kb.op("scalar", lambda e: e.activation(
                out=SL2("RSTD2"), in_=SL2("V12"), func=AF.Ln),
                r=[rSB2["V12"]], w=[rSB2["RSTD2"]])
            kb.op("scalar", lambda e: e.activation(
                out=SL2("RSTD2"), in_=SL2("RSTD2"), func=AF.Exp, scale=-0.5),
                r=[rSB2["RSTD2"]], w=[rSB2["RSTD2"]])
            newton_rsqrt(SL2("RSTD2"), SL2("V12"), rSB2["RSTD2"],
                         SL2("T2S2"), rextra=[rSB2["V12"], rSB2["T2S2"]])
            kb.op("vector", lambda e: e.scalar_tensor_tensor(
                out=SL2("NMR2"), in0=SL2("MS2"), scalar=-1.0,
                in1=SL2("RSTD2"), op0=OP.mult, op1=OP.mult),
                r=[rSB2["MS2"], rSB2["RSTD2"]], w=[rSB2["NMR2"]])


        kb.barrier()
        with ExitStack() as g2:
            def sb2(name, shape, dtype):
                return g2.enter_context(nc.sbuf_tensor(name, shape, dtype))

            def ps2(name, shape, dtype):
                return g2.enter_context(nc.psum_tensor(name, shape, dtype))

            gub = [sb2(f"gub{p}", [128, CCH * 1536], BF16) for p in (0, 1)]
            dnb = [sb2(f"dnb{p}", [128, HCB * C], BF16) for p in (0, 1)]

            def w_dma(b):
                par = b % 2
                kb.op("gpsimd", lambda e, b=b, par=par: e.dma_start(
                    out=gub[par][:].rearrange("p (cc h) -> p cc h", cc=CCH),
                    in_=wup.rearrange("(cc p) H -> p cc H", p=128)
                        [:, :, b * 1536:(b + 1) * 1536]),
                    w=[rgub[par]], dma=True)
                kb.op("gpsimd", lambda e, b=b, par=par: e.dma_start(
                    out=dnb[par][:].rearrange("p (hc c) -> p hc c", hc=HCB),
                    in_=wdn.rearrange("(hc p) c -> p hc c", p=128)
                        [:, b * HCB:(b + 1) * HCB, :]),
                    w=[rdnb[par]], dma=True)

            w_dma(0)
            w_dma(1)

            XT = sb2("XT", [128, CCH * TOK], BF16)
            xn2b = [sb2(f"xn2b{p}", [128, C], BF16) for p in (0, 1)]
            cbT = sb2("cbT", [16, TOK], BF16)
            cbB = [sb2(f"cbB{i}", [128, TOK], BF16) for i in range(6)]
            sg = [sb2(f"sg{p}", [128, 256], F32) for p in (0, 1)]
            hb = [sb2(f"hb{p}", [128, 256], BF16) for p in (0, 1)]
            hbb = [sb2(f"hbb{p}", [128, 256], BF16) for p in (0, 1)]

            acc_ps = [ps2(f"acc{p}", [128, C], F32) for p in (0, 1)]
            g_ps = [ps2(f"gps{p}", [128, 256], F32) for p in (0, 1)]
            u_ps = [ps2(f"ups{p}", [128, 256], F32) for p in (0, 1)]

            XTv = XT[:].rearrange("p (cc t) -> p cc t", cc=CCH)
            rXT = [res() for _ in range(TT)]
            rxn2b = [res(), res()]
            rcbT = res()
            rcbB = [res() for _ in range(6)]
            rsg = [res(), res()]
            rhb = [res(), res()]
            rhbb = [res(), res()]
            racc = [res(), res()]
            rg = [res(), res()]
            ru = [res(), res()]
            cb0f = cbB[0][:].bitcast(F32)
            cb1f = cbB[1][:].bitcast(F32)
            cb2f = cbB[2][:].bitcast(F32)
            LG = cb0f[:, 0:256]
            SCC = cb0f[:, 256:512]
            SBA = cb0f[:, 512:768]
            WRK = cb0f[:, 768:1024]
            GTA = cb1f[:, 64:128]
            GTB = cb1f[:, 128:192]
            GSA = cb1f[:, 0:64]
            T2C = cb1f[:, 192:208]
            MKC = cb1f[:, 208:224]
            THR = cb1f[:, 224:240]
            WSC = cb1f[:, 240:256]
            GMF = cb1f[:, 272:336]
            CMB = cb1f[:, 336:592]
            KMF = cb1f[:, 592:848]
            COMBB = cbB[2][:, 0:256]
            GMU8 = cbB[2][:].bitcast(mybir.dt.uint8)[:, 512:768]
            BIAS16 = cb2f[:, 256:512]
            NEGB = cb2f[:, 512:768]
            rcb0, rcb1, rcb2 = rcbB[0], rcbB[1], rcbB[2]
            rLGh = [res(), res()]
            rRH = [res(), res()]
            xn2f = cbB[4][:].bitcast(F32)[:, 0:C]
            rxn2f = rcbB[4]
            xn2fT = [cbB[5][:].bitcast(F32)[:, 0:C],
                     cbB[3][:].bitcast(F32)[:, 0:C]]
            rxn2fT = [rcbB[5], rcbB[3]]
            xtf_ps = acc_ps[0][:]

            xt_ps = acc_ps[0][:].bitcast(BF16)[:, 0:C]
            cbtr_ps = acc_ps[1][:].bitcast(BF16)[0:16, 0:128]
            lgt_ps = g_ps[0][:, 0:NE]

            kb.op("sync", lambda e: e.dma_start(
                out=BIAS16, in_=crow2[0:1, :].to_broadcast((128, 256))),
                w=[rcb2], dma=True)
            kb.op("sync", lambda e: e.dma_start(
                out=NEGB, in_=crow2[1:2, :].to_broadcast((128, 256))),
                w=[rcb2], dma=True)

            xtf_v = [acc_ps[0][:], acc_ps[1][:]]
            xt_v = [acc_ps[0][:].bitcast(BF16)[:, 0:C],
                    acc_ps[1][:].bitcast(BF16)[:, 0:C]]

            def c2_stage1(tt_):
                srcX = X[:, tt_ * C:(tt_ + 1) * C]
                par = tt_ % 2
                kb.op("vector", lambda e, srcX=srcX, tt_=tt_: e.tensor_scalar(
                    out=xn2f, in0=srcX,
                    scalar1=SB2["RSTD2"][:, tt_:tt_ + 1],
                    scalar2=SB2["NMR2"][:, tt_:tt_ + 1],
                    op0=OP.mult, op1=OP.add),
                    r=[rX[tt_], rSB2["RSTD2"], rSB2["NMR2"]], w=[rxn2f])
                kb.op("vector", lambda e, par=par: e.tensor_copy(
                    out=xn2b[par][:], in_=xn2f), r=[rxn2f], w=[rxn2b[par]])

            def c2_fpT(tt_):
                par = tt_ % 2
                for cc in range(CCH):
                    kb.op("tensor", lambda e, cc=cc, par=par: e.transpose(
                        xtf_v[par][:, cc * 128:(cc + 1) * 128],
                        xn2f[:, cc * 128:(cc + 1) * 128], idf[:]),
                        r=[rxn2f, ridf], w=[racc[par]])
                kb.op("scalar", lambda e, par=par: e.activation(
                    out=xn2fT[par], in_=xtf_v[par], func=AF.Copy),
                    r=[racc[par]], w=[rxn2fT[par]])

            def c2_bfT(tt_):
                par = tt_ % 2
                for cc in range(CCH):
                    kb.op("tensor", lambda e, cc=cc, par=par: e.transpose(
                        xt_v[par][:, cc * 128:(cc + 1) * 128],
                        xn2b[par][:, cc * 128:(cc + 1) * 128], idb[:]),
                        r=[rxn2b[par], ridb], w=[racc[par]])
                kb.op("scalar", lambda e, tt_=tt_, par=par: e.activation(
                    out=XTv[:, :, tt_ * 128:(tt_ + 1) * 128],
                    in_=xt_v[par][:].rearrange("p (cc t) -> p cc t", cc=CCH),
                    func=AF.Copy), r=[racc[par]], w=[rXT[tt_]])

            def c2_router(tt_):
                par = tt_ % 2
                for cc in range(CCH):
                    kb.op("tensor", lambda e, cc=cc, par=par: e.matmul(
                        lgt_ps, xn2fT[par][:, cc * 128:(cc + 1) * 128],
                        rtb[:, cc * NE:(cc + 1) * NE],
                        start=(cc == 0), stop=(cc == CCH - 1)),
                        r=[rxn2fT[par], rrtb], w=[rg[0]])
                kb.op("scalar", lambda e, tt_=tt_: e.activation(
                    out=LG[:, NE * tt_:NE * (tt_ + 1)], in_=lgt_ps,
                    func=AF.Copy), r=[rg[0]], w=[rLGh[tt_ // 8]])

            def router_half_ops(h):
                TH = TT // 2
                c0, c1 = h * 128, (h + 1) * 128
                LGh = LG[:, c0:c1]
                SCCh = SCC[:, c0:c1]
                SBAh = SBA[:, c0:c1]
                WRKh = WRK[:, c0:c1]
                CMBh = CMB[:, c0:c1]
                KMFh = KMF[:, c0:c1]
                COMBBh = COMBB[:, c0:c1]
                GMU8h = GMU8[:, c0:c1]
                BIAS16h = BIAS16[:, c0:c1]
                NEGBh = NEGB[:, c0:c1]
                GSAh = GSA[:, h * 32:(h + 1) * 32]
                GTAh = GTA[:, h * 32:(h + 1) * 32]
                GTBh = GTB[:, h * 32:(h + 1) * 32]
                GMFh = GMF[:, h * 32:(h + 1) * 32]
                T2Ch = T2C[:, h * 8:(h + 1) * 8]
                MKCh = MKC[:, h * 8:(h + 1) * 8]
                THRh = THR[:, h * 8:(h + 1) * 8]
                WSCh = WSC[:, h * 8:(h + 1) * 8]
                ops = []

                def rop(eng, fn):
                    ops.append(lambda eng=eng, fn=fn: kb.op(
                        eng, fn, r=[rLGh[h], rRH[h], rcb2],
                        w=[rLGh[h], rRH[h]]))

                def v3(ap):
                    return ap.rearrange("p (t e) -> p t e", e=NE)

                def g3(ap):
                    return ap.rearrange("p (t g) -> p t g", g=4)

                def bct(ap, n):
                    return ap.rearrange("p (t o) -> p t o", o=1).broadcast_to(
                        (128, TH, n))

                rop("scalar", lambda e: e.activation(
                    out=SCCh, in_=LGh, func=AF.Exp, scale=-1.0))
                rop("vector", lambda e: e.tensor_scalar(
                    out=SCCh, in0=SCCh, scalar1=1.0, scalar2=None, op0=OP.add))
                rop("vector", lambda e: e.reciprocal(out=LGh, in_=SCCh))
                rop("vector", lambda e: e.tensor_add(
                    out=SBAh, in0=LGh, in1=BIAS16h))
                sl = [SBAh[:, rr:128:4] for rr in range(4)]
                pairs = [(0, 1), (0, 2), (0, 3), (1, 2), (1, 3), (2, 3)]
                first = True
                for (a_, b_) in pairs:
                    dst = GTAh if first else GTBh
                    rop("vector", lambda e, a_=a_, b_=b_, dst=dst: e.tensor_add(
                        out=dst, in0=sl[a_], in1=sl[b_]))
                    if not first:
                        rop("vector", lambda e: e.tensor_max(
                            out=GTAh, in0=GTAh, in1=GTBh))
                    first = False
                rop("vector", lambda e: e.tensor_copy(out=GSAh, in_=GTAh))
                gl = [GSAh[:, gg:32:4] for gg in range(4)]
                first = True
                for (a_, b_) in pairs:
                    dst = T2Ch if first else MKCh
                    rop("vector", lambda e, a_=a_, b_=b_, dst=dst:
                        e.tensor_tensor(
                            out=dst, in0=gl[a_], in1=gl[b_], op=OP.min))
                    if not first:
                        rop("vector", lambda e: e.tensor_max(
                            out=T2Ch, in0=T2Ch, in1=MKCh))
                    first = False
                rop("vector", lambda e: e.tensor_tensor(
                    out=g3(GMFh), in0=g3(GSAh),
                    in1=T2Ch.rearrange("p (t o) -> p t o", o=1).broadcast_to(
                        (128, TH, 4)), op=OP.is_ge))
                rop("vector", lambda e: e.tensor_copy(
                    out=GMU8h.rearrange("p (t g r) -> p t g r", g=4, r=4),
                    in_=g3(GMFh).rearrange("p t (g o) -> p t g o", o=1)
                        .broadcast_to((128, TH, 4, 4))))
                rop("vector", lambda e: e.select(
                    out=WRKh, mask=GMU8h, on_true=SBAh, on_false=NEGBh))
                rop("vector", lambda e: e.tensor_copy(out=SCCh, in_=WRKh))
                for _ in range(3):
                    rop("vector", lambda e: e.reduce_max(
                        MKCh, v3(SCCh), axis=AX))
                    rop("vector", lambda e: e.tensor_tensor(
                        out=v3(CMBh), in0=v3(SCCh), in1=bct(MKCh, NE),
                        op=OP.is_ge))
                    rop("vector", lambda e: e.scalar_tensor_tensor(
                        out=SCCh, in0=CMBh, scalar=-1e30, in1=SCCh,
                        op0=OP.mult, op1=OP.add))
                rop("vector", lambda e: e.reduce_max(THRh, v3(SCCh), axis=AX))
                rop("vector", lambda e: e.tensor_tensor(
                    out=v3(KMFh), in0=v3(WRKh), in1=bct(THRh, NE),
                    op=OP.is_ge))
                rop("vector", lambda e: e.tensor_mul(
                    out=SCCh, in0=LGh, in1=KMFh))
                rop("vector", lambda e: e.reduce_sum(WSCh, v3(SCCh), axis=AX))
                rop("vector", lambda e: e.tensor_scalar(
                    out=WSCh, in0=WSCh, scalar1=1e-20, scalar2=None,
                    op0=OP.add))
                rop("vector", lambda e: e.reciprocal(out=T2Ch, in_=WSCh))
                rop("vector", lambda e: e.tensor_mul(
                    out=v3(CMBh), in0=v3(SCCh), in1=bct(T2Ch, NE)))
                rop("vector", lambda e: e.tensor_copy(out=COMBBh, in_=CMBh))
                return ops

            def emit_cbtr_half(h):
                for tt_ in range(h * 8, h * 8 + 8):
                    kb.op("tensor", lambda e, tt_=tt_: e.transpose(
                        cbtr_ps, COMBB[:, NE * tt_:NE * (tt_ + 1)], idb[:]),
                        r=[rRH[0], rRH[1], ridb], w=[racc[1]])
                    kb.op("vector", lambda e, tt_=tt_: e.tensor_copy(
                        out=cbT[:, tt_ * 128:(tt_ + 1) * 128], in_=cbtr_ps),
                        r=[racc[1]], w=[rcbT])
                    kb.op("vector", lambda e: e.tensor_copy(
                        out=s4[:, 0:1], in_=s4[:, 0:1]),
                        r=[racc[1]], w=[rs4])

            c2_stage1(0)
            c2_fpT(0)
            rh0 = None
            for tt_ in range(TT):
                if tt_ + 1 < TT:
                    c2_stage1(tt_ + 1)
                c2_bfT(tt_)
                if tt_ + 1 < TT:
                    c2_fpT(tt_ + 1)
                c2_router(tt_)
                if tt_ == 7:
                    rh0 = router_half_ops(0)
                elif tt_ > 7:
                    for _ in range(min(5, len(rh0))):
                        rh0.pop(0)()


            def owner(b, i):
                h0 = (b * HCB + i) * 128
                if h0 < 2048:
                    return None
                return (h0 - 2048) // 256

            blk_experts = []
            for b in range(NBLK):
                es = []
                for i in range(HCB):
                    e_ = owner(b, i)
                    if e_ is not None and e_ not in es:
                        es.append(e_)
                blk_experts.append(es)

            def cb_slot(b, e_):
                return (b % 2) * 3 + blk_experts[b].index(e_)

            def cbB_dma(b):
                for e_ in blk_experts[b]:
                    s_ = cb_slot(b, e_)
                    kb.op("sync", lambda e, e_=e_, s_=s_: e.dma_start(
                        out=cbB[s_][:],
                        in_=cbt_d[e_:e_ + 1, :].to_broadcast((128, TOK))),
                        r=[rcbt_d, rRH[0], rRH[1], rLGh[0], rLGh[1]],
                        w=[rcbB[s_]], dma=True)

            gubv = [gub[p][:].rearrange("p (cc h) -> p cc h", cc=CCH)
                    for p in (0, 1)]
            dnbv = [dnb[p][:].rearrange("p (hc c) -> p hc c", hc=HCB)
                    for p in (0, 1)]

            def emit_down(b, i, lh, rlh, bpar, t0, t1):
                for tl, tile in ((0, t0), (1, t1)):
                    for nh in range(2):
                        kb.op("tensor",
                              lambda e, tl=tl, nh=nh, i=i, lh=lh,
                              bpar=bpar: e.matmul(
                                  acc_ps[tl][:, nh * 512:(nh + 1) * 512],
                                  lh[:, tl * 128:(tl + 1) * 128],
                                  dnbv[bpar][:, i, nh * 512:(nh + 1) * 512],
                                  start=(i == 0), stop=(i == HCB - 1)),
                              r=[rlh, rdnb[bpar]], w=[racc[tl]])
                if i == HCB - 1:
                    for tl, tile in ((0, t0), (1, t1)):
                        xt = X[:, tile * C:(tile + 1) * C]
                        kb.op("vector", lambda e, xt=xt, tl=tl: e.tensor_add(
                            out=xt, in0=xt, in1=acc_ps[tl][:]),
                            r=[racc[tl], rX[tile]], w=[rX[tile]])
                        if b == NBLK - 1:
                            kb.op("sync", lambda e, xt=xt, tile=tile:
                                  e.dma_start(
                                      out=out_d[tile * 128:(tile + 1) * 128, :],
                                      in_=xt),
                                  r=[rX[tile]], dma=True)

            for f in rh0:
                f()
            emit_cbtr_half(0)
            for f in router_half_ops(1):
                f()
            emit_cbtr_half(1)
            kb.op("sync", lambda e: e.dma_start(out=cbt_d, in_=cbT[:]),
                  r=[rcbT], w=[rcbt_d], dma=True)
            cbB_dma(2)
            cbB_dma(3)

            for b in range(NBLK):
                bpar = b % 2
                pend = None
                for pair in range(NPAIR):
                    t0, t1 = 2 * pair, 2 * pair + 1
                    for i in range(HCB):
                        par = i % 2
                        e_ = owner(b, i)
                        for cc in range(CCH):
                            kb.op("tensor",
                                  lambda e, cc=cc, i=i, bpar=bpar, par=par,
                                  pair=pair: e.matmul(
                                      g_ps[par][:],
                                      gubv[bpar][:, cc, i * 128:(i + 1) * 128],
                                      XTv[:, cc, pair * 256:(pair + 1) * 256],
                                      start=(cc == 0), stop=(cc == CCH - 1)),
                                  r=[rgub[bpar], rXT[t0], rXT[t1]],
                                  w=[rg[par]])
                        for cc in range(CCH):
                            kb.op("tensor",
                                  lambda e, cc=cc, i=i, bpar=bpar, par=par,
                                  pair=pair: e.matmul(
                                      u_ps[par][:],
                                      gubv[bpar][:, cc,
                                                 768 + i * 128:768 + (i + 1) * 128],
                                      XTv[:, cc, pair * 256:(pair + 1) * 256],
                                      start=(cc == 0), stop=(cc == CCH - 1)),
                                  r=[rgub[bpar], rXT[t0], rXT[t1]],
                                  w=[ru[par]])
                        kb.op("scalar", lambda e, par=par: e.activation(
                            out=sg[par][:], in_=g_ps[par][:], func=AF.Silu),
                            r=[rg[par]], w=[rsg[par]])
                        kb.op("vector", lambda e, par=par: e.tensor_mul(
                            out=hb[par][:], in0=sg[par][:], in1=u_ps[par][:]),
                            r=[rsg[par], ru[par]], w=[rhb[par]])
                        if e_ is not None:
                            s_ = cb_slot(b, e_)
                            kb.op("vector",
                                  lambda e, par=par, s_=s_, pair=pair:
                                  e.tensor_mul(
                                      out=hbb[par][:], in0=hb[par][:],
                                      in1=cbB[s_][:, pair * 256:(pair + 1) * 256]),
                                  r=[rhb[par], rcbB[s_]], w=[rhbb[par]])
                            lh, rlh = hbb[par], rhbb[par]
                        else:
                            lh, rlh = hb[par], rhb[par]
                        if pend is not None:
                            pend()
                        pend = (lambda b=b, i=i, lh=lh, rlh=rlh, bpar=bpar,
                                t0=t0, t1=t1:
                                emit_down(b, i, lh, rlh, bpar, t0, t1))
                if pend is not None:
                    pend()
                    pend = None
                if b + 2 < NBLK:
                    w_dma(b + 2)
                    if b >= 2:
                        cbB_dma(b + 2)

            kb.finalize()
            with nc.Block() as block:
                kb.emit(block, sems, dsems, final=True)
    return nc


_NC_CACHE = {}


def _pw_mats():
    W = 8

    def lin(s, k):
        w = min(s, W)
        if not (s - W + k >= 0 and s >= 1):
            return 0.0
        if w > 1:
            j = k - (W - w)
            return 0.1 + 0.9 * j / max(w - 1, 1)
        return 0.1

    PA_g = np.zeros((128, 128), np.float32)
    PB = np.zeros((7, 128), np.float32)
    for sp in range(128):
        for k in range(8):
            dp = sp + k
            val = 0.1 + 0.9 * k / 7.0
            if dp < 128:
                PA_g[dp, sp] = val
            elif dp - 128 < 7:
                PB[dp - 128, sp] = val
    PA_s = np.zeros((128, 128), np.float32)
    for sp in range(128):
        for k in range(8):
            dp = sp + k
            if dp < 128:
                PA_s[dp, sp] = lin(sp, k)
    bf = ml_dtypes.bfloat16
    return PA_s.astype(bf), PA_g.astype(bf), PB.astype(bf)


def kernel(L0_attn_embeddings, L0_attn_guidance_trajectories,
           L0_ffn_router_w, L0_ffn_correction_bias,
           L0_ffn_gate_w, L0_ffn_up_w, L0_ffn_down_w,
           L0_ffn_shared_gate_w, L0_ffn_shared_up_w, L0_ffn_shared_down_w):
    f = np.float32
    emb = np.asarray(L0_attn_embeddings, f)
    guid = np.ascontiguousarray(np.asarray(L0_attn_guidance_trajectories, f))
    guidT = np.ascontiguousarray(guid.T)
    rT = np.ascontiguousarray(np.asarray(L0_ffn_router_w, f).T)
    biasr = np.ascontiguousarray(np.asarray(L0_ffn_correction_bias, f)
                                 .reshape(1, NE))
    gw = np.asarray(L0_ffn_gate_w, f)
    uw = np.asarray(L0_ffn_up_w, f)
    dw = np.asarray(L0_ffn_down_w, f)
    sgw = np.asarray(L0_ffn_shared_gate_w, f)
    suw = np.asarray(L0_ffn_shared_up_w, f)
    sdw = np.asarray(L0_ffn_shared_down_w, f)

    gate_all = np.concatenate(
        [sgw.T, gw.transpose(1, 0, 2).reshape(C, 4096)], axis=1)
    up_all = np.concatenate(
        [suw.T, uw.transpose(1, 0, 2).reshape(C, 4096)], axis=1)
    wup = np.empty((C, 2 * HTOT), f)
    for b in range(NBLK):
        wup[:, b * 1536:b * 1536 + 768] = gate_all[:, b * 768:(b + 1) * 768]
        wup[:, b * 1536 + 768:(b + 1) * 1536] = up_all[:, b * 768:(b + 1) * 768]
    wup = np.ascontiguousarray(wup)
    wdn = np.ascontiguousarray(
        np.concatenate([sdw.T, dw.reshape(4096, C)], axis=0))

    PA_s, PA_g, PB = _pw_mats()
    ident = np.eye(128, dtype=ml_dtypes.bfloat16)
    identf = np.eye(128, dtype=f)

    in_maps = []
    for c in range(NCORES):
        bb, half = c // 2, c % 2
        start = half * 2048
        z = np.zeros((R, C), f)
        lo = start - 8
        s0, s1 = max(0, lo), min(S, start + 2169)
        z[s0 - lo:s1 - lo] = emb[bb, s0:s1]
        crow = np.zeros((1, 32), f)
        crow[0, 4:8] = NEGBIG
        crow[0, 16:32] = NEGBIG
        crow2 = np.empty((2, 256), f)
        crow2[0] = np.tile(np.asarray(L0_ffn_correction_bias, f), 16)
        crow2[1] = NEGBIG
        in_maps.append(dict(
            emb=z, pwa_s=(PA_s if half == 0 else PA_g), pwa_g=PA_g, pwb=PB,
            idb=ident, idf=identf, guid=guid, guidT=guidT, rT=rT,
            biasr=biasr, crow=crow, crow2=crow2,
            wup=wup, wdn=wdn))

    if "nc" not in _NC_CACHE:
        _NC_CACHE["nc"] = build_nc()
    nc = _NC_CACHE["nc"]
    res = run_bass_kernel_spmd(nc, in_maps, core_ids=list(range(NCORES)))
    out = np.empty((B, S, C), f)
    for c in range(NCORES):
        bb, half = c // 2, c % 2
        out[bb, half * 2048:(half + 1) * 2048] = res.results[c]["out"]
    return out


# version 45
# speedup vs baseline: 1.0022x; 1.0022x over previous
import numpy as np
import ml_dtypes
from contextlib import ExitStack

import concourse.bass as bass
import concourse.mybir as mybir
from concourse.bass_utils import run_bass_kernel_spmd

F32 = mybir.dt.float32
BF16 = mybir.dt.bfloat16
AX = mybir.AxisListType.X
AF = mybir.ActivationFunctionType
OP = mybir.AluOpType

B, S, C = 4, 4096, 1024
NCORES = 8
TOK = 2048
R = 2177
TT = 16
DTl = 17
CCH = 8
G = 512
GCH = 4
NE = 16
HTOT = 6144
NBLK = 8
HBLK = 768
HCB = 6
NPAIR = 8
INV_C = 1.0 / 1024.0
NEGBIG = -1e30


class Res:
    __slots__ = ("w", "r")

    def __init__(self):
        self.w = None
        self.r = []


class KB:
    ENG = ("tensor", "vector", "scalar", "gpsimd", "sync")
    NDSEM = 32

    def __init__(self):
        self.ops = {e: [] for e in self.ENG}
        self.barrier_tokens = None
        self.barrier_todo = set()
        self.dma_n = 0
        self.dma_uses = [0] * self.NDSEM

    def op(self, eng, emit, r=(), w=(), dma=False, deps=()):
        tok = (eng, len(self.ops[eng]))
        dl = list(deps)
        for x in r:
            if x.w is not None:
                dl.append(x.w)
        for x in w:
            if x.w is not None:
                dl.append(x.w)
            dl.extend(x.r)
        if self.barrier_tokens is not None and eng in self.barrier_todo:
            dl.extend(self.barrier_tokens)
            self.barrier_todo.discard(eng)
        if dma:
            idx = self.dma_n % self.NDSEM
            self.dma_n += 1
            self.dma_uses[idx] += 1
            dsem = (idx, 16 * self.dma_uses[idx])
        else:
            dsem = None
        self.ops[eng].append([emit, dl, dma, dsem])
        for x in r:
            x.r.append(tok)
        for x in w:
            x.w = tok
            x.r = []
        return tok

    def barrier(self):
        toks = []
        for e in self.ENG:
            if self.ops[e]:
                toks.append((e, len(self.ops[e]) - 1))
        self.barrier_tokens = toks
        self.barrier_todo = set(self.ENG)

    def finalize(self):
        self.need = {e: [False] * len(self.ops[e]) for e in self.ENG}
        for e in self.ENG:
            for _, deps, _, _ in self.ops[e]:
                for (pe, pi) in deps:
                    po = self.ops[pe][pi]
                    if po[2]:
                        continue
                    if pe == e == "tensor":
                        continue
                    self.need[pe][pi] = True
        self.incval = {}
        for e in self.ENG:
            c = 0
            iv = []
            for i, o in enumerate(self.ops[e]):
                if self.need[e][i]:
                    c += 1
                iv.append(c)
            self.incval[e] = iv

    def emit(self, block, sems, dsems, final):
        water = {e: {pe: 0 for pe in self.ENG} for e in self.ENG}
        dwater = {e: [0] * self.NDSEM for e in self.ENG}

        def mk(e):
            plan = self.ops[e]
            need = self.need[e]
            wtr = water[e]
            dwtr = dwater[e]

            def emitter(eng):
                for i, (emit, deps, dma, dsem) in enumerate(plan):
                    req = {}
                    dreq = {}
                    for (pe, pi) in deps:
                        po = self.ops[pe][pi]
                        if po[2]:
                            di, dv = po[3]
                            if dv > dreq.get(di, 0):
                                dreq[di] = dv
                            continue
                        if pe == e == "tensor":
                            continue
                        v = self.incval[pe][pi]
                        if v > req.get(pe, 0):
                            req[pe] = v
                    for pe, v in req.items():
                        if v > wtr[pe]:
                            eng.wait_ge(sems[pe], v)
                            wtr[pe] = v
                    for di, dv in dreq.items():
                        if dv > dwtr[di]:
                            eng.wait_ge(dsems[di], dv)
                            dwtr[di] = dv
                    inst = emit(eng)
                    if dma:
                        inst.then_inc(dsems[dsem[0]], 16)
                    elif need[i]:
                        inst.then_inc(sems[e], 1)
                if final:
                    for di in range(self.NDSEM):
                        tot = 16 * self.dma_uses[di]
                        if tot > 0 and dwtr[di] < tot:
                            eng.wait_ge(dsems[di], tot)
                            dwtr[di] = tot
            return emitter

        for e in self.ENG:
            if self.ops[e]:
                getattr(block, e)(mk(e))


def build_nc():
    nc = bass.Bass("TRN2", target_bir_lowering=False)
    kb = KB()

    emb = nc.dram_tensor("emb", [R, C], F32, kind="ExternalInput").ap()
    pwa_s = nc.dram_tensor("pwa_s", [128, 128], BF16, kind="ExternalInput").ap()
    pwa_g = nc.dram_tensor("pwa_g", [128, 128], BF16, kind="ExternalInput").ap()
    pwb = nc.dram_tensor("pwb", [7, 128], BF16, kind="ExternalInput").ap()
    idb_d = nc.dram_tensor("idb", [128, 128], BF16, kind="ExternalInput").ap()
    idf_d = nc.dram_tensor("idf", [128, 128], F32, kind="ExternalInput").ap()
    guid = nc.dram_tensor("guid", [G, C], F32, kind="ExternalInput").ap()
    guidT = nc.dram_tensor("guidT", [C, G], F32, kind="ExternalInput").ap()
    rT = nc.dram_tensor("rT", [C, NE], F32, kind="ExternalInput").ap()
    biasr = nc.dram_tensor("biasr", [1, NE], F32, kind="ExternalInput").ap()
    crow = nc.dram_tensor("crow", [1, 32], F32, kind="ExternalInput").ap()
    crow2 = nc.dram_tensor("crow2", [2, 256], F32, kind="ExternalInput").ap()
    wup = nc.dram_tensor("wup", [C, 2 * HTOT], F32, kind="ExternalInput").ap()
    wdn = nc.dram_tensor("wdn", [HTOT, C], F32, kind="ExternalInput").ap()
    out_d = nc.dram_tensor("out", [TOK, C], F32, kind="ExternalOutput").ap()

    srow_d = nc.dram_tensor("srow_scr", [1, G], BF16).ap()
    cbt_d = nc.dram_tensor("cbt_scr", [NE, TOK], BF16).ap()

    o_tiles = out_d.rearrange("(t p) c -> t p c", p=128)
    emb_r = emb

    def res():
        return Res()

    with ExitStack() as octx:
        def osb(name, shape, dtype):
            return octx.enter_context(nc.sbuf_tensor(name, shape, dtype))

        sems = {}
        for e in KB.ENG:
            sems[e] = octx.enter_context(nc.semaphore("sem_" + e))
        dsems = [octx.enter_context(nc.semaphore(f"dsem{i}"))
                 for i in range(KB.NDSEM)]

        X = osb("X", [128, TT * C], F32)
        idb = osb("idb_sb", [128, 128], BF16)
        idf = osb("idf_sb", [128, 128], F32)
        rtb = osb("rtb", [128, CCH * NE], F32)
        biasb = osb("biasb", [128, NE], F32)
        pwaS = osb("pwaS", [128, 128], BF16)
        pwaG = osb("pwaG", [128, 128], BF16)
        pwbS = osb("pwbS", [7, 128], BF16)
        CS2 = {nm: osb("cs_" + nm, [128, 20], F32) for nm in ("SUM2", "SQ2")}
        SB2 = {nm: osb("c_" + nm, [128, 20], F32)
               for nm in ("MS2", "V12", "T2S2", "RSTD2", "NMR2")}

        rX = [res() for _ in range(TT)]
        rgub = [res(), res()]
        rdnb = [res(), res()]
        ridb, rrtb, rbiasb = res(), res(), res()
        ridf = res()
        rpwaS, rpwaG, rpwbS = res(), res(), res()
        rCS2 = {nm: res() for nm in CS2}
        rSB2 = {nm: res() for nm in SB2}
        rsrow_d, rcbt_d = res(), res()

        with ExitStack() as g1:
            def sb1(name, shape, dtype):
                return g1.enter_context(nc.sbuf_tensor(name, shape, dtype))

            def ps1(name, shape, dtype):
                return g1.enter_context(nc.psum_tensor(name, shape, dtype))

            guidb = sb1("guidb", [128, GCH * C], BF16)
            guidTs = sb1("guidTs", [128, CCH * G], BF16)
            Sfull = sb1("Sfull", [128, G], BF16)
            WB = sb1("WBar", [128, DTl * C], BF16)
            mwcols = sb1("mwcols", [128, DTl], BF16)
            eb = [sb1(f"eb{p}", [128, C], F32) for p in (0, 1)]
            ebB = [sb1(f"ebB{p}", [128, C], F32) for p in (0, 1)]
            xbs = [sb1(f"xbs{p}", [128, C], F32) for p in (0, 1)]
            xnb = [sb1(f"xnb{p}", [128, C], F32) for p in (0, 1)]
            dbf6 = [sb1(f"dbf6_{p}", [128, C], F32) for p in range(6)]
            nb6 = [sb1(f"nb6_{p}", [128, C], BF16) for p in range(6)]
            nsb = [sb1(f"nsb{p}", [128, C], BF16) for p in (0, 1)]
            Eb = [sb1(f"Eb{p}", [128, G], BF16) for p in (0, 1)]
            etb = [sb1(f"etb{p}", [128, G], BF16) for p in (0, 1)]
            t2b = [sb1(f"t2b{p}", [128, C], BF16) for p in (0, 1)]
            sqb = [sb1(f"sqb{p}", [128, C], BF16) for p in (0, 1)]
            colsB = [sb1(f"colsB{p}", [128, 16], F32) for p in (0, 1)]
            colsS = [sb1(f"colsS{p}", [128, 16], F32) for p in (0, 1)]
            colsD0 = sb1("colsD0", [128, 132], F32)
            s4 = sb1("s4", [128, 4], F32)
            s4b = sb1("s4b", [128, 4], BF16)
            SLAB = {}
            for nm in ("SUMS", "SQS", "MS", "V1", "T2S", "RSTD", "NMR",
                       "RSTDS", "NMRS", "XS", "XQ",
                       "MAG2", "MC", "IVR", "INVS", "MAGT", "TMW", "T1S",
                       "RQS", "MWS", "ACO"):
                SLAB[nm] = sb1("sl_" + nm, [128, 20], F32)
            TSTG = sb1("tstg", [64, 128], F32)
            rTSTG = res()

            nT_ps = ps1("nT_ps", [128, C], BF16)
            dots_ps = ps1("dots_ps", [128, G], F32)
            ET_ps = ps1("ET_ps", [128, G], BF16)
            infl_ps = ps1("infl_ps", [128, C], F32)
            num_ps = ps1("num_ps", [128, C], F32)
            den_ps = ps1("den_ps", [128, 8], F32)

            rguidb, rguidTs, rSfull = res(), res(), res()
            rWB = [res() for _ in range(DTl)]
            reb = [res(), res()]
            rebB = [res(), res()]
            rxbs = [res(), res()]
            rxnb = [res(), res()]
            rdbf6 = [res() for _ in range(6)]
            rnb6 = [res() for _ in range(6)]
            rnsb = [res(), res()]
            rEb = [res(), res()]
            retb = [res(), res()]
            rt2b = [res(), res()]
            rsqb = [res(), res()]
            rs4, rs4b = res(), res()
            rnT, rdots, rET, rinfl, rnum, rden = (res(), res(), res(), res(),
                                                  res(), res())
            rcolsB = [res(), res()]
            rcolsS = [res(), res()]
            rcolsD = [res()]
            rSLAB = {nm: res() for nm in SLAB}
            rmwG = [res(), res(), res()]

            def mkcol(tensor):
                def col(i):
                    return tensor[:, 4 * i:4 * i + 1]
                return col

            def colv(i):
                return colsD0[:, 4 * i:4 * i + 16:4]

            colD0 = mkcol(colsD0)

            kb.op("scalar", lambda e: e.dma_start(out=idb[:], in_=idb_d),
                  w=[ridb], dma=True)
            kb.op("scalar", lambda e: e.dma_start(out=idf[:], in_=idf_d),
                  w=[ridf], dma=True)
            kb.op("scalar", lambda e: e.dma_start(out=pwaS[:], in_=pwa_s),
                  w=[rpwaS], dma=True)
            kb.op("scalar", lambda e: e.dma_start(out=pwaG[:], in_=pwa_g),
                  w=[rpwaG], dma=True)
            kb.op("scalar", lambda e: e.dma_start(out=pwbS[:], in_=pwb),
                  w=[rpwbS], dma=True)
            kb.op("scalar", lambda e: e.dma_start(
                out=biasb[:], in_=biasr.to_broadcast((128, NE))),
                w=[rbiasb], dma=True)
            kb.op("sync", lambda e: e.dma_start(
                out=rtb[:].rearrange("p (cc e) -> p cc e", cc=CCH),
                in_=rT.rearrange("(cc p) e -> p cc e", p=128)),
                w=[rrtb], dma=True)
            kb.op("gpsimd", lambda e: e.dma_start(
                out=guidb[:].rearrange("p (gc c) -> p gc c", gc=GCH),
                in_=guid.rearrange("(gc p) c -> p gc c", p=128)),
                w=[rguidb], dma=True)
            kb.op("gpsimd", lambda e: e.dma_start(
                out=guidTs[:].rearrange("p (cc g) -> p cc g", cc=CCH),
                in_=guidT.rearrange("(cc p) g -> p cc g", p=128)),
                w=[rguidTs], dma=True)
            for t in range(TT):
                kb.op("sync" if t % 2 == 0 else "scalar",
                      lambda e, t=t: e.dma_start(
                          out=X[:, t * C:(t + 1) * C],
                          in_=emb_r[8 + 128 * t: 8 + 128 * (t + 1), :]),
                      w=[rX[t]], dma=True)

            def newton_rsqrt(y_ap, v_ap, rres, t_ap, rextra=()):
                rr = [rres] + list(rextra)
                kb.op("vector", lambda e: e.tensor_mul(
                    out=t_ap, in0=y_ap, in1=y_ap), r=rr, w=rr)
                kb.op("vector", lambda e: e.scalar_tensor_tensor(
                    out=t_ap, in0=v_ap, scalar=-0.5, in1=t_ap,
                    op0=OP.mult, op1=OP.mult), r=rr, w=rr)
                kb.op("vector", lambda e: e.tensor_scalar(
                    out=t_ap, in0=t_ap, scalar1=1.5, scalar2=None,
                    op0=OP.add), r=rr, w=rr)
                kb.op("vector", lambda e: e.tensor_mul(
                    out=y_ap, in0=y_ap, in1=t_ap), r=rr, w=rr)

            def SL(nm, a=0, b=DTl):
                return SLAB[nm][:, a:b]

            def normedb_one(d):
                par6 = d % 6
                kb.op("scalar", lambda e, d=d, par6=par6: e.activation(
                    out=nb6[par6][:], in_=dbf6[par6][:], func=AF.Copy,
                    scale=SLAB["INVS"][:, d:d + 1]),
                    r=[rdbf6[par6], rSLAB["INVS"]], w=[rnb6[par6]])

            def s5_alpha(d):
                par = d % 2
                par6 = d % 6
                for cc in range(CCH):
                    kb.op("tensor", lambda e, cc=cc, par6=par6: e.transpose(
                        nT_ps[:, cc * 128:(cc + 1) * 128],
                        nb6[par6][:, cc * 128:(cc + 1) * 128], idb[:]),
                        r=[rnb6[par6], ridb], w=[rnT])
                kb.op("scalar", lambda e, par=par: e.activation(
                    out=nsb[par][:], in_=nT_ps[:], func=AF.Copy),
                    r=[rnT], w=[rnsb[par]])

            def s5_mid(d):
                par = d % 2
                cS = mkcol(colsS[par])
                rcS = rcolsS[par]
                for cc in range(CCH):
                    kb.op("tensor", lambda e, cc=cc, par=par: e.matmul(
                        dots_ps[:], nsb[par][:, cc * 128:(cc + 1) * 128],
                        guidTs[:, cc * G:(cc + 1) * G],
                        start=(cc == 0), stop=(cc == CCH - 1)),
                        r=[rnsb[par], rguidTs], w=[rdots])
                kb.op("vector", lambda e, cS=cS: e.reduce_max(
                    cS(0), dots_ps[:], axis=AX, negate=True),
                    r=[rdots], w=[rcS])
                kb.op("scalar", lambda e, par=par, cS=cS: e.activation(
                    out=Eb[par][:], in_=dots_ps[:], func=AF.Exp,
                    bias=cS(0)),
                    r=[rdots, rcS], w=[rEb[par]])

            def s5_fin(d):
                par = d % 2
                par6 = d % 6
                cS = mkcol(colsS[par])
                rcS = rcolsS[par]
                kb.op("vector", lambda e, par=par, cS=cS: e.reduce_sum(
                    cS(1), Eb[par][:], axis=AX), r=[rEb[par]], w=[rcS])
                kb.op("vector", lambda e, cS=cS: e.reciprocal(
                    out=cS(2), in_=cS(1)), r=[rcS], w=[rcS])
                kb.op("vector", lambda e, d=d, cS=cS: e.scalar_tensor_tensor(
                    out=cS(3), in0=SLAB["MWS"][:, d:d + 1], scalar=0.4,
                    in1=cS(2), op0=OP.mult, op1=OP.mult),
                    r=[rcS, rSLAB["MWS"]], w=[rcS])
                for gc in range(GCH):
                    kb.op("tensor", lambda e, gc=gc, par=par: e.transpose(
                        ET_ps[:, gc * 128:(gc + 1) * 128],
                        Eb[par][:, gc * 128:(gc + 1) * 128], idb[:]),
                        r=[rEb[par], ridb], w=[rET])
                kb.op("scalar", lambda e, par=par: e.activation(
                    out=etb[par][:], in_=ET_ps[:], func=AF.Copy),
                    r=[rET], w=[retb[par]])
                for gc in range(GCH):
                    for nh in range(2):
                        kb.op("tensor", lambda e, gc=gc, nh=nh, par=par:
                              e.matmul(
                                  infl_ps[:, nh * 512:(nh + 1) * 512],
                                  etb[par][:, gc * 128:(gc + 1) * 128],
                                  guidb[:, gc * C + nh * 512:
                                        gc * C + (nh + 1) * 512],
                                  start=(gc == 0), stop=(gc == GCH - 1)),
                              r=[retb[par], rguidb], w=[rinfl])
                kb.op("scalar", lambda e, par=par, cS=cS: e.activation(
                    out=t2b[par][:], in_=infl_ps[:], func=AF.Copy,
                    scale=cS(3)), r=[rinfl, rcS], w=[rt2b[par]])

            def s5_fin_b(d):
                par = d % 2
                par6 = d % 6
                kb.op("vector", lambda e, d=d, par6=par6, par=par:
                      e.scalar_tensor_tensor(
                          out=WB[:, d * C:(d + 1) * C], in0=nb6[par6][:],
                          scalar=SLAB["ACO"][:, d:d + 1], in1=t2b[par][:],
                          op0=OP.mult, op1=OP.add),
                      r=[rnb6[par6], rSLAB["ACO"], rt2b[par]], w=[rWB[d]])

            def batch_mag_p1(g0, g1):
                n = g1 - g0
                A = lambda nm: SLAB[nm][:, g0:g1]
                R = [rSLAB[nm] for nm in SLAB]

                def bop(fn, rd=(), wr=()):
                    kb.op("vector", fn, r=list(rd), w=list(wr))

                def sop(fn, rd=(), wr=()):
                    kb.op("scalar", fn, r=list(rd), w=list(wr))

                bop(lambda e: e.tensor_scalar(
                    out=A("MC"), in0=A("MAG2"), scalar1=1e-12, scalar2=None,
                    op0=OP.max), [rSLAB["MAG2"]], [rSLAB["MC"]])
                sop(lambda e: e.activation(out=A("T1S"), in_=A("MC"),
                                           func=AF.Ln),
                    [rSLAB["MC"]], [rSLAB["T1S"]])
                sop(lambda e: e.activation(out=A("IVR"), in_=A("T1S"),
                                           func=AF.Exp, scale=-0.5),
                    [rSLAB["T1S"]], [rSLAB["IVR"]])
                newton_rsqrt(A("IVR"), A("MC"), rSLAB["IVR"], A("T2S"),
                             rextra=[rSLAB["MC"], rSLAB["T2S"]])
                bop(lambda e: e.tensor_scalar(
                    out=A("T1S"), in0=A("MAG2"), scalar1=1e-12, scalar2=None,
                    op0=OP.is_gt), [rSLAB["MAG2"]], [rSLAB["T1S"]])
                bop(lambda e: e.tensor_mul(out=A("INVS"), in0=A("IVR"),
                                           in1=A("T1S")),
                    [rSLAB["IVR"], rSLAB["T1S"]], [rSLAB["INVS"]])

            def batch_mag_p2(g0, g1):
                A = lambda nm: SLAB[nm][:, g0:g1]

                def bop(fn, rd=(), wr=()):
                    kb.op("vector", fn, r=list(rd), w=list(wr))

                def sop(fn, rd=(), wr=()):
                    kb.op("scalar", fn, r=list(rd), w=list(wr))

                bop(lambda e: e.tensor_mul(out=A("MAGT"), in0=A("MC"),
                                           in1=A("IVR")),
                    [rSLAB["MC"], rSLAB["IVR"]], [rSLAB["MAGT"]])
                sop(lambda e: e.activation(out=A("TMW"), in_=A("MAGT"),
                                           func=AF.Exp, scale=-4.0),
                    [rSLAB["MAGT"]], [rSLAB["TMW"]])
                bop(lambda e: e.tensor_scalar(
                    out=A("T1S"), in0=A("TMW"), scalar1=-1.0, scalar2=1.0,
                    op0=OP.mult, op1=OP.add), [rSLAB["TMW"]], [rSLAB["T1S"]])
                bop(lambda e: e.tensor_scalar(
                    out=A("T2S"), in0=A("TMW"), scalar1=1.0, scalar2=None,
                    op0=OP.add), [rSLAB["TMW"]], [rSLAB["T2S"]])
                bop(lambda e: e.reciprocal(out=A("RQS"), in_=A("T2S")),
                    [rSLAB["T2S"]], [rSLAB["RQS"]])
                bop(lambda e: e.tensor_mul(out=A("MWS"), in0=A("T1S"),
                                           in1=A("RQS")),
                    [rSLAB["T1S"], rSLAB["RQS"]], [rSLAB["MWS"]])
                gidx = min(g0 // 6, 2)
                kb.op("vector", lambda e: e.tensor_copy(
                    out=mwcols[:, g0:g1], in_=A("MWS")),
                    r=[rSLAB["MWS"]], w=[rmwG[gidx]])
                bop(lambda e: e.tensor_scalar(
                    out=A("ACO"), in0=A("MWS"), scalar1=0.6, scalar2=None,
                    op0=OP.mult), [rSLAB["MWS"]], [rSLAB["ACO"]])

            def token_tile(tt_):
                colB = mkcol(colsB[tt_ % 2])
                rcB = rcolsB[tt_ % 2]
                pA = pwaS if tt_ == 0 else pwaG
                rpA = rpwaS if tt_ == 0 else rpwaG
                wb0 = WB[:, tt_ * C:(tt_ + 1) * C]
                wb1 = WB[0:7, (tt_ + 1) * C:(tt_ + 2) * C]
                for nh in range(2):
                    kb.op("tensor", lambda e, nh=nh, pA=pA, wb0=wb0: e.matmul(
                        num_ps[:, nh * 512:(nh + 1) * 512], pA[:],
                        wb0[:, nh * 512:(nh + 1) * 512],
                        start=True, stop=False),
                        r=[rpA, rWB[tt_]], w=[rnum])
                    kb.op("tensor", lambda e, nh=nh, wb1=wb1: e.matmul(
                        num_ps[:, nh * 512:(nh + 1) * 512], pwbS[:],
                        wb1[:, nh * 512:(nh + 1) * 512],
                        start=False, stop=True),
                        r=[rpwbS, rWB[tt_ + 1]], w=[rnum])
                kb.op("tensor", lambda e, pA=pA, tt_=tt_: e.matmul(
                    den_ps[:, 0:1], pA[:], mwcols[:, tt_:tt_ + 1],
                    start=True, stop=False),
                    r=[rpA, rmwG[min(tt_ // 6, 2)]], w=[rden])
                kb.op("tensor", lambda e, tt_=tt_: e.matmul(
                    den_ps[:, 0:1], pwbS[:], mwcols[0:7, tt_ + 1:tt_ + 2],
                    start=False, stop=True),
                    r=[rpwbS, rmwG[min((tt_ + 1) // 6, 2)]], w=[rden])
                kb.op("vector", lambda e: e.tensor_scalar(
                    out=colB(0), in0=den_ps[:, 0:1], scalar1=1e-8,
                    scalar2=None, op0=OP.max), r=[rden], w=[rcB])
                kb.op("vector", lambda e: e.reciprocal(
                    out=colB(1), in_=colB(0)), r=[rcB], w=[rcB])
                xt = X[:, tt_ * C:(tt_ + 1) * C]
                kb.op("vector", lambda e, xt=xt: e.scalar_tensor_tensor(
                    out=xt, in0=num_ps[:], scalar=colB(1), in1=xt,
                    op0=OP.mult, op1=OP.add),
                    r=[rnum, rcB, rX[tt_]], w=[rX[tt_]])

            kb.op("sync", lambda e: e.dma_start(
                out=eb[0][0:8, :], in_=emb_r[0:8, :]),
                w=[reb[0]], dma=True)
            kb.op("gpsimd", lambda e: e.tensor_tensor(
                out=xnb[0][0:8, :], in0=eb[0][0:8, :], in1=eb[0][0:8, :],
                op=OP.mult), r=[reb[0]], w=[rxnb[0]])
            for t in range(TT):
                par = t % 2
                kb.op("vector", lambda e, t=t: e.reduce_sum(
                    SLAB["XS"][:, t:t + 1], X[:, t * C:(t + 1) * C], axis=AX),
                    r=[rX[t]], w=[rSLAB["XS"]])
                kb.op("gpsimd", lambda e, t=t, par=par: e.tensor_tensor(
                    out=sqb[par][:], in0=X[:, t * C:(t + 1) * C],
                    in1=X[:, t * C:(t + 1) * C], op=OP.mult),
                    r=[rX[t]], w=[rsqb[par]])
                kb.op("vector", lambda e, t=t, par=par: e.reduce_sum(
                    SLAB["XQ"][:, t:t + 1], sqb[par][:], axis=AX),
                    r=[rsqb[par]], w=[rSLAB["XQ"]])
            dtv = dots_ps[:]
            itv = infl_ps[:]
            ntv = num_ps[:]
            kb.op("tensor", lambda e: e.transpose(
                dtv[0:17, 0:128], SLAB["XS"][:, 0:17], idf[:]),
                r=[rSLAB["XS"], ridf], w=[rdots])
            kb.op("tensor", lambda e: e.transpose(
                ntv[0:17, 0:128], SLAB["XQ"][:, 0:17], idf[:]),
                r=[rSLAB["XQ"], ridf], w=[rnum])
            kb.op("vector", lambda e: e.tensor_copy(
                out=TSTG[0:17, 8:128], in_=dtv[0:17, 0:120]),
                r=[rdots], w=[rTSTG])
            kb.op("vector", lambda e: e.tensor_copy(
                out=TSTG[32:49, 8:128], in_=ntv[0:17, 0:120]),
                r=[rnum], w=[rTSTG])
            kb.op("tensor", lambda e: e.transpose(
                itv[0:128, 0:17], TSTG[0:17, 0:128], idf[0:17, 0:17]),
                r=[rTSTG, ridf], w=[rinfl])
            kb.op("tensor", lambda e: e.transpose(
                itv[0:128, 17:34], TSTG[32:49, 0:128],
                idf[32:49, 32:49]),
                r=[rTSTG, ridf], w=[rinfl])
            kb.op("scalar", lambda e: e.activation(
                out=SLAB["SUMS"][:, 0:17], in_=itv[:, 0:17], func=AF.Copy),
                r=[rinfl], w=[rSLAB["SUMS"]])
            kb.op("scalar", lambda e: e.activation(
                out=SLAB["SQS"][:, 0:17], in_=itv[:, 17:34], func=AF.Copy),
                r=[rinfl], w=[rSLAB["SQS"]])
            kb.op("sync", lambda e: e.dma_start(
                out=SLAB["SUMS"][0:8, 1:TT + 1],
                in_=SLAB["XS"][120:128, 0:TT]),
                r=[rSLAB["XS"]], w=[rSLAB["SUMS"]], dma=True)
            kb.op("vector", lambda e: e.reduce_sum(
                SLAB["SUMS"][0:8, 0:1], eb[0][0:8, :], axis=AX),
                r=[reb[0]], w=[rSLAB["SUMS"]])
            kb.op("vector", lambda e: e.reduce_sum(
                SLAB["SQS"][0:8, 0:1], xnb[0][0:8, :], axis=AX),
                r=[rxnb[0]], w=[rSLAB["SQS"]])
            gn_ops = []
            _orig_op = kb.op
            kb.op = lambda *a, **k: gn_ops.append((a, k))
            for gc in range(GCH):
                gslice = guidb[:, gc * C:(gc + 1) * C]
                gp = gc % 2
                kb.op("gpsimd", lambda e, gs_=gslice, gp=gp: e.tensor_tensor(
                    out=sqb[gp][:], in0=gs_, in1=gs_, op=OP.mult),
                    r=[rguidb], w=[rsqb[gp]])
                kb.op("vector", lambda e, gc=gc, gp=gp: e.reduce_sum(
                    colD0(28 + gc), sqb[gp][:], axis=AX),
                    r=[rsqb[gp]], w=[rcolsD[0]])
            kb.op("vector", lambda e: e.tensor_scalar(
                out=colv(28), in0=colv(28), scalar1=1e-16,
                scalar2=None, op0=OP.max), r=[rcolsD[0]], w=[rcolsD[0]])
            kb.op("scalar", lambda e: e.activation(
                out=s4[:], in_=colv(28), func=AF.Ln),
                r=[rcolsD[0]], w=[rs4])
            kb.op("scalar", lambda e: e.activation(
                out=s4[:], in_=s4[:], func=AF.Exp, scale=-0.5),
                r=[rs4], w=[rs4])
            kb.op("vector", lambda e: e.tensor_mul(
                out=colv(24), in0=s4[:], in1=s4[:]),
                r=[rs4, rcolsD[0]], w=[rcolsD[0]])
            kb.op("vector", lambda e: e.scalar_tensor_tensor(
                out=colv(24), in0=colv(28), scalar=-0.5,
                in1=colv(24), op0=OP.mult, op1=OP.mult),
                r=[rcolsD[0]], w=[rcolsD[0]])
            kb.op("vector", lambda e: e.tensor_scalar(
                out=colv(24), in0=colv(24), scalar1=1.5,
                scalar2=None, op0=OP.add), r=[rcolsD[0]], w=[rcolsD[0]])
            kb.op("vector", lambda e: e.tensor_mul(
                out=s4[:], in0=s4[:], in1=colv(24)),
                r=[rs4, rcolsD[0]], w=[rs4])
            kb.op("vector", lambda e: e.tensor_scalar(
                out=s4b[:], in0=s4[:], scalar1=2.0, scalar2=None,
                op0=OP.mult), r=[rs4], w=[rs4b])
            kb.op("tensor", lambda e: e.transpose(
                nT_ps[0:4, 0:128], s4b[:], idb[:]),
                r=[rs4b, ridb], w=[rnT])
            kb.op("scalar", lambda e: e.activation(
                out=Sfull[0:4, 0:128], in_=nT_ps[0:4, 0:128], func=AF.Copy),
                r=[rnT], w=[rSfull])
            for gc in range(GCH):
                kb.op("scalar", lambda e, gc=gc: e.dma_start(
                    out=srow_d[0:1, gc * 128:(gc + 1) * 128],
                    in_=Sfull[gc:gc + 1, 0:128]),
                    r=[rSfull], w=[rsrow_d], dma=True)
            kb.op("scalar", lambda e: e.dma_start(
                out=Sfull[:], in_=srow_d.to_broadcast((128, G))),
                r=[rsrow_d], w=[rSfull], dma=True)
            for cc in range(CCH):
                sl = guidTs[:, cc * G:(cc + 1) * G]
                kb.op("vector", lambda e, sl=sl: e.tensor_mul(
                    out=sl, in0=sl, in1=Sfull[:]),
                    r=[rSfull, rguidTs], w=[rguidTs])
            kb.op = _orig_op

            def gdrip(n):
                for _ in range(min(n, len(gn_ops))):
                    a_, k_ = gn_ops.pop(0)
                    _orig_op(*a_, **k_)
            gdrip(5)


            kb.op("scalar", lambda e: e.dma_start(
                out=SLAB["SQS"][0:8, 1:TT + 1],
                in_=SLAB["XQ"][120:128, 0:TT]),
                r=[rSLAB["XQ"]], w=[rSLAB["SQS"]], dma=True)

            kb.op("vector", lambda e: e.tensor_scalar(
                out=SL("MS"), in0=SL("SUMS"), scalar1=INV_C, scalar2=None,
                op0=OP.mult), r=[rSLAB["SUMS"]], w=[rSLAB["MS"]])
            kb.op("vector", lambda e: e.tensor_mul(
                out=SL("T2S"), in0=SL("MS"), in1=SL("MS")),
                r=[rSLAB["MS"]], w=[rSLAB["T2S"]])
            kb.op("vector", lambda e: e.tensor_scalar(
                out=SL("V1"), in0=SL("SQS"), scalar1=INV_C, scalar2=None,
                op0=OP.mult), r=[rSLAB["SQS"]], w=[rSLAB["V1"]])
            kb.op("vector", lambda e: e.tensor_sub(
                out=SL("V1"), in0=SL("V1"), in1=SL("T2S")),
                r=[rSLAB["V1"], rSLAB["T2S"]], w=[rSLAB["V1"]])
            kb.op("vector", lambda e: e.tensor_scalar(
                out=SL("V1"), in0=SL("V1"), scalar1=1e-5, scalar2=None,
                op0=OP.add), r=[rSLAB["V1"]], w=[rSLAB["V1"]])
            kb.op("scalar", lambda e: e.activation(
                out=SL("RSTD"), in_=SL("V1"), func=AF.Ln),
                r=[rSLAB["V1"]], w=[rSLAB["RSTD"]])
            kb.op("scalar", lambda e: e.activation(
                out=SL("RSTD"), in_=SL("RSTD"), func=AF.Exp, scale=-0.5),
                r=[rSLAB["RSTD"]], w=[rSLAB["RSTD"]])
            gdrip(6)
            newton_rsqrt(SL("RSTD"), SL("V1"), rSLAB["RSTD"], SL("T2S"),
                         rextra=[rSLAB["V1"], rSLAB["T2S"]])
            kb.op("vector", lambda e: e.scalar_tensor_tensor(
                out=SL("NMR"), in0=SL("MS"), scalar=-1.0, in1=SL("RSTD"),
                op0=OP.mult, op1=OP.mult),
                r=[rSLAB["MS"], rSLAB["RSTD"]], w=[rSLAB["NMR"]])
            gdrip(5)
            kb.op("tensor", lambda e: e.transpose(
                dtv[0:17, 0:128], SLAB["RSTD"][:, 0:17], idf[:]),
                r=[rSLAB["RSTD"], ridf], w=[rdots])
            kb.op("tensor", lambda e: e.transpose(
                ntv[0:17, 0:128], SLAB["NMR"][:, 0:17], idf[:]),
                r=[rSLAB["NMR"], ridf], w=[rnum])
            kb.op("vector", lambda e: e.tensor_copy(
                out=TSTG[0:17, 0:127], in_=dtv[0:17, 1:128]),
                r=[rdots], w=[rTSTG])
            kb.op("vector", lambda e: e.tensor_copy(
                out=TSTG[32:49, 0:127], in_=ntv[0:17, 1:128]),
                r=[rnum], w=[rTSTG])
            kb.op("tensor", lambda e: e.transpose(
                itv[0:128, 0:17], TSTG[0:17, 0:128], idf[0:17, 0:17]),
                r=[rTSTG, ridf], w=[rinfl])
            kb.op("tensor", lambda e: e.transpose(
                itv[0:128, 17:34], TSTG[32:49, 0:128],
                idf[32:49, 32:49]),
                r=[rTSTG, ridf], w=[rinfl])
            kb.op("scalar", lambda e: e.activation(
                out=SLAB["RSTDS"][:, 0:17], in_=itv[:, 0:17], func=AF.Copy),
                r=[rinfl], w=[rSLAB["RSTDS"]])
            kb.op("scalar", lambda e: e.activation(
                out=SLAB["NMRS"][:, 0:17], in_=itv[:, 17:34], func=AF.Copy),
                r=[rinfl], w=[rSLAB["NMRS"]])
            gdrip(6)
            for nm in ("RSTD", "NMR"):
                kb.op("scalar", lambda e, nm=nm: e.dma_start(
                    out=SLAB[nm + "S"][127:128, 0:DTl - 1],
                    in_=SLAB[nm][0:1, 1:DTl]),
                    r=[rSLAB[nm]], w=[rSLAB[nm + "S"]], dma=True)

            def stats_load(d):
                par = d % 2
                kb.op("sync", lambda e, d=d, par=par: e.dma_start(
                    out=eb[par][:], in_=emb_r[128 * d:128 * (d + 1), :]),
                    w=[reb[par]], dma=True)
                kb.op("sync", lambda e, d=d, par=par: e.dma_start(
                    out=ebB[par][:],
                    in_=emb_r[128 * d + 1:128 * (d + 1) + 1, :]),
                    w=[rebB[par]], dma=True)

            def stats_comp_a(d):
                par = d % 2
                sl6 = d % 6
                kb.op("vector", lambda e, d=d, par=par: e.tensor_scalar(
                    out=xnb[par][:], in0=eb[par][:],
                    scalar1=SLAB["RSTD"][:, d:d + 1],
                    scalar2=SLAB["NMR"][:, d:d + 1],
                    op0=OP.mult, op1=OP.add),
                    r=[reb[par], rSLAB["RSTD"], rSLAB["NMR"]],
                    w=[rxnb[par]])
                kb.op("vector", lambda e, d=d, par=par: e.tensor_scalar(
                    out=xbs[par][:], in0=ebB[par][:],
                    scalar1=SLAB["RSTDS"][:, d:d + 1],
                    scalar2=SLAB["NMRS"][:, d:d + 1],
                    op0=OP.mult, op1=OP.add),
                    r=[rebB[par], rSLAB["RSTDS"], rSLAB["NMRS"]],
                    w=[rxbs[par]])
                kb.op("gpsimd", lambda e, par=par, sl6=sl6: e.tensor_tensor(
                    out=dbf6[sl6][:], in0=xbs[par][:], in1=xnb[par][:],
                    op=OP.subtract),
                    r=[rxbs[par], rxnb[par]], w=[rdbf6[sl6]])
                kb.op("gpsimd", lambda e, sl6=sl6, par=par: e.tensor_tensor(
                    out=sqb[par][:], in0=dbf6[sl6][:], in1=dbf6[sl6][:],
                    op=OP.mult),
                    r=[rdbf6[sl6]], w=[rsqb[par]])

            def stats_comp_a_fast(d):
                par = d % 2
                sl6 = d % 6
                kb.op("vector", lambda e, d=d, par=par: e.tensor_scalar(
                    out=xnb[par][:], in0=eb[par][:],
                    scalar1=SLAB["RSTD"][:, d:d + 1],
                    scalar2=SLAB["NMR"][:, d:d + 1],
                    op0=OP.mult, op1=OP.add),
                    r=[reb[par], rSLAB["RSTD"], rSLAB["NMR"]],
                    w=[rxnb[par]])
                kb.op("vector", lambda e, d=d, par=par: e.tensor_scalar(
                    out=xbs[par][:], in0=ebB[par][:],
                    scalar1=SLAB["RSTDS"][:, d:d + 1],
                    scalar2=SLAB["NMRS"][:, d:d + 1],
                    op0=OP.mult, op1=OP.add),
                    r=[rebB[par], rSLAB["RSTDS"], rSLAB["NMRS"]],
                    w=[rxbs[par]])
                kb.op("vector", lambda e, par=par, sl6=sl6: e.tensor_sub(
                    out=dbf6[sl6][:], in0=xbs[par][:], in1=xnb[par][:]),
                    r=[rxbs[par], rxnb[par]], w=[rdbf6[sl6]])
                kb.op("scalar", lambda e, sl6=sl6, par=par: e.activation(
                    out=sqb[par][:], in_=dbf6[sl6][:], func=AF.Square),
                    r=[rdbf6[sl6]], w=[rsqb[par]])

            def stats_comp_b(d):
                par = d % 2
                kb.op("vector", lambda e, d=d, par=par: e.reduce_sum(
                    SLAB["MAG2"][:, d:d + 1], sqb[par][:], axis=AX),
                    r=[rsqb[par]], w=[rSLAB["MAG2"]])

            def stats_comp(d):
                stats_comp_a(d)
                stats_comp_b(d)

            c1_done = [False] * TT

            def c1_stat(tt_):
                srcX = X[:, tt_ * C:(tt_ + 1) * C]
                par = (tt_ + 1) % 2
                kb.op("vector", lambda e, srcX=srcX, tt_=tt_: e.reduce_sum(
                    CS2["SUM2"][:, tt_:tt_ + 1], srcX, axis=AX),
                    r=[rX[tt_]], w=[rCS2["SUM2"]])
                kb.op("gpsimd", lambda e, srcX=srcX, par=par: e.tensor_tensor(
                    out=sqb[par][:], in0=srcX, in1=srcX, op=OP.mult),
                    r=[rX[tt_]], w=[rsqb[par]])
                kb.op("vector", lambda e, tt_=tt_, par=par: e.reduce_sum(
                    CS2["SQ2"][:, tt_:tt_ + 1], sqb[par][:], axis=AX),
                    r=[rsqb[par]], w=[rCS2["SQ2"]])
                c1_done[tt_] = True

            nb_hi = -1
            mag_hi = -1

            def ensure_nb(upto):
                nonlocal nb_hi
                for d in range(nb_hi + 1, min(upto, mag_hi) + 1):
                    normedb_one(d)
                    nb_hi = d

            gdrip(99)
            stats_load(0)
            stats_load(1)
            stats_comp_a_fast(0)
            stats_comp_b(0)
            stats_load(2)
            stats_comp_a_fast(1)
            stats_comp_b(1)
            stats_load(3)
            batch_mag_p1(0, 2)
            mag_hi = 1
            ensure_nb(1)
            s5_alpha(0)
            batch_mag_p2(0, 2)
            p1sched = {0: (2, 4), 1: (4, 6), 3: (6, 8), 5: (8, 10),
                       7: (10, 12), 9: (12, 14), 11: (14, 16),
                       12: (16, DTl)}
            p2sched = {i + 1: v for i, v in p1sched.items()}
            compsched = {0: [2, 3], 1: [4, 5]}
            loadsched = {0: [4, 5], 1: [6, 7]}
            for i in range(2, 13):
                compsched[i] = [4 + i]
                if 6 + i <= DTl - 1:
                    loadsched[i] = [6 + i]
            for i in range(0, DTl + 1):
                for d in compsched.get(i, []):
                    (stats_comp_a_fast if d < 4 else stats_comp_a)(d)
                for d in loadsched.get(i, []):
                    stats_load(d)
                ensure_nb(i + 2)
                if i + 1 <= nb_hi:
                    s5_alpha(i + 1)
                if i < DTl:
                    s5_mid(i)
                if 2 <= i and i - 2 < TT:
                    token_tile(i - 2)
                if i < DTl:
                    s5_fin(i)
                for d in compsched.get(i, []):
                    stats_comp_b(d)
                if 2 <= i and i - 2 < TT:
                    c1_stat(i - 2)
                if i < DTl:
                    s5_fin_b(i)
                if i in p1sched:
                    g0m, g1m = p1sched[i]
                    batch_mag_p1(g0m, g1m)
                    mag_hi = g1m - 1
                if i in p2sched:
                    batch_mag_p2(*p2sched[i])

            def SL2(nm):
                return SB2[nm][:, 0:TT]
            kb.op("vector", lambda e: e.tensor_scalar(
                out=SL2("MS2"), in0=CS2["SUM2"][:, 0:TT], scalar1=INV_C,
                scalar2=None,
                op0=OP.mult), r=[rCS2["SUM2"]], w=[rSB2["MS2"]])
            kb.op("vector", lambda e: e.tensor_mul(
                out=SL2("T2S2"), in0=SL2("MS2"), in1=SL2("MS2")),
                r=[rSB2["MS2"]], w=[rSB2["T2S2"]])
            kb.op("vector", lambda e: e.tensor_scalar(
                out=SL2("V12"), in0=CS2["SQ2"][:, 0:TT], scalar1=INV_C,
                scalar2=None,
                op0=OP.mult), r=[rCS2["SQ2"]], w=[rSB2["V12"]])
            kb.op("vector", lambda e: e.tensor_sub(
                out=SL2("V12"), in0=SL2("V12"), in1=SL2("T2S2")),
                r=[rSB2["V12"], rSB2["T2S2"]], w=[rSB2["V12"]])
            kb.op("vector", lambda e: e.tensor_scalar(
                out=SL2("V12"), in0=SL2("V12"), scalar1=1e-5, scalar2=None,
                op0=OP.add), r=[rSB2["V12"]], w=[rSB2["V12"]])
            kb.op("scalar", lambda e: e.activation(
                out=SL2("RSTD2"), in_=SL2("V12"), func=AF.Ln),
                r=[rSB2["V12"]], w=[rSB2["RSTD2"]])
            kb.op("scalar", lambda e: e.activation(
                out=SL2("RSTD2"), in_=SL2("RSTD2"), func=AF.Exp, scale=-0.5),
                r=[rSB2["RSTD2"]], w=[rSB2["RSTD2"]])
            newton_rsqrt(SL2("RSTD2"), SL2("V12"), rSB2["RSTD2"],
                         SL2("T2S2"), rextra=[rSB2["V12"], rSB2["T2S2"]])
            kb.op("vector", lambda e: e.scalar_tensor_tensor(
                out=SL2("NMR2"), in0=SL2("MS2"), scalar=-1.0,
                in1=SL2("RSTD2"), op0=OP.mult, op1=OP.mult),
                r=[rSB2["MS2"], rSB2["RSTD2"]], w=[rSB2["NMR2"]])


        kb.barrier()
        with ExitStack() as g2:
            def sb2(name, shape, dtype):
                return g2.enter_context(nc.sbuf_tensor(name, shape, dtype))

            def ps2(name, shape, dtype):
                return g2.enter_context(nc.psum_tensor(name, shape, dtype))

            gub = [sb2(f"gub{p}", [128, CCH * 1536], BF16) for p in (0, 1)]
            dnb = [sb2(f"dnb{p}", [128, HCB * C], BF16) for p in (0, 1)]

            def w_dma(b):
                par = b % 2
                kb.op("gpsimd", lambda e, b=b, par=par: e.dma_start(
                    out=gub[par][:].rearrange("p (cc h) -> p cc h", cc=CCH),
                    in_=wup.rearrange("(cc p) H -> p cc H", p=128)
                        [:, :, b * 1536:(b + 1) * 1536]),
                    w=[rgub[par]], dma=True)
                kb.op("gpsimd", lambda e, b=b, par=par: e.dma_start(
                    out=dnb[par][:].rearrange("p (hc c) -> p hc c", hc=HCB),
                    in_=wdn.rearrange("(hc p) c -> p hc c", p=128)
                        [:, b * HCB:(b + 1) * HCB, :]),
                    w=[rdnb[par]], dma=True)

            w_dma(0)
            w_dma(1)

            XT = sb2("XT", [128, CCH * TOK], BF16)
            xn2b = [sb2(f"xn2b{p}", [128, C], BF16) for p in (0, 1)]
            cbT = sb2("cbT", [16, TOK], BF16)
            cbB = [sb2(f"cbB{i}", [128, TOK], BF16) for i in range(6)]
            sg = [sb2(f"sg{p}", [128, 256], F32) for p in (0, 1)]
            hb = [sb2(f"hb{p}", [128, 256], BF16) for p in (0, 1)]
            hbb = [sb2(f"hbb{p}", [128, 256], BF16) for p in (0, 1)]

            acc_ps = [ps2(f"acc{p}", [128, C], F32) for p in (0, 1)]
            g_ps = [ps2(f"gps{p}", [128, 256], F32) for p in (0, 1)]
            u_ps = [ps2(f"ups{p}", [128, 256], F32) for p in (0, 1)]

            XTv = XT[:].rearrange("p (cc t) -> p cc t", cc=CCH)
            rXT = [res() for _ in range(TT)]
            rxn2b = [res(), res()]
            rcbT = res()
            rcbB = [res() for _ in range(6)]
            rsg = [res(), res()]
            rhb = [res(), res()]
            rhbb = [res(), res()]
            racc = [res(), res()]
            rg = [res(), res()]
            ru = [res(), res()]
            cb0f = cbB[0][:].bitcast(F32)
            cb1f = cbB[1][:].bitcast(F32)
            cb2f = cbB[2][:].bitcast(F32)
            LG = cb0f[:, 0:256]
            SCC = cb0f[:, 256:512]
            SBA = cb0f[:, 512:768]
            WRK = cb0f[:, 768:1024]
            GTA = cb1f[:, 64:128]
            GTB = cb1f[:, 128:192]
            GSA = cb1f[:, 0:64]
            T2C = cb1f[:, 192:208]
            MKC = cb1f[:, 208:224]
            THR = cb1f[:, 224:240]
            WSC = cb1f[:, 240:256]
            GMF = cb1f[:, 272:336]
            CMB = cb1f[:, 336:592]
            KMF = cb1f[:, 592:848]
            COMBB = cbB[2][:, 0:256]
            GMU8 = cbB[2][:].bitcast(mybir.dt.uint8)[:, 512:768]
            BIAS16 = cb2f[:, 256:512]
            NEGB = cb2f[:, 512:768]
            rcb0, rcb1, rcb2 = rcbB[0], rcbB[1], rcbB[2]
            rLGh = [res(), res()]
            rRH = [res(), res()]
            xn2f = cbB[4][:].bitcast(F32)[:, 0:C]
            rxn2f = rcbB[4]
            xn2fT = [cbB[5][:].bitcast(F32)[:, 0:C],
                     cbB[3][:].bitcast(F32)[:, 0:C]]
            rxn2fT = [rcbB[5], rcbB[3]]
            xtf_ps = acc_ps[0][:]

            xt_ps = acc_ps[0][:].bitcast(BF16)[:, 0:C]
            cbtr_ps = acc_ps[1][:].bitcast(BF16)[0:16, 0:128]
            lgt_ps = g_ps[0][:, 0:NE]

            kb.op("sync", lambda e: e.dma_start(
                out=BIAS16, in_=crow2[0:1, :].to_broadcast((128, 256))),
                w=[rcb2], dma=True)
            kb.op("sync", lambda e: e.dma_start(
                out=NEGB, in_=crow2[1:2, :].to_broadcast((128, 256))),
                w=[rcb2], dma=True)

            xtf_v = [acc_ps[0][:], acc_ps[1][:]]
            xt_v = [acc_ps[0][:].bitcast(BF16)[:, 0:C],
                    acc_ps[1][:].bitcast(BF16)[:, 0:C]]

            def c2_stage1(tt_):
                srcX = X[:, tt_ * C:(tt_ + 1) * C]
                par = tt_ % 2
                kb.op("vector", lambda e, srcX=srcX, tt_=tt_: e.tensor_scalar(
                    out=xn2f, in0=srcX,
                    scalar1=SB2["RSTD2"][:, tt_:tt_ + 1],
                    scalar2=SB2["NMR2"][:, tt_:tt_ + 1],
                    op0=OP.mult, op1=OP.add),
                    r=[rX[tt_], rSB2["RSTD2"], rSB2["NMR2"]], w=[rxn2f])
                kb.op("vector", lambda e, par=par: e.tensor_copy(
                    out=xn2b[par][:], in_=xn2f), r=[rxn2f], w=[rxn2b[par]])

            def c2_fpT(tt_):
                par = tt_ % 2
                for cc in range(CCH):
                    kb.op("tensor", lambda e, cc=cc, par=par: e.transpose(
                        xtf_v[par][:, cc * 128:(cc + 1) * 128],
                        xn2f[:, cc * 128:(cc + 1) * 128], idf[:]),
                        r=[rxn2f, ridf], w=[racc[par]])
                kb.op("scalar", lambda e, par=par: e.activation(
                    out=xn2fT[par], in_=xtf_v[par], func=AF.Copy),
                    r=[racc[par]], w=[rxn2fT[par]])

            def c2_bfT(tt_):
                par = tt_ % 2
                for cc in range(CCH):
                    kb.op("tensor", lambda e, cc=cc, par=par: e.transpose(
                        xt_v[par][:, cc * 128:(cc + 1) * 128],
                        xn2b[par][:, cc * 128:(cc + 1) * 128], idb[:]),
                        r=[rxn2b[par], ridb], w=[racc[par]])
                kb.op("scalar", lambda e, tt_=tt_, par=par: e.activation(
                    out=XTv[:, :, tt_ * 128:(tt_ + 1) * 128],
                    in_=xt_v[par][:].rearrange("p (cc t) -> p cc t", cc=CCH),
                    func=AF.Copy), r=[racc[par]], w=[rXT[tt_]])

            def c2_router(tt_):
                par = tt_ % 2
                for cc in range(CCH):
                    kb.op("tensor", lambda e, cc=cc, par=par: e.matmul(
                        lgt_ps, xn2fT[par][:, cc * 128:(cc + 1) * 128],
                        rtb[:, cc * NE:(cc + 1) * NE],
                        start=(cc == 0), stop=(cc == CCH - 1)),
                        r=[rxn2fT[par], rrtb], w=[rg[0]])
                kb.op("scalar", lambda e, tt_=tt_: e.activation(
                    out=LG[:, NE * tt_:NE * (tt_ + 1)], in_=lgt_ps,
                    func=AF.Copy), r=[rg[0]], w=[rLGh[tt_ // 8]])

            def router_half_ops(h):
                TH = TT // 2
                c0, c1 = h * 128, (h + 1) * 128
                LGh = LG[:, c0:c1]
                SCCh = SCC[:, c0:c1]
                SBAh = SBA[:, c0:c1]
                WRKh = WRK[:, c0:c1]
                CMBh = CMB[:, c0:c1]
                KMFh = KMF[:, c0:c1]
                COMBBh = COMBB[:, c0:c1]
                GMU8h = GMU8[:, c0:c1]
                BIAS16h = BIAS16[:, c0:c1]
                NEGBh = NEGB[:, c0:c1]
                GSAh = GSA[:, h * 32:(h + 1) * 32]
                GTAh = GTA[:, h * 32:(h + 1) * 32]
                GTBh = GTB[:, h * 32:(h + 1) * 32]
                GMFh = GMF[:, h * 32:(h + 1) * 32]
                T2Ch = T2C[:, h * 8:(h + 1) * 8]
                MKCh = MKC[:, h * 8:(h + 1) * 8]
                THRh = THR[:, h * 8:(h + 1) * 8]
                WSCh = WSC[:, h * 8:(h + 1) * 8]
                ops = []

                def rop(eng, fn):
                    ops.append(lambda eng=eng, fn=fn: kb.op(
                        eng, fn, r=[rLGh[h], rRH[h], rcb2],
                        w=[rLGh[h], rRH[h]]))

                def v3(ap):
                    return ap.rearrange("p (t e) -> p t e", e=NE)

                def g3(ap):
                    return ap.rearrange("p (t g) -> p t g", g=4)

                def bct(ap, n):
                    return ap.rearrange("p (t o) -> p t o", o=1).broadcast_to(
                        (128, TH, n))

                rop("scalar", lambda e: e.activation(
                    out=SCCh, in_=LGh, func=AF.Exp, scale=-1.0))
                rop("vector", lambda e: e.tensor_scalar(
                    out=SCCh, in0=SCCh, scalar1=1.0, scalar2=None, op0=OP.add))
                rop("vector", lambda e: e.reciprocal(out=LGh, in_=SCCh))
                rop("vector", lambda e: e.tensor_add(
                    out=SBAh, in0=LGh, in1=BIAS16h))
                sl = [SBAh[:, rr:128:4] for rr in range(4)]
                pairs = [(0, 1), (0, 2), (0, 3), (1, 2), (1, 3), (2, 3)]
                first = True
                for (a_, b_) in pairs:
                    dst = GTAh if first else GTBh
                    rop("vector", lambda e, a_=a_, b_=b_, dst=dst: e.tensor_add(
                        out=dst, in0=sl[a_], in1=sl[b_]))
                    if not first:
                        rop("vector", lambda e: e.tensor_max(
                            out=GTAh, in0=GTAh, in1=GTBh))
                    first = False
                rop("vector", lambda e: e.tensor_copy(out=GSAh, in_=GTAh))
                gl = [GSAh[:, gg:32:4] for gg in range(4)]
                first = True
                for (a_, b_) in pairs:
                    dst = T2Ch if first else MKCh
                    rop("vector", lambda e, a_=a_, b_=b_, dst=dst:
                        e.tensor_tensor(
                            out=dst, in0=gl[a_], in1=gl[b_], op=OP.min))
                    if not first:
                        rop("vector", lambda e: e.tensor_max(
                            out=T2Ch, in0=T2Ch, in1=MKCh))
                    first = False
                rop("vector", lambda e: e.tensor_tensor(
                    out=g3(GMFh), in0=g3(GSAh),
                    in1=T2Ch.rearrange("p (t o) -> p t o", o=1).broadcast_to(
                        (128, TH, 4)), op=OP.is_ge))
                rop("vector", lambda e: e.tensor_copy(
                    out=GMU8h.rearrange("p (t g r) -> p t g r", g=4, r=4),
                    in_=g3(GMFh).rearrange("p t (g o) -> p t g o", o=1)
                        .broadcast_to((128, TH, 4, 4))))
                rop("vector", lambda e: e.select(
                    out=WRKh, mask=GMU8h, on_true=SBAh, on_false=NEGBh))
                rop("vector", lambda e: e.tensor_copy(out=SCCh, in_=WRKh))
                for _ in range(3):
                    rop("vector", lambda e: e.reduce_max(
                        MKCh, v3(SCCh), axis=AX))
                    rop("vector", lambda e: e.tensor_tensor(
                        out=v3(CMBh), in0=v3(SCCh), in1=bct(MKCh, NE),
                        op=OP.is_ge))
                    rop("vector", lambda e: e.scalar_tensor_tensor(
                        out=SCCh, in0=CMBh, scalar=-1e30, in1=SCCh,
                        op0=OP.mult, op1=OP.add))
                rop("vector", lambda e: e.reduce_max(THRh, v3(SCCh), axis=AX))
                rop("vector", lambda e: e.tensor_tensor(
                    out=v3(KMFh), in0=v3(WRKh), in1=bct(THRh, NE),
                    op=OP.is_ge))
                rop("vector", lambda e: e.tensor_mul(
                    out=SCCh, in0=LGh, in1=KMFh))
                rop("vector", lambda e: e.reduce_sum(WSCh, v3(SCCh), axis=AX))
                rop("vector", lambda e: e.tensor_scalar(
                    out=WSCh, in0=WSCh, scalar1=1e-20, scalar2=None,
                    op0=OP.add))
                rop("vector", lambda e: e.reciprocal(out=T2Ch, in_=WSCh))
                rop("vector", lambda e: e.tensor_mul(
                    out=v3(CMBh), in0=v3(SCCh), in1=bct(T2Ch, NE)))
                rop("vector", lambda e: e.tensor_copy(out=COMBBh, in_=CMBh))
                return ops

            def emit_cbtr_half(h):
                for tt_ in range(h * 8, h * 8 + 8):
                    kb.op("tensor", lambda e, tt_=tt_: e.transpose(
                        cbtr_ps, COMBB[:, NE * tt_:NE * (tt_ + 1)], idb[:]),
                        r=[rRH[0], rRH[1], ridb], w=[racc[1]])
                    kb.op("vector", lambda e, tt_=tt_: e.tensor_copy(
                        out=cbT[:, tt_ * 128:(tt_ + 1) * 128], in_=cbtr_ps),
                        r=[racc[1]], w=[rcbT])
                    kb.op("vector", lambda e: e.tensor_copy(
                        out=s4[:, 0:1], in_=s4[:, 0:1]),
                        r=[racc[1]], w=[rs4])

            c2_stage1(0)
            c2_fpT(0)
            rh0 = None
            for tt_ in range(TT):
                if tt_ + 1 < TT:
                    c2_stage1(tt_ + 1)
                c2_bfT(tt_)
                if tt_ + 1 < TT:
                    c2_fpT(tt_ + 1)
                c2_router(tt_)
                if tt_ == 7:
                    rh0 = router_half_ops(0)
                elif tt_ > 7:
                    for _ in range(min(5, len(rh0))):
                        rh0.pop(0)()


            def owner(b, i):
                h0 = (b * HCB + i) * 128
                if h0 < 2048:
                    return None
                return (h0 - 2048) // 256

            blk_experts = []
            for b in range(NBLK):
                es = []
                for i in range(HCB):
                    e_ = owner(b, i)
                    if e_ is not None and e_ not in es:
                        es.append(e_)
                blk_experts.append(es)

            def cb_slot(b, e_):
                return (b % 2) * 3 + blk_experts[b].index(e_)

            def cbB_dma(b):
                for e_ in blk_experts[b]:
                    s_ = cb_slot(b, e_)
                    kb.op("sync", lambda e, e_=e_, s_=s_: e.dma_start(
                        out=cbB[s_][:],
                        in_=cbt_d[e_:e_ + 1, :].to_broadcast((128, TOK))),
                        r=[rcbt_d, rRH[0], rRH[1], rLGh[0], rLGh[1]],
                        w=[rcbB[s_]], dma=True)

            gubv = [gub[p][:].rearrange("p (cc h) -> p cc h", cc=CCH)
                    for p in (0, 1)]
            dnbv = [dnb[p][:].rearrange("p (hc c) -> p hc c", hc=HCB)
                    for p in (0, 1)]

            def emit_down(b, i, lh, rlh, bpar, t0, t1):
                for tl, tile in ((0, t0), (1, t1)):
                    for nh in range(2):
                        kb.op("tensor",
                              lambda e, tl=tl, nh=nh, i=i, lh=lh,
                              bpar=bpar: e.matmul(
                                  acc_ps[tl][:, nh * 512:(nh + 1) * 512],
                                  lh[:, tl * 128:(tl + 1) * 128],
                                  dnbv[bpar][:, i, nh * 512:(nh + 1) * 512],
                                  start=(i == 0), stop=(i == HCB - 1)),
                              r=[rlh, rdnb[bpar]], w=[racc[tl]])
                if i == HCB - 1:
                    for tl, tile in ((0, t0), (1, t1)):
                        xt = X[:, tile * C:(tile + 1) * C]
                        kb.op("vector", lambda e, xt=xt, tl=tl: e.tensor_add(
                            out=xt, in0=xt, in1=acc_ps[tl][:]),
                            r=[racc[tl], rX[tile]], w=[rX[tile]])
                        if b == NBLK - 1:
                            kb.op("sync", lambda e, xt=xt, tile=tile:
                                  e.dma_start(
                                      out=out_d[tile * 128:(tile + 1) * 128, :],
                                      in_=xt),
                                  r=[rX[tile]], dma=True)

            for f in rh0:
                f()
            emit_cbtr_half(0)
            for f in router_half_ops(1):
                f()
            emit_cbtr_half(1)
            kb.op("sync", lambda e: e.dma_start(out=cbt_d, in_=cbT[:]),
                  r=[rcbT], w=[rcbt_d], dma=True)
            cbB_dma(2)
            cbB_dma(3)

            for b in range(NBLK):
                bpar = b % 2
                pend = None
                for pair in range(NPAIR):
                    t0, t1 = 2 * pair, 2 * pair + 1
                    for i in range(HCB):
                        par = i % 2
                        e_ = owner(b, i)
                        for cc in range(CCH):
                            kb.op("tensor",
                                  lambda e, cc=cc, i=i, bpar=bpar, par=par,
                                  pair=pair: e.matmul(
                                      g_ps[par][:],
                                      gubv[bpar][:, cc, i * 128:(i + 1) * 128],
                                      XTv[:, cc, pair * 256:(pair + 1) * 256],
                                      start=(cc == 0), stop=(cc == CCH - 1)),
                                  r=[rgub[bpar], rXT[t0], rXT[t1]],
                                  w=[rg[par]])
                        for cc in range(CCH):
                            kb.op("tensor",
                                  lambda e, cc=cc, i=i, bpar=bpar, par=par,
                                  pair=pair: e.matmul(
                                      u_ps[par][:],
                                      gubv[bpar][:, cc,
                                                 768 + i * 128:768 + (i + 1) * 128],
                                      XTv[:, cc, pair * 256:(pair + 1) * 256],
                                      start=(cc == 0), stop=(cc == CCH - 1)),
                                  r=[rgub[bpar], rXT[t0], rXT[t1]],
                                  w=[ru[par]])
                        kb.op("scalar", lambda e, par=par: e.activation(
                            out=sg[par][:], in_=g_ps[par][:], func=AF.Silu),
                            r=[rg[par]], w=[rsg[par]])
                        kb.op("vector", lambda e, par=par: e.tensor_mul(
                            out=hb[par][:], in0=sg[par][:], in1=u_ps[par][:]),
                            r=[rsg[par], ru[par]], w=[rhb[par]])
                        if e_ is not None:
                            s_ = cb_slot(b, e_)
                            kb.op("vector",
                                  lambda e, par=par, s_=s_, pair=pair:
                                  e.tensor_mul(
                                      out=hbb[par][:], in0=hb[par][:],
                                      in1=cbB[s_][:, pair * 256:(pair + 1) * 256]),
                                  r=[rhb[par], rcbB[s_]], w=[rhbb[par]])
                            lh, rlh = hbb[par], rhbb[par]
                        else:
                            lh, rlh = hb[par], rhb[par]
                        if pend is not None:
                            pend()
                        pend = (lambda b=b, i=i, lh=lh, rlh=rlh, bpar=bpar,
                                t0=t0, t1=t1:
                                emit_down(b, i, lh, rlh, bpar, t0, t1))
                if pend is not None:
                    pend()
                    pend = None
                if b + 2 < NBLK:
                    w_dma(b + 2)
                    if b >= 2:
                        cbB_dma(b + 2)

            kb.finalize()
            with nc.Block() as block:
                kb.emit(block, sems, dsems, final=True)
    return nc


_NC_CACHE = {}


def _pw_mats():
    W = 8

    def lin(s, k):
        w = min(s, W)
        if not (s - W + k >= 0 and s >= 1):
            return 0.0
        if w > 1:
            j = k - (W - w)
            return 0.1 + 0.9 * j / max(w - 1, 1)
        return 0.1

    PA_g = np.zeros((128, 128), np.float32)
    PB = np.zeros((7, 128), np.float32)
    for sp in range(128):
        for k in range(8):
            dp = sp + k
            val = 0.1 + 0.9 * k / 7.0
            if dp < 128:
                PA_g[dp, sp] = val
            elif dp - 128 < 7:
                PB[dp - 128, sp] = val
    PA_s = np.zeros((128, 128), np.float32)
    for sp in range(128):
        for k in range(8):
            dp = sp + k
            if dp < 128:
                PA_s[dp, sp] = lin(sp, k)
    bf = ml_dtypes.bfloat16
    return PA_s.astype(bf), PA_g.astype(bf), PB.astype(bf)


def kernel(L0_attn_embeddings, L0_attn_guidance_trajectories,
           L0_ffn_router_w, L0_ffn_correction_bias,
           L0_ffn_gate_w, L0_ffn_up_w, L0_ffn_down_w,
           L0_ffn_shared_gate_w, L0_ffn_shared_up_w, L0_ffn_shared_down_w):
    f = np.float32
    emb = np.asarray(L0_attn_embeddings, f)
    guid = np.ascontiguousarray(np.asarray(L0_attn_guidance_trajectories, f))
    guidT = np.ascontiguousarray(guid.T)
    rT = np.ascontiguousarray(np.asarray(L0_ffn_router_w, f).T)
    biasr = np.ascontiguousarray(np.asarray(L0_ffn_correction_bias, f)
                                 .reshape(1, NE))
    gw = np.asarray(L0_ffn_gate_w, f)
    uw = np.asarray(L0_ffn_up_w, f)
    dw = np.asarray(L0_ffn_down_w, f)
    sgw = np.asarray(L0_ffn_shared_gate_w, f)
    suw = np.asarray(L0_ffn_shared_up_w, f)
    sdw = np.asarray(L0_ffn_shared_down_w, f)

    gate_all = np.concatenate(
        [sgw.T, gw.transpose(1, 0, 2).reshape(C, 4096)], axis=1)
    up_all = np.concatenate(
        [suw.T, uw.transpose(1, 0, 2).reshape(C, 4096)], axis=1)
    wup = np.empty((C, 2 * HTOT), f)
    for b in range(NBLK):
        wup[:, b * 1536:b * 1536 + 768] = gate_all[:, b * 768:(b + 1) * 768]
        wup[:, b * 1536 + 768:(b + 1) * 1536] = up_all[:, b * 768:(b + 1) * 768]
    wup = np.ascontiguousarray(wup)
    wdn = np.ascontiguousarray(
        np.concatenate([sdw.T, dw.reshape(4096, C)], axis=0))

    PA_s, PA_g, PB = _pw_mats()
    ident = np.eye(128, dtype=ml_dtypes.bfloat16)
    identf = np.eye(128, dtype=f)

    in_maps = []
    for c in range(NCORES):
        bb, half = c // 2, c % 2
        start = half * 2048
        z = np.zeros((R, C), f)
        lo = start - 8
        s0, s1 = max(0, lo), min(S, start + 2169)
        z[s0 - lo:s1 - lo] = emb[bb, s0:s1]
        crow = np.zeros((1, 32), f)
        crow[0, 4:8] = NEGBIG
        crow[0, 16:32] = NEGBIG
        crow2 = np.empty((2, 256), f)
        crow2[0] = np.tile(np.asarray(L0_ffn_correction_bias, f), 16)
        crow2[1] = NEGBIG
        in_maps.append(dict(
            emb=z, pwa_s=(PA_s if half == 0 else PA_g), pwa_g=PA_g, pwb=PB,
            idb=ident, idf=identf, guid=guid, guidT=guidT, rT=rT,
            biasr=biasr, crow=crow, crow2=crow2,
            wup=wup, wdn=wdn))

    if "nc" not in _NC_CACHE:
        _NC_CACHE["nc"] = build_nc()
    nc = _NC_CACHE["nc"]
    res = run_bass_kernel_spmd(nc, in_maps, core_ids=list(range(NCORES)))
    out = np.empty((B, S, C), f)
    for c in range(NCORES):
        bb, half = c // 2, c % 2
        out[bb, half * 2048:(half + 1) * 2048] = res.results[c]["out"]
    return out


# version 47
# speedup vs baseline: 1.0059x; 1.0037x over previous
import numpy as np
import ml_dtypes
from contextlib import ExitStack

import concourse.bass as bass
import concourse.mybir as mybir
from concourse.bass_utils import run_bass_kernel_spmd

F32 = mybir.dt.float32
BF16 = mybir.dt.bfloat16
AX = mybir.AxisListType.X
AF = mybir.ActivationFunctionType
OP = mybir.AluOpType

B, S, C = 4, 4096, 1024
NCORES = 8
TOK = 2048
R = 2177
TT = 16
DTl = 17
CCH = 8
G = 512
GCH = 4
NE = 16
HTOT = 6144
NBLK = 8
HBLK = 768
HCB = 6
NPAIR = 8
INV_C = 1.0 / 1024.0
NEGBIG = -1e30


class Res:
    __slots__ = ("w", "r")

    def __init__(self):
        self.w = None
        self.r = []


class KB:
    ENG = ("tensor", "vector", "scalar", "gpsimd", "sync")
    NDSEM = 32

    def __init__(self):
        self.ops = {e: [] for e in self.ENG}
        self.barrier_tokens = None
        self.barrier_todo = set()
        self.dma_n = 0
        self.dma_uses = [0] * self.NDSEM

    def op(self, eng, emit, r=(), w=(), dma=False, deps=()):
        tok = (eng, len(self.ops[eng]))
        dl = list(deps)
        for x in r:
            if x.w is not None:
                dl.append(x.w)
        for x in w:
            if x.w is not None:
                dl.append(x.w)
            dl.extend(x.r)
        if self.barrier_tokens is not None and eng in self.barrier_todo:
            dl.extend(self.barrier_tokens)
            self.barrier_todo.discard(eng)
        if dma:
            idx = self.dma_n % self.NDSEM
            self.dma_n += 1
            self.dma_uses[idx] += 1
            dsem = (idx, 16 * self.dma_uses[idx])
        else:
            dsem = None
        self.ops[eng].append([emit, dl, dma, dsem])
        for x in r:
            x.r.append(tok)
        for x in w:
            x.w = tok
            x.r = []
        return tok

    def barrier(self):
        toks = []
        for e in self.ENG:
            if self.ops[e]:
                toks.append((e, len(self.ops[e]) - 1))
        self.barrier_tokens = toks
        self.barrier_todo = set(self.ENG)

    def finalize(self):
        self.need = {e: [False] * len(self.ops[e]) for e in self.ENG}
        for e in self.ENG:
            for _, deps, _, _ in self.ops[e]:
                for (pe, pi) in deps:
                    po = self.ops[pe][pi]
                    if po[2]:
                        continue
                    if pe == e == "tensor":
                        continue
                    self.need[pe][pi] = True
        self.incval = {}
        for e in self.ENG:
            c = 0
            iv = []
            for i, o in enumerate(self.ops[e]):
                if self.need[e][i]:
                    c += 1
                iv.append(c)
            self.incval[e] = iv

    def emit(self, block, sems, dsems, final):
        water = {e: {pe: 0 for pe in self.ENG} for e in self.ENG}
        dwater = {e: [0] * self.NDSEM for e in self.ENG}

        def mk(e):
            plan = self.ops[e]
            need = self.need[e]
            wtr = water[e]
            dwtr = dwater[e]

            def emitter(eng):
                for i, (emit, deps, dma, dsem) in enumerate(plan):
                    req = {}
                    dreq = {}
                    for (pe, pi) in deps:
                        po = self.ops[pe][pi]
                        if po[2]:
                            di, dv = po[3]
                            if dv > dreq.get(di, 0):
                                dreq[di] = dv
                            continue
                        if pe == e == "tensor":
                            continue
                        v = self.incval[pe][pi]
                        if v > req.get(pe, 0):
                            req[pe] = v
                    for pe, v in req.items():
                        if v > wtr[pe]:
                            eng.wait_ge(sems[pe], v)
                            wtr[pe] = v
                    for di, dv in dreq.items():
                        if dv > dwtr[di]:
                            eng.wait_ge(dsems[di], dv)
                            dwtr[di] = dv
                    if dma and dsem[1] > 16:
                        prev = dsem[1] - 16
                        if prev > dwtr[dsem[0]]:
                            eng.wait_ge(dsems[dsem[0]], prev)
                            dwtr[dsem[0]] = prev
                    inst = emit(eng)
                    if dma:
                        inst.then_inc(dsems[dsem[0]], 16)
                    elif need[i]:
                        inst.then_inc(sems[e], 1)
                if final:
                    for di in range(self.NDSEM):
                        tot = 16 * self.dma_uses[di]
                        if tot > 0 and dwtr[di] < tot:
                            eng.wait_ge(dsems[di], tot)
                            dwtr[di] = tot
            return emitter

        for e in self.ENG:
            if self.ops[e]:
                getattr(block, e)(mk(e))


def build_nc():
    nc = bass.Bass("TRN2", target_bir_lowering=False)
    kb = KB()

    emb = nc.dram_tensor("emb", [R, C], F32, kind="ExternalInput").ap()
    pwa_s = nc.dram_tensor("pwa_s", [128, 128], BF16, kind="ExternalInput").ap()
    pwa_g = nc.dram_tensor("pwa_g", [128, 128], BF16, kind="ExternalInput").ap()
    pwb = nc.dram_tensor("pwb", [7, 128], BF16, kind="ExternalInput").ap()
    idb_d = nc.dram_tensor("idb", [128, 128], BF16, kind="ExternalInput").ap()
    idf_d = nc.dram_tensor("idf", [128, 128], F32, kind="ExternalInput").ap()
    guid = nc.dram_tensor("guid", [G, C], F32, kind="ExternalInput").ap()
    guidT = nc.dram_tensor("guidT", [C, G], F32, kind="ExternalInput").ap()
    rT = nc.dram_tensor("rT", [C, NE], F32, kind="ExternalInput").ap()
    biasr = nc.dram_tensor("biasr", [1, NE], F32, kind="ExternalInput").ap()
    crow = nc.dram_tensor("crow", [1, 32], F32, kind="ExternalInput").ap()
    crow2 = nc.dram_tensor("crow2", [2, 256], F32, kind="ExternalInput").ap()
    zrow = nc.dram_tensor("zrow", [1, 128], F32, kind="ExternalInput").ap()
    wup = nc.dram_tensor("wup", [C, 2 * HTOT], F32, kind="ExternalInput").ap()
    wdn = nc.dram_tensor("wdn", [HTOT, C], F32, kind="ExternalInput").ap()
    out_d = nc.dram_tensor("out", [TOK, C], F32, kind="ExternalOutput").ap()

    srow_d = nc.dram_tensor("srow_scr", [1, G], BF16).ap()
    cbt_d = nc.dram_tensor("cbt_scr", [NE, TOK], BF16).ap()

    o_tiles = out_d.rearrange("(t p) c -> t p c", p=128)
    emb_r = emb

    def res():
        return Res()

    with ExitStack() as octx:
        def osb(name, shape, dtype):
            return octx.enter_context(nc.sbuf_tensor(name, shape, dtype))

        sems = {}
        for e in KB.ENG:
            sems[e] = octx.enter_context(nc.semaphore("sem_" + e))
        dsems = [octx.enter_context(nc.semaphore(f"dsem{i}"))
                 for i in range(KB.NDSEM)]

        X = osb("X", [128, TT * C], F32)
        idb = osb("idb_sb", [128, 128], BF16)
        idf = osb("idf_sb", [128, 128], F32)
        rtb = osb("rtb", [128, CCH * NE], F32)
        biasb = osb("biasb", [128, NE], F32)
        pwaS = osb("pwaS", [128, 128], BF16)
        pwaG = osb("pwaG", [128, 128], BF16)
        pwbS = osb("pwbS", [7, 128], BF16)
        CS2 = {nm: osb("cs_" + nm, [128, 20], F32) for nm in ("SUM2", "SQ2")}
        SB2 = {nm: osb("c_" + nm, [128, 20], F32)
               for nm in ("MS2", "V12", "T2S2", "RSTD2", "NMR2")}

        rX = [res() for _ in range(TT)]
        rgub = [res(), res()]
        rdnb = [res(), res()]
        ridb, rrtb, rbiasb = res(), res(), res()
        ridf = res()
        rpwaS, rpwaG, rpwbS = res(), res(), res()
        rCS2 = {nm: res() for nm in CS2}
        rSB2 = {nm: res() for nm in SB2}
        rsrow_d, rcbt_d = res(), res()

        with ExitStack() as g1:
            def sb1(name, shape, dtype):
                return g1.enter_context(nc.sbuf_tensor(name, shape, dtype))

            def ps1(name, shape, dtype):
                return g1.enter_context(nc.psum_tensor(name, shape, dtype))

            guidb = sb1("guidb", [128, GCH * C], BF16)
            guidTs = sb1("guidTs", [128, CCH * G], BF16)
            Sfull = sb1("Sfull", [128, G], BF16)
            WB = sb1("WBar", [128, DTl * C], BF16)
            mwcols = sb1("mwcols", [128, DTl], BF16)
            eb = [sb1(f"eb{p}", [128, C], F32) for p in (0, 1)]
            ebB = [sb1(f"ebB{p}", [128, C], F32) for p in (0, 1)]
            xbs = [sb1(f"xbs{p}", [128, C], F32) for p in (0, 1)]
            xnb = [sb1(f"xnb{p}", [128, C], F32) for p in (0, 1)]
            dbf6 = [sb1(f"dbf6_{p}", [128, C], F32) for p in range(6)]
            nb6 = [sb1(f"nb6_{p}", [128, C], BF16) for p in range(6)]
            nsb = [sb1(f"nsb{p}", [128, C], BF16) for p in (0, 1)]
            Eb = [sb1(f"Eb{p}", [128, G], BF16) for p in (0, 1)]
            etb = [sb1(f"etb{p}", [128, G], BF16) for p in (0, 1)]
            t2b = [sb1(f"t2b{p}", [128, C], BF16) for p in (0, 1)]
            sqb = [sb1(f"sqb{p}", [128, C], BF16) for p in (0, 1)]
            colsB = [sb1(f"colsB{p}", [128, 16], F32) for p in (0, 1)]
            colsS = [sb1(f"colsS{p}", [128, 16], F32) for p in (0, 1)]
            colsD0 = sb1("colsD0", [128, 132], F32)
            s4 = sb1("s4", [128, 4], F32)
            s4b = sb1("s4b", [128, 4], BF16)
            SLAB = {}
            for nm in ("SUMS", "SQS", "MS", "V1", "T2S", "RSTD", "NMR",
                       "RSTDS", "NMRS", "XS", "XQ",
                       "MAG2", "MC", "IVR", "INVS", "MAGT", "TMW", "T1S",
                       "RQS", "MWS", "ACO"):
                SLAB[nm] = sb1("sl_" + nm, [128, 20], F32)
            TSTG = sb1("tstg", [64, 128], F32)
            rTSTG = res()

            nT_ps = ps1("nT_ps", [128, C], BF16)
            dots_ps = ps1("dots_ps", [128, G], F32)
            ET_ps = ps1("ET_ps", [128, G], BF16)
            infl_ps = ps1("infl_ps", [128, C], F32)
            num_ps = ps1("num_ps", [128, C], F32)
            den_ps = ps1("den_ps", [128, 8], F32)

            rguidb, rguidTs, rSfull = res(), res(), res()
            rWB = [res() for _ in range(DTl)]
            reb = [res(), res()]
            rebB = [res(), res()]
            rxbs = [res(), res()]
            rxnb = [res(), res()]
            rdbf6 = [res() for _ in range(6)]
            rnb6 = [res() for _ in range(6)]
            rnsb = [res(), res()]
            rEb = [res(), res()]
            retb = [res(), res()]
            rt2b = [res(), res()]
            rsqb = [res(), res()]
            rs4, rs4b = res(), res()
            rnT, rdots, rET, rinfl, rnum, rden = (res(), res(), res(), res(),
                                                  res(), res())
            rcolsB = [res(), res()]
            rcolsS = [res(), res()]
            rcolsD = [res()]
            rSLAB = {nm: res() for nm in SLAB}
            rmwG = [res(), res(), res()]

            def mkcol(tensor):
                def col(i):
                    return tensor[:, 4 * i:4 * i + 1]
                return col

            def colv(i):
                return colsD0[:, 4 * i:4 * i + 16:4]

            colD0 = mkcol(colsD0)

            kb.op("scalar", lambda e: e.dma_start(out=idb[:], in_=idb_d),
                  w=[ridb], dma=True)
            kb.op("scalar", lambda e: e.dma_start(out=idf[:], in_=idf_d),
                  w=[ridf], dma=True)
            kb.op("scalar", lambda e: e.dma_start(out=pwaS[:], in_=pwa_s),
                  w=[rpwaS], dma=True)
            kb.op("scalar", lambda e: e.dma_start(out=pwaG[:], in_=pwa_g),
                  w=[rpwaG], dma=True)
            kb.op("scalar", lambda e: e.dma_start(out=pwbS[:], in_=pwb),
                  w=[rpwbS], dma=True)
            kb.op("scalar", lambda e: e.dma_start(
                out=biasb[:], in_=biasr.to_broadcast((128, NE))),
                w=[rbiasb], dma=True)
            kb.op("sync", lambda e: e.dma_start(
                out=rtb[:].rearrange("p (cc e) -> p cc e", cc=CCH),
                in_=rT.rearrange("(cc p) e -> p cc e", p=128)),
                w=[rrtb], dma=True)
            kb.op("gpsimd", lambda e: e.dma_start(
                out=guidb[:].rearrange("p (gc c) -> p gc c", gc=GCH),
                in_=guid.rearrange("(gc p) c -> p gc c", p=128)),
                w=[rguidb], dma=True)
            kb.op("gpsimd", lambda e: e.dma_start(
                out=guidTs[:].rearrange("p (cc g) -> p cc g", cc=CCH),
                in_=guidT.rearrange("(cc p) g -> p cc g", p=128)),
                w=[rguidTs], dma=True)
            kb.op("scalar", lambda e: e.dma_start(
                out=SLAB["XS"][:, 16:17],
                in_=zrow[0:1, 0:1].to_broadcast((128, 1))),
                w=[rSLAB["XS"]], dma=True)
            kb.op("scalar", lambda e: e.dma_start(
                out=SLAB["XQ"][:, 16:17],
                in_=zrow[0:1, 0:1].to_broadcast((128, 1))),
                w=[rSLAB["XQ"]], dma=True)
            kb.op("scalar", lambda e: e.dma_start(
                out=TSTG[0:64, 0:128], in_=zrow.to_broadcast((64, 128))),
                w=[rTSTG], dma=True)
            for t in range(TT):
                kb.op("sync" if t % 2 == 0 else "scalar",
                      lambda e, t=t: e.dma_start(
                          out=X[:, t * C:(t + 1) * C],
                          in_=emb_r[8 + 128 * t: 8 + 128 * (t + 1), :]),
                      w=[rX[t]], dma=True)

            def newton_rsqrt(y_ap, v_ap, rres, t_ap, rextra=()):
                rr = [rres] + list(rextra)
                kb.op("vector", lambda e: e.tensor_mul(
                    out=t_ap, in0=y_ap, in1=y_ap), r=rr, w=rr)
                kb.op("vector", lambda e: e.scalar_tensor_tensor(
                    out=t_ap, in0=v_ap, scalar=-0.5, in1=t_ap,
                    op0=OP.mult, op1=OP.mult), r=rr, w=rr)
                kb.op("vector", lambda e: e.tensor_scalar(
                    out=t_ap, in0=t_ap, scalar1=1.5, scalar2=None,
                    op0=OP.add), r=rr, w=rr)
                kb.op("vector", lambda e: e.tensor_mul(
                    out=y_ap, in0=y_ap, in1=t_ap), r=rr, w=rr)

            def SL(nm, a=0, b=DTl):
                return SLAB[nm][:, a:b]

            def normedb_one(d):
                par6 = d % 6
                kb.op("scalar", lambda e, d=d, par6=par6: e.activation(
                    out=nb6[par6][:], in_=dbf6[par6][:], func=AF.Copy,
                    scale=SLAB["INVS"][:, d:d + 1]),
                    r=[rdbf6[par6], rSLAB["INVS"]], w=[rnb6[par6]])

            def s5_alpha(d):
                par = d % 2
                par6 = d % 6
                for cc in range(CCH):
                    kb.op("tensor", lambda e, cc=cc, par6=par6: e.transpose(
                        nT_ps[:, cc * 128:(cc + 1) * 128],
                        nb6[par6][:, cc * 128:(cc + 1) * 128], idb[:]),
                        r=[rnb6[par6], ridb], w=[rnT])
                kb.op("scalar", lambda e, par=par: e.activation(
                    out=nsb[par][:], in_=nT_ps[:], func=AF.Copy),
                    r=[rnT], w=[rnsb[par]])

            def s5_mid(d):
                par = d % 2
                cS = mkcol(colsS[par])
                rcS = rcolsS[par]
                for cc in range(CCH):
                    kb.op("tensor", lambda e, cc=cc, par=par: e.matmul(
                        dots_ps[:], nsb[par][:, cc * 128:(cc + 1) * 128],
                        guidTs[:, cc * G:(cc + 1) * G],
                        start=(cc == 0), stop=(cc == CCH - 1)),
                        r=[rnsb[par], rguidTs], w=[rdots])
                kb.op("vector", lambda e, cS=cS: e.reduce_max(
                    cS(0), dots_ps[:], axis=AX, negate=True),
                    r=[rdots], w=[rcS])
                kb.op("scalar", lambda e, par=par, cS=cS: e.activation(
                    out=Eb[par][:], in_=dots_ps[:], func=AF.Exp,
                    bias=cS(0)),
                    r=[rdots, rcS], w=[rEb[par]])

            def s5_fin(d):
                par = d % 2
                par6 = d % 6
                cS = mkcol(colsS[par])
                rcS = rcolsS[par]
                kb.op("vector", lambda e, par=par, cS=cS: e.reduce_sum(
                    cS(1), Eb[par][:], axis=AX), r=[rEb[par]], w=[rcS])
                kb.op("vector", lambda e, cS=cS: e.reciprocal(
                    out=cS(2), in_=cS(1)), r=[rcS], w=[rcS])
                kb.op("vector", lambda e, d=d, cS=cS: e.scalar_tensor_tensor(
                    out=cS(3), in0=SLAB["MWS"][:, d:d + 1], scalar=0.4,
                    in1=cS(2), op0=OP.mult, op1=OP.mult),
                    r=[rcS, rSLAB["MWS"]], w=[rcS])
                for gc in range(GCH):
                    kb.op("tensor", lambda e, gc=gc, par=par: e.transpose(
                        ET_ps[:, gc * 128:(gc + 1) * 128],
                        Eb[par][:, gc * 128:(gc + 1) * 128], idb[:]),
                        r=[rEb[par], ridb], w=[rET])
                kb.op("scalar", lambda e, par=par: e.activation(
                    out=etb[par][:], in_=ET_ps[:], func=AF.Copy),
                    r=[rET], w=[retb[par]])
                for gc in range(GCH):
                    for nh in range(2):
                        kb.op("tensor", lambda e, gc=gc, nh=nh, par=par:
                              e.matmul(
                                  infl_ps[:, nh * 512:(nh + 1) * 512],
                                  etb[par][:, gc * 128:(gc + 1) * 128],
                                  guidb[:, gc * C + nh * 512:
                                        gc * C + (nh + 1) * 512],
                                  start=(gc == 0), stop=(gc == GCH - 1)),
                              r=[retb[par], rguidb], w=[rinfl])
                kb.op("scalar", lambda e, par=par, cS=cS: e.activation(
                    out=t2b[par][:], in_=infl_ps[:], func=AF.Copy,
                    scale=cS(3)), r=[rinfl, rcS], w=[rt2b[par]])

            def s5_fin_b(d):
                par = d % 2
                par6 = d % 6
                kb.op("vector", lambda e, d=d, par6=par6, par=par:
                      e.scalar_tensor_tensor(
                          out=WB[:, d * C:(d + 1) * C], in0=nb6[par6][:],
                          scalar=SLAB["ACO"][:, d:d + 1], in1=t2b[par][:],
                          op0=OP.mult, op1=OP.add),
                      r=[rnb6[par6], rSLAB["ACO"], rt2b[par]], w=[rWB[d]])

            def batch_mag_p1(g0, g1):
                n = g1 - g0
                A = lambda nm: SLAB[nm][:, g0:g1]
                R = [rSLAB[nm] for nm in SLAB]

                def bop(fn, rd=(), wr=()):
                    kb.op("vector", fn, r=list(rd), w=list(wr))

                def sop(fn, rd=(), wr=()):
                    kb.op("scalar", fn, r=list(rd), w=list(wr))

                bop(lambda e: e.tensor_scalar(
                    out=A("MC"), in0=A("MAG2"), scalar1=1e-12, scalar2=None,
                    op0=OP.max), [rSLAB["MAG2"]], [rSLAB["MC"]])
                sop(lambda e: e.activation(out=A("T1S"), in_=A("MC"),
                                           func=AF.Ln),
                    [rSLAB["MC"]], [rSLAB["T1S"]])
                sop(lambda e: e.activation(out=A("IVR"), in_=A("T1S"),
                                           func=AF.Exp, scale=-0.5),
                    [rSLAB["T1S"]], [rSLAB["IVR"]])
                newton_rsqrt(A("IVR"), A("MC"), rSLAB["IVR"], A("T2S"),
                             rextra=[rSLAB["MC"], rSLAB["T2S"]])
                bop(lambda e: e.tensor_scalar(
                    out=A("T1S"), in0=A("MAG2"), scalar1=1e-12, scalar2=None,
                    op0=OP.is_gt), [rSLAB["MAG2"]], [rSLAB["T1S"]])
                bop(lambda e: e.tensor_mul(out=A("INVS"), in0=A("IVR"),
                                           in1=A("T1S")),
                    [rSLAB["IVR"], rSLAB["T1S"]], [rSLAB["INVS"]])

            def batch_mag_p2(g0, g1):
                A = lambda nm: SLAB[nm][:, g0:g1]

                def bop(fn, rd=(), wr=()):
                    kb.op("vector", fn, r=list(rd), w=list(wr))

                def sop(fn, rd=(), wr=()):
                    kb.op("scalar", fn, r=list(rd), w=list(wr))

                bop(lambda e: e.tensor_mul(out=A("MAGT"), in0=A("MC"),
                                           in1=A("IVR")),
                    [rSLAB["MC"], rSLAB["IVR"]], [rSLAB["MAGT"]])
                sop(lambda e: e.activation(out=A("TMW"), in_=A("MAGT"),
                                           func=AF.Exp, scale=-4.0),
                    [rSLAB["MAGT"]], [rSLAB["TMW"]])
                bop(lambda e: e.tensor_scalar(
                    out=A("T1S"), in0=A("TMW"), scalar1=-1.0, scalar2=1.0,
                    op0=OP.mult, op1=OP.add), [rSLAB["TMW"]], [rSLAB["T1S"]])
                bop(lambda e: e.tensor_scalar(
                    out=A("T2S"), in0=A("TMW"), scalar1=1.0, scalar2=None,
                    op0=OP.add), [rSLAB["TMW"]], [rSLAB["T2S"]])
                bop(lambda e: e.reciprocal(out=A("RQS"), in_=A("T2S")),
                    [rSLAB["T2S"]], [rSLAB["RQS"]])
                bop(lambda e: e.tensor_mul(out=A("MWS"), in0=A("T1S"),
                                           in1=A("RQS")),
                    [rSLAB["T1S"], rSLAB["RQS"]], [rSLAB["MWS"]])
                gidx = min(g0 // 6, 2)
                kb.op("vector", lambda e: e.tensor_copy(
                    out=mwcols[:, g0:g1], in_=A("MWS")),
                    r=[rSLAB["MWS"]], w=[rmwG[gidx]])
                bop(lambda e: e.tensor_scalar(
                    out=A("ACO"), in0=A("MWS"), scalar1=0.6, scalar2=None,
                    op0=OP.mult), [rSLAB["MWS"]], [rSLAB["ACO"]])

            def token_tile(tt_):
                colB = mkcol(colsB[tt_ % 2])
                rcB = rcolsB[tt_ % 2]
                pA = pwaS if tt_ == 0 else pwaG
                rpA = rpwaS if tt_ == 0 else rpwaG
                wb0 = WB[:, tt_ * C:(tt_ + 1) * C]
                wb1 = WB[0:7, (tt_ + 1) * C:(tt_ + 2) * C]
                for nh in range(2):
                    kb.op("tensor", lambda e, nh=nh, pA=pA, wb0=wb0: e.matmul(
                        num_ps[:, nh * 512:(nh + 1) * 512], pA[:],
                        wb0[:, nh * 512:(nh + 1) * 512],
                        start=True, stop=False),
                        r=[rpA, rWB[tt_]], w=[rnum])
                    kb.op("tensor", lambda e, nh=nh, wb1=wb1: e.matmul(
                        num_ps[:, nh * 512:(nh + 1) * 512], pwbS[:],
                        wb1[:, nh * 512:(nh + 1) * 512],
                        start=False, stop=True),
                        r=[rpwbS, rWB[tt_ + 1]], w=[rnum])
                kb.op("tensor", lambda e, pA=pA, tt_=tt_: e.matmul(
                    den_ps[:, 0:1], pA[:], mwcols[:, tt_:tt_ + 1],
                    start=True, stop=False),
                    r=[rpA, rmwG[min(tt_ // 6, 2)]], w=[rden])
                kb.op("tensor", lambda e, tt_=tt_: e.matmul(
                    den_ps[:, 0:1], pwbS[:], mwcols[0:7, tt_ + 1:tt_ + 2],
                    start=False, stop=True),
                    r=[rpwbS, rmwG[min((tt_ + 1) // 6, 2)]], w=[rden])
                kb.op("vector", lambda e: e.tensor_scalar(
                    out=colB(0), in0=den_ps[:, 0:1], scalar1=1e-8,
                    scalar2=None, op0=OP.max), r=[rden], w=[rcB])
                kb.op("vector", lambda e: e.reciprocal(
                    out=colB(1), in_=colB(0)), r=[rcB], w=[rcB])
                xt = X[:, tt_ * C:(tt_ + 1) * C]
                kb.op("vector", lambda e, xt=xt: e.scalar_tensor_tensor(
                    out=xt, in0=num_ps[:], scalar=colB(1), in1=xt,
                    op0=OP.mult, op1=OP.add),
                    r=[rnum, rcB, rX[tt_]], w=[rX[tt_]])

            kb.op("sync", lambda e: e.dma_start(
                out=eb[0][0:8, :], in_=emb_r[0:8, :]),
                w=[reb[0]], dma=True)
            kb.op("gpsimd", lambda e: e.tensor_tensor(
                out=xnb[0][0:8, :], in0=eb[0][0:8, :], in1=eb[0][0:8, :],
                op=OP.mult), r=[reb[0]], w=[rxnb[0]])
            for t in range(TT):
                par = t % 2
                kb.op("vector", lambda e, t=t: e.reduce_sum(
                    SLAB["XS"][:, t:t + 1], X[:, t * C:(t + 1) * C], axis=AX),
                    r=[rX[t]], w=[rSLAB["XS"]])
                kb.op("gpsimd", lambda e, t=t, par=par: e.tensor_tensor(
                    out=sqb[par][:], in0=X[:, t * C:(t + 1) * C],
                    in1=X[:, t * C:(t + 1) * C], op=OP.mult),
                    r=[rX[t]], w=[rsqb[par]])
                kb.op("vector", lambda e, t=t, par=par: e.reduce_sum(
                    SLAB["XQ"][:, t:t + 1], sqb[par][:], axis=AX),
                    r=[rsqb[par]], w=[rSLAB["XQ"]])
            dtv = dots_ps[:]
            itv = infl_ps[:]
            ntv = num_ps[:]
            kb.op("tensor", lambda e: e.transpose(
                dtv[0:17, 0:128], SLAB["XS"][:, 0:17], idf[:]),
                r=[rSLAB["XS"], ridf], w=[rdots])
            kb.op("tensor", lambda e: e.transpose(
                ntv[0:17, 0:128], SLAB["XQ"][:, 0:17], idf[:]),
                r=[rSLAB["XQ"], ridf], w=[rnum])
            kb.op("vector", lambda e: e.tensor_copy(
                out=TSTG[0:17, 8:128], in_=dtv[0:17, 0:120]),
                r=[rdots], w=[rTSTG])
            kb.op("vector", lambda e: e.tensor_copy(
                out=TSTG[32:49, 8:128], in_=ntv[0:17, 0:120]),
                r=[rnum], w=[rTSTG])
            kb.op("tensor", lambda e: e.transpose(
                itv[0:128, 0:17], TSTG[0:17, 0:128], idf[0:17, 0:17]),
                r=[rTSTG, ridf], w=[rinfl])
            kb.op("tensor", lambda e: e.transpose(
                itv[0:128, 17:34], TSTG[32:49, 0:128],
                idf[32:49, 32:49]),
                r=[rTSTG, ridf], w=[rinfl])
            kb.op("scalar", lambda e: e.activation(
                out=SLAB["SUMS"][:, 0:17], in_=itv[:, 0:17], func=AF.Copy),
                r=[rinfl], w=[rSLAB["SUMS"]])
            kb.op("scalar", lambda e: e.activation(
                out=SLAB["SQS"][:, 0:17], in_=itv[:, 17:34], func=AF.Copy),
                r=[rinfl], w=[rSLAB["SQS"]])
            kb.op("sync", lambda e: e.dma_start(
                out=SLAB["SUMS"][0:8, 1:TT + 1],
                in_=SLAB["XS"][120:128, 0:TT]),
                r=[rSLAB["XS"]], w=[rSLAB["SUMS"]], dma=True)
            kb.op("vector", lambda e: e.reduce_sum(
                SLAB["SUMS"][0:8, 0:1], eb[0][0:8, :], axis=AX),
                r=[reb[0]], w=[rSLAB["SUMS"]])
            kb.op("vector", lambda e: e.reduce_sum(
                SLAB["SQS"][0:8, 0:1], xnb[0][0:8, :], axis=AX),
                r=[rxnb[0]], w=[rSLAB["SQS"]])
            gn_ops = []
            _orig_op = kb.op
            kb.op = lambda *a, **k: gn_ops.append((a, k))
            for gc in range(GCH):
                gslice = guidb[:, gc * C:(gc + 1) * C]
                gp = gc % 2
                kb.op("gpsimd", lambda e, gs_=gslice, gp=gp: e.tensor_tensor(
                    out=sqb[gp][:], in0=gs_, in1=gs_, op=OP.mult),
                    r=[rguidb], w=[rsqb[gp]])
                kb.op("vector", lambda e, gc=gc, gp=gp: e.reduce_sum(
                    colD0(28 + gc), sqb[gp][:], axis=AX),
                    r=[rsqb[gp]], w=[rcolsD[0]])
            kb.op("vector", lambda e: e.tensor_scalar(
                out=colv(28), in0=colv(28), scalar1=1e-16,
                scalar2=None, op0=OP.max), r=[rcolsD[0]], w=[rcolsD[0]])
            kb.op("scalar", lambda e: e.activation(
                out=s4[:], in_=colv(28), func=AF.Ln),
                r=[rcolsD[0]], w=[rs4])
            kb.op("scalar", lambda e: e.activation(
                out=s4[:], in_=s4[:], func=AF.Exp, scale=-0.5),
                r=[rs4], w=[rs4])
            kb.op("vector", lambda e: e.tensor_mul(
                out=colv(24), in0=s4[:], in1=s4[:]),
                r=[rs4, rcolsD[0]], w=[rcolsD[0]])
            kb.op("vector", lambda e: e.scalar_tensor_tensor(
                out=colv(24), in0=colv(28), scalar=-0.5,
                in1=colv(24), op0=OP.mult, op1=OP.mult),
                r=[rcolsD[0]], w=[rcolsD[0]])
            kb.op("vector", lambda e: e.tensor_scalar(
                out=colv(24), in0=colv(24), scalar1=1.5,
                scalar2=None, op0=OP.add), r=[rcolsD[0]], w=[rcolsD[0]])
            kb.op("vector", lambda e: e.tensor_mul(
                out=s4[:], in0=s4[:], in1=colv(24)),
                r=[rs4, rcolsD[0]], w=[rs4])
            kb.op("vector", lambda e: e.tensor_scalar(
                out=s4b[:], in0=s4[:], scalar1=2.0, scalar2=None,
                op0=OP.mult), r=[rs4], w=[rs4b])
            kb.op("tensor", lambda e: e.transpose(
                nT_ps[0:4, 0:128], s4b[:], idb[:]),
                r=[rs4b, ridb], w=[rnT])
            kb.op("scalar", lambda e: e.activation(
                out=Sfull[0:4, 0:128], in_=nT_ps[0:4, 0:128], func=AF.Copy),
                r=[rnT], w=[rSfull])
            for gc in range(GCH):
                kb.op("scalar", lambda e, gc=gc: e.dma_start(
                    out=srow_d[0:1, gc * 128:(gc + 1) * 128],
                    in_=Sfull[gc:gc + 1, 0:128]),
                    r=[rSfull], w=[rsrow_d], dma=True)
            kb.op("scalar", lambda e: e.dma_start(
                out=Sfull[:], in_=srow_d.to_broadcast((128, G))),
                r=[rsrow_d], w=[rSfull], dma=True)
            for cc in range(CCH):
                sl = guidTs[:, cc * G:(cc + 1) * G]
                kb.op("vector", lambda e, sl=sl: e.tensor_mul(
                    out=sl, in0=sl, in1=Sfull[:]),
                    r=[rSfull, rguidTs], w=[rguidTs])
            kb.op = _orig_op

            def gdrip(n):
                for _ in range(min(n, len(gn_ops))):
                    a_, k_ = gn_ops.pop(0)
                    _orig_op(*a_, **k_)
            gdrip(5)


            kb.op("scalar", lambda e: e.dma_start(
                out=SLAB["SQS"][0:8, 1:TT + 1],
                in_=SLAB["XQ"][120:128, 0:TT]),
                r=[rSLAB["XQ"]], w=[rSLAB["SQS"]], dma=True)

            kb.op("vector", lambda e: e.tensor_scalar(
                out=SL("MS"), in0=SL("SUMS"), scalar1=INV_C, scalar2=None,
                op0=OP.mult), r=[rSLAB["SUMS"]], w=[rSLAB["MS"]])
            kb.op("vector", lambda e: e.tensor_mul(
                out=SL("T2S"), in0=SL("MS"), in1=SL("MS")),
                r=[rSLAB["MS"]], w=[rSLAB["T2S"]])
            kb.op("vector", lambda e: e.tensor_scalar(
                out=SL("V1"), in0=SL("SQS"), scalar1=INV_C, scalar2=None,
                op0=OP.mult), r=[rSLAB["SQS"]], w=[rSLAB["V1"]])
            kb.op("vector", lambda e: e.tensor_sub(
                out=SL("V1"), in0=SL("V1"), in1=SL("T2S")),
                r=[rSLAB["V1"], rSLAB["T2S"]], w=[rSLAB["V1"]])
            kb.op("vector", lambda e: e.tensor_scalar(
                out=SL("V1"), in0=SL("V1"), scalar1=1e-5, scalar2=None,
                op0=OP.add), r=[rSLAB["V1"]], w=[rSLAB["V1"]])
            kb.op("scalar", lambda e: e.activation(
                out=SL("RSTD"), in_=SL("V1"), func=AF.Ln),
                r=[rSLAB["V1"]], w=[rSLAB["RSTD"]])
            kb.op("scalar", lambda e: e.activation(
                out=SL("RSTD"), in_=SL("RSTD"), func=AF.Exp, scale=-0.5),
                r=[rSLAB["RSTD"]], w=[rSLAB["RSTD"]])
            gdrip(6)
            newton_rsqrt(SL("RSTD"), SL("V1"), rSLAB["RSTD"], SL("T2S"),
                         rextra=[rSLAB["V1"], rSLAB["T2S"]])
            kb.op("vector", lambda e: e.scalar_tensor_tensor(
                out=SL("NMR"), in0=SL("MS"), scalar=-1.0, in1=SL("RSTD"),
                op0=OP.mult, op1=OP.mult),
                r=[rSLAB["MS"], rSLAB["RSTD"]], w=[rSLAB["NMR"]])
            gdrip(5)
            kb.op("tensor", lambda e: e.transpose(
                dtv[0:17, 0:128], SLAB["RSTD"][:, 0:17], idf[:]),
                r=[rSLAB["RSTD"], ridf], w=[rdots])
            kb.op("tensor", lambda e: e.transpose(
                ntv[0:17, 0:128], SLAB["NMR"][:, 0:17], idf[:]),
                r=[rSLAB["NMR"], ridf], w=[rnum])
            kb.op("vector", lambda e: e.tensor_copy(
                out=TSTG[0:17, 0:127], in_=dtv[0:17, 1:128]),
                r=[rdots], w=[rTSTG])
            kb.op("vector", lambda e: e.tensor_copy(
                out=TSTG[32:49, 0:127], in_=ntv[0:17, 1:128]),
                r=[rnum], w=[rTSTG])
            kb.op("tensor", lambda e: e.transpose(
                itv[0:128, 0:17], TSTG[0:17, 0:128], idf[0:17, 0:17]),
                r=[rTSTG, ridf], w=[rinfl])
            kb.op("tensor", lambda e: e.transpose(
                itv[0:128, 17:34], TSTG[32:49, 0:128],
                idf[32:49, 32:49]),
                r=[rTSTG, ridf], w=[rinfl])
            kb.op("scalar", lambda e: e.activation(
                out=SLAB["RSTDS"][:, 0:17], in_=itv[:, 0:17], func=AF.Copy),
                r=[rinfl], w=[rSLAB["RSTDS"]])
            kb.op("scalar", lambda e: e.activation(
                out=SLAB["NMRS"][:, 0:17], in_=itv[:, 17:34], func=AF.Copy),
                r=[rinfl], w=[rSLAB["NMRS"]])
            gdrip(6)
            for nm in ("RSTD", "NMR"):
                kb.op("scalar", lambda e, nm=nm: e.dma_start(
                    out=SLAB[nm + "S"][127:128, 0:DTl - 1],
                    in_=SLAB[nm][0:1, 1:DTl]),
                    r=[rSLAB[nm]], w=[rSLAB[nm + "S"]], dma=True)

            def stats_load(d):
                par = d % 2
                kb.op("sync", lambda e, d=d, par=par: e.dma_start(
                    out=eb[par][:], in_=emb_r[128 * d:128 * (d + 1), :]),
                    w=[reb[par]], dma=True)
                kb.op("sync", lambda e, d=d, par=par: e.dma_start(
                    out=ebB[par][:],
                    in_=emb_r[128 * d + 1:128 * (d + 1) + 1, :]),
                    w=[rebB[par]], dma=True)

            def stats_comp_a(d):
                par = d % 2
                sl6 = d % 6
                kb.op("vector", lambda e, d=d, par=par: e.tensor_scalar(
                    out=xnb[par][:], in0=eb[par][:],
                    scalar1=SLAB["RSTD"][:, d:d + 1],
                    scalar2=SLAB["NMR"][:, d:d + 1],
                    op0=OP.mult, op1=OP.add),
                    r=[reb[par], rSLAB["RSTD"], rSLAB["NMR"]],
                    w=[rxnb[par]])
                kb.op("vector", lambda e, d=d, par=par: e.tensor_scalar(
                    out=xbs[par][:], in0=ebB[par][:],
                    scalar1=SLAB["RSTDS"][:, d:d + 1],
                    scalar2=SLAB["NMRS"][:, d:d + 1],
                    op0=OP.mult, op1=OP.add),
                    r=[rebB[par], rSLAB["RSTDS"], rSLAB["NMRS"]],
                    w=[rxbs[par]])
                kb.op("gpsimd", lambda e, par=par, sl6=sl6: e.tensor_tensor(
                    out=dbf6[sl6][:], in0=xbs[par][:], in1=xnb[par][:],
                    op=OP.subtract),
                    r=[rxbs[par], rxnb[par]], w=[rdbf6[sl6]])
                kb.op("gpsimd", lambda e, sl6=sl6, par=par: e.tensor_tensor(
                    out=sqb[par][:], in0=dbf6[sl6][:], in1=dbf6[sl6][:],
                    op=OP.mult),
                    r=[rdbf6[sl6]], w=[rsqb[par]])

            def stats_comp_a_fast(d):
                par = d % 2
                sl6 = d % 6
                kb.op("vector", lambda e, d=d, par=par: e.tensor_scalar(
                    out=xnb[par][:], in0=eb[par][:],
                    scalar1=SLAB["RSTD"][:, d:d + 1],
                    scalar2=SLAB["NMR"][:, d:d + 1],
                    op0=OP.mult, op1=OP.add),
                    r=[reb[par], rSLAB["RSTD"], rSLAB["NMR"]],
                    w=[rxnb[par]])
                kb.op("vector", lambda e, d=d, par=par: e.tensor_scalar(
                    out=xbs[par][:], in0=ebB[par][:],
                    scalar1=SLAB["RSTDS"][:, d:d + 1],
                    scalar2=SLAB["NMRS"][:, d:d + 1],
                    op0=OP.mult, op1=OP.add),
                    r=[rebB[par], rSLAB["RSTDS"], rSLAB["NMRS"]],
                    w=[rxbs[par]])
                kb.op("vector", lambda e, par=par, sl6=sl6: e.tensor_sub(
                    out=dbf6[sl6][:], in0=xbs[par][:], in1=xnb[par][:]),
                    r=[rxbs[par], rxnb[par]], w=[rdbf6[sl6]])
                kb.op("scalar", lambda e, sl6=sl6, par=par: e.activation(
                    out=sqb[par][:], in_=dbf6[sl6][:], func=AF.Square),
                    r=[rdbf6[sl6]], w=[rsqb[par]])

            def stats_comp_b(d):
                par = d % 2
                kb.op("vector", lambda e, d=d, par=par: e.reduce_sum(
                    SLAB["MAG2"][:, d:d + 1], sqb[par][:], axis=AX),
                    r=[rsqb[par]], w=[rSLAB["MAG2"]])

            def stats_comp(d):
                stats_comp_a(d)
                stats_comp_b(d)

            c1_done = [False] * TT

            def c1_stat(tt_):
                srcX = X[:, tt_ * C:(tt_ + 1) * C]
                par = (tt_ + 1) % 2
                kb.op("vector", lambda e, srcX=srcX, tt_=tt_: e.reduce_sum(
                    CS2["SUM2"][:, tt_:tt_ + 1], srcX, axis=AX),
                    r=[rX[tt_]], w=[rCS2["SUM2"]])
                kb.op("gpsimd", lambda e, srcX=srcX, par=par: e.tensor_tensor(
                    out=sqb[par][:], in0=srcX, in1=srcX, op=OP.mult),
                    r=[rX[tt_]], w=[rsqb[par]])
                kb.op("vector", lambda e, tt_=tt_, par=par: e.reduce_sum(
                    CS2["SQ2"][:, tt_:tt_ + 1], sqb[par][:], axis=AX),
                    r=[rsqb[par]], w=[rCS2["SQ2"]])
                c1_done[tt_] = True

            nb_hi = -1
            mag_hi = -1

            def ensure_nb(upto):
                nonlocal nb_hi
                for d in range(nb_hi + 1, min(upto, mag_hi) + 1):
                    normedb_one(d)
                    nb_hi = d

            gdrip(99)
            stats_load(0)
            stats_load(1)
            stats_comp_a_fast(0)
            stats_comp_b(0)
            stats_load(2)
            stats_comp_a_fast(1)
            stats_comp_b(1)
            stats_load(3)
            batch_mag_p1(0, 2)
            mag_hi = 1
            ensure_nb(1)
            s5_alpha(0)
            batch_mag_p2(0, 2)
            p1sched = {0: (2, 4), 1: (4, 6), 3: (6, 8), 5: (8, 10),
                       7: (10, 12), 9: (12, 14), 11: (14, 16),
                       12: (16, DTl)}
            p2sched = {i + 1: v for i, v in p1sched.items()}
            compsched = {0: [2, 3], 1: [4, 5]}
            loadsched = {0: [4, 5], 1: [6, 7]}
            for i in range(2, 13):
                compsched[i] = [4 + i]
                if 6 + i <= DTl - 1:
                    loadsched[i] = [6 + i]
            for i in range(0, DTl + 1):
                for d in compsched.get(i, []):
                    (stats_comp_a_fast if d < 4 else stats_comp_a)(d)
                for d in loadsched.get(i, []):
                    stats_load(d)
                ensure_nb(i + 2)
                if i + 1 <= nb_hi:
                    s5_alpha(i + 1)
                if i < DTl:
                    s5_mid(i)
                if 2 <= i and i - 2 < TT:
                    token_tile(i - 2)
                if i < DTl:
                    s5_fin(i)
                for d in compsched.get(i, []):
                    stats_comp_b(d)
                if 2 <= i and i - 2 < TT:
                    c1_stat(i - 2)
                if i < DTl:
                    s5_fin_b(i)
                if i in p1sched:
                    g0m, g1m = p1sched[i]
                    batch_mag_p1(g0m, g1m)
                    mag_hi = g1m - 1
                if i in p2sched:
                    batch_mag_p2(*p2sched[i])

            def SL2(nm):
                return SB2[nm][:, 0:TT]
            kb.op("vector", lambda e: e.tensor_scalar(
                out=SL2("MS2"), in0=CS2["SUM2"][:, 0:TT], scalar1=INV_C,
                scalar2=None,
                op0=OP.mult), r=[rCS2["SUM2"]], w=[rSB2["MS2"]])
            kb.op("vector", lambda e: e.tensor_mul(
                out=SL2("T2S2"), in0=SL2("MS2"), in1=SL2("MS2")),
                r=[rSB2["MS2"]], w=[rSB2["T2S2"]])
            kb.op("vector", lambda e: e.tensor_scalar(
                out=SL2("V12"), in0=CS2["SQ2"][:, 0:TT], scalar1=INV_C,
                scalar2=None,
                op0=OP.mult), r=[rCS2["SQ2"]], w=[rSB2["V12"]])
            kb.op("vector", lambda e: e.tensor_sub(
                out=SL2("V12"), in0=SL2("V12"), in1=SL2("T2S2")),
                r=[rSB2["V12"], rSB2["T2S2"]], w=[rSB2["V12"]])
            kb.op("vector", lambda e: e.tensor_scalar(
                out=SL2("V12"), in0=SL2("V12"), scalar1=1e-5, scalar2=None,
                op0=OP.add), r=[rSB2["V12"]], w=[rSB2["V12"]])
            kb.op("scalar", lambda e: e.activation(
                out=SL2("RSTD2"), in_=SL2("V12"), func=AF.Ln),
                r=[rSB2["V12"]], w=[rSB2["RSTD2"]])
            kb.op("scalar", lambda e: e.activation(
                out=SL2("RSTD2"), in_=SL2("RSTD2"), func=AF.Exp, scale=-0.5),
                r=[rSB2["RSTD2"]], w=[rSB2["RSTD2"]])
            newton_rsqrt(SL2("RSTD2"), SL2("V12"), rSB2["RSTD2"],
                         SL2("T2S2"), rextra=[rSB2["V12"], rSB2["T2S2"]])
            kb.op("vector", lambda e: e.scalar_tensor_tensor(
                out=SL2("NMR2"), in0=SL2("MS2"), scalar=-1.0,
                in1=SL2("RSTD2"), op0=OP.mult, op1=OP.mult),
                r=[rSB2["MS2"], rSB2["RSTD2"]], w=[rSB2["NMR2"]])


        kb.barrier()
        with ExitStack() as g2:
            def sb2(name, shape, dtype):
                return g2.enter_context(nc.sbuf_tensor(name, shape, dtype))

            def ps2(name, shape, dtype):
                return g2.enter_context(nc.psum_tensor(name, shape, dtype))

            gub = [sb2(f"gub{p}", [128, CCH * 1536], BF16) for p in (0, 1)]
            dnb = [sb2(f"dnb{p}", [128, HCB * C], BF16) for p in (0, 1)]

            def w_dma(b):
                par = b % 2
                kb.op("gpsimd", lambda e, b=b, par=par: e.dma_start(
                    out=gub[par][:].rearrange("p (cc h) -> p cc h", cc=CCH),
                    in_=wup.rearrange("(cc p) H -> p cc H", p=128)
                        [:, :, b * 1536:(b + 1) * 1536]),
                    w=[rgub[par]], dma=True)
                kb.op("gpsimd", lambda e, b=b, par=par: e.dma_start(
                    out=dnb[par][:].rearrange("p (hc c) -> p hc c", hc=HCB),
                    in_=wdn.rearrange("(hc p) c -> p hc c", p=128)
                        [:, b * HCB:(b + 1) * HCB, :]),
                    w=[rdnb[par]], dma=True)

            w_dma(0)
            w_dma(1)

            XT = sb2("XT", [128, CCH * TOK], BF16)
            xn2b = [sb2(f"xn2b{p}", [128, C], BF16) for p in (0, 1)]
            cbT = sb2("cbT", [16, TOK], BF16)
            cbB = [sb2(f"cbB{i}", [128, TOK], BF16) for i in range(6)]
            sg = [sb2(f"sg{p}", [128, 256], F32) for p in (0, 1)]
            hb = [sb2(f"hb{p}", [128, 256], BF16) for p in (0, 1)]
            hbb = [sb2(f"hbb{p}", [128, 256], BF16) for p in (0, 1)]

            acc_ps = [ps2(f"acc{p}", [128, C], F32) for p in (0, 1)]
            g_ps = [ps2(f"gps{p}", [128, 256], F32) for p in (0, 1)]
            u_ps = [ps2(f"ups{p}", [128, 256], F32) for p in (0, 1)]

            XTv = XT[:].rearrange("p (cc t) -> p cc t", cc=CCH)
            rXT = [res() for _ in range(TT)]
            rxn2b = [res(), res()]
            rcbT = res()
            rcbB = [res() for _ in range(6)]
            rsg = [res(), res()]
            rhb = [res(), res()]
            rhbb = [res(), res()]
            racc = [res(), res()]
            rg = [res(), res()]
            ru = [res(), res()]
            cb0f = cbB[0][:].bitcast(F32)
            cb1f = cbB[1][:].bitcast(F32)
            cb2f = cbB[2][:].bitcast(F32)
            LG = cb0f[:, 0:256]
            SCC = cb0f[:, 256:512]
            SBA = cb0f[:, 512:768]
            WRK = cb0f[:, 768:1024]
            GTA = cb1f[:, 64:128]
            GTB = cb1f[:, 128:192]
            GSA = cb1f[:, 0:64]
            T2C = cb1f[:, 192:208]
            MKC = cb1f[:, 208:224]
            THR = cb1f[:, 224:240]
            WSC = cb1f[:, 240:256]
            GMF = cb1f[:, 272:336]
            CMB = cb1f[:, 336:592]
            KMF = cb1f[:, 592:848]
            COMBB = cbB[2][:, 0:256]
            GMU8 = cbB[2][:].bitcast(mybir.dt.uint8)[:, 512:768]
            BIAS16 = cb2f[:, 256:512]
            NEGB = cb2f[:, 512:768]
            rcb0, rcb1, rcb2 = rcbB[0], rcbB[1], rcbB[2]
            rLGh = [res(), res()]
            rRH = [res(), res()]
            xn2f = cbB[4][:].bitcast(F32)[:, 0:C]
            rxn2f = rcbB[4]
            xn2fT = [cbB[5][:].bitcast(F32)[:, 0:C],
                     cbB[3][:].bitcast(F32)[:, 0:C]]
            rxn2fT = [rcbB[5], rcbB[3]]
            xtf_ps = acc_ps[0][:]

            xt_ps = acc_ps[0][:].bitcast(BF16)[:, 0:C]
            cbtr_ps = acc_ps[1][:].bitcast(BF16)[0:16, 0:128]
            lgt_ps = g_ps[0][:, 0:NE]

            kb.op("sync", lambda e: e.dma_start(
                out=BIAS16, in_=crow2[0:1, :].to_broadcast((128, 256))),
                w=[rcb2], dma=True)
            kb.op("sync", lambda e: e.dma_start(
                out=NEGB, in_=crow2[1:2, :].to_broadcast((128, 256))),
                w=[rcb2], dma=True)

            xtf_v = [acc_ps[0][:], acc_ps[1][:]]
            xt_v = [acc_ps[0][:].bitcast(BF16)[:, 0:C],
                    acc_ps[1][:].bitcast(BF16)[:, 0:C]]

            def c2_stage1(tt_):
                srcX = X[:, tt_ * C:(tt_ + 1) * C]
                par = tt_ % 2
                kb.op("vector", lambda e, srcX=srcX, tt_=tt_: e.tensor_scalar(
                    out=xn2f, in0=srcX,
                    scalar1=SB2["RSTD2"][:, tt_:tt_ + 1],
                    scalar2=SB2["NMR2"][:, tt_:tt_ + 1],
                    op0=OP.mult, op1=OP.add),
                    r=[rX[tt_], rSB2["RSTD2"], rSB2["NMR2"]], w=[rxn2f])
                kb.op("vector", lambda e, par=par: e.tensor_copy(
                    out=xn2b[par][:], in_=xn2f), r=[rxn2f], w=[rxn2b[par]])

            def c2_fpT(tt_):
                par = tt_ % 2
                for cc in range(CCH):
                    kb.op("tensor", lambda e, cc=cc, par=par: e.transpose(
                        xtf_v[par][:, cc * 128:(cc + 1) * 128],
                        xn2f[:, cc * 128:(cc + 1) * 128], idf[:]),
                        r=[rxn2f, ridf], w=[racc[par]])
                kb.op("scalar", lambda e, par=par: e.activation(
                    out=xn2fT[par], in_=xtf_v[par], func=AF.Copy),
                    r=[racc[par]], w=[rxn2fT[par]])

            def c2_bfT(tt_):
                par = tt_ % 2
                for cc in range(CCH):
                    kb.op("tensor", lambda e, cc=cc, par=par: e.transpose(
                        xt_v[par][:, cc * 128:(cc + 1) * 128],
                        xn2b[par][:, cc * 128:(cc + 1) * 128], idb[:]),
                        r=[rxn2b[par], ridb], w=[racc[par]])
                kb.op("scalar", lambda e, tt_=tt_, par=par: e.activation(
                    out=XTv[:, :, tt_ * 128:(tt_ + 1) * 128],
                    in_=xt_v[par][:].rearrange("p (cc t) -> p cc t", cc=CCH),
                    func=AF.Copy), r=[racc[par]], w=[rXT[tt_]])

            def c2_router(tt_):
                par = tt_ % 2
                for cc in range(CCH):
                    kb.op("tensor", lambda e, cc=cc, par=par: e.matmul(
                        lgt_ps, xn2fT[par][:, cc * 128:(cc + 1) * 128],
                        rtb[:, cc * NE:(cc + 1) * NE],
                        start=(cc == 0), stop=(cc == CCH - 1)),
                        r=[rxn2fT[par], rrtb], w=[rg[0]])
                kb.op("scalar", lambda e, tt_=tt_: e.activation(
                    out=LG[:, NE * tt_:NE * (tt_ + 1)], in_=lgt_ps,
                    func=AF.Copy), r=[rg[0]], w=[rLGh[tt_ // 8]])

            def router_half_ops(h):
                TH = TT // 2
                c0, c1 = h * 128, (h + 1) * 128
                LGh = LG[:, c0:c1]
                SCCh = SCC[:, c0:c1]
                SBAh = SBA[:, c0:c1]
                WRKh = WRK[:, c0:c1]
                CMBh = CMB[:, c0:c1]
                KMFh = KMF[:, c0:c1]
                COMBBh = COMBB[:, c0:c1]
                GMU8h = GMU8[:, c0:c1]
                BIAS16h = BIAS16[:, c0:c1]
                NEGBh = NEGB[:, c0:c1]
                GSAh = GSA[:, h * 32:(h + 1) * 32]
                GTAh = GTA[:, h * 32:(h + 1) * 32]
                GTBh = GTB[:, h * 32:(h + 1) * 32]
                GMFh = GMF[:, h * 32:(h + 1) * 32]
                T2Ch = T2C[:, h * 8:(h + 1) * 8]
                MKCh = MKC[:, h * 8:(h + 1) * 8]
                THRh = THR[:, h * 8:(h + 1) * 8]
                WSCh = WSC[:, h * 8:(h + 1) * 8]
                ops = []

                def rop(eng, fn):
                    ops.append(lambda eng=eng, fn=fn: kb.op(
                        eng, fn, r=[rLGh[h], rRH[h], rcb2],
                        w=[rLGh[h], rRH[h]]))

                def v3(ap):
                    return ap.rearrange("p (t e) -> p t e", e=NE)

                def g3(ap):
                    return ap.rearrange("p (t g) -> p t g", g=4)

                def bct(ap, n):
                    return ap.rearrange("p (t o) -> p t o", o=1).broadcast_to(
                        (128, TH, n))

                rop("scalar", lambda e: e.activation(
                    out=SCCh, in_=LGh, func=AF.Exp, scale=-1.0))
                rop("vector", lambda e: e.tensor_scalar(
                    out=SCCh, in0=SCCh, scalar1=1.0, scalar2=None, op0=OP.add))
                rop("vector", lambda e: e.reciprocal(out=LGh, in_=SCCh))
                rop("vector", lambda e: e.tensor_add(
                    out=SBAh, in0=LGh, in1=BIAS16h))
                sl = [SBAh[:, rr:128:4] for rr in range(4)]
                pairs = [(0, 1), (0, 2), (0, 3), (1, 2), (1, 3), (2, 3)]
                first = True
                for (a_, b_) in pairs:
                    dst = GTAh if first else GTBh
                    rop("vector", lambda e, a_=a_, b_=b_, dst=dst: e.tensor_add(
                        out=dst, in0=sl[a_], in1=sl[b_]))
                    if not first:
                        rop("vector", lambda e: e.tensor_max(
                            out=GTAh, in0=GTAh, in1=GTBh))
                    first = False
                rop("vector", lambda e: e.tensor_copy(out=GSAh, in_=GTAh))
                gl = [GSAh[:, gg:32:4] for gg in range(4)]
                first = True
                for (a_, b_) in pairs:
                    dst = T2Ch if first else MKCh
                    rop("vector", lambda e, a_=a_, b_=b_, dst=dst:
                        e.tensor_tensor(
                            out=dst, in0=gl[a_], in1=gl[b_], op=OP.min))
                    if not first:
                        rop("vector", lambda e: e.tensor_max(
                            out=T2Ch, in0=T2Ch, in1=MKCh))
                    first = False
                rop("vector", lambda e: e.tensor_tensor(
                    out=g3(GMFh), in0=g3(GSAh),
                    in1=T2Ch.rearrange("p (t o) -> p t o", o=1).broadcast_to(
                        (128, TH, 4)), op=OP.is_ge))
                rop("vector", lambda e: e.tensor_copy(
                    out=GMU8h.rearrange("p (t g r) -> p t g r", g=4, r=4),
                    in_=g3(GMFh).rearrange("p t (g o) -> p t g o", o=1)
                        .broadcast_to((128, TH, 4, 4))))
                rop("vector", lambda e: e.select(
                    out=WRKh, mask=GMU8h, on_true=SBAh, on_false=NEGBh))
                rop("vector", lambda e: e.tensor_copy(out=SCCh, in_=WRKh))
                for _ in range(3):
                    rop("vector", lambda e: e.reduce_max(
                        MKCh, v3(SCCh), axis=AX))
                    rop("vector", lambda e: e.tensor_tensor(
                        out=v3(CMBh), in0=v3(SCCh), in1=bct(MKCh, NE),
                        op=OP.is_ge))
                    rop("vector", lambda e: e.scalar_tensor_tensor(
                        out=SCCh, in0=CMBh, scalar=-1e30, in1=SCCh,
                        op0=OP.mult, op1=OP.add))
                rop("vector", lambda e: e.reduce_max(THRh, v3(SCCh), axis=AX))
                rop("vector", lambda e: e.tensor_tensor(
                    out=v3(KMFh), in0=v3(WRKh), in1=bct(THRh, NE),
                    op=OP.is_ge))
                rop("vector", lambda e: e.tensor_mul(
                    out=SCCh, in0=LGh, in1=KMFh))
                rop("vector", lambda e: e.reduce_sum(WSCh, v3(SCCh), axis=AX))
                rop("vector", lambda e: e.tensor_scalar(
                    out=WSCh, in0=WSCh, scalar1=1e-20, scalar2=None,
                    op0=OP.add))
                rop("vector", lambda e: e.reciprocal(out=T2Ch, in_=WSCh))
                rop("vector", lambda e: e.tensor_mul(
                    out=v3(CMBh), in0=v3(SCCh), in1=bct(T2Ch, NE)))
                rop("vector", lambda e: e.tensor_copy(out=COMBBh, in_=CMBh))
                return ops

            def emit_cbtr_half(h):
                for tt_ in range(h * 8, h * 8 + 8):
                    kb.op("tensor", lambda e, tt_=tt_: e.transpose(
                        cbtr_ps, COMBB[:, NE * tt_:NE * (tt_ + 1)], idb[:]),
                        r=[rRH[0], rRH[1], ridb], w=[racc[1]])
                    kb.op("vector", lambda e, tt_=tt_: e.tensor_copy(
                        out=cbT[:, tt_ * 128:(tt_ + 1) * 128], in_=cbtr_ps),
                        r=[racc[1]], w=[rcbT])
                    kb.op("vector", lambda e: e.tensor_copy(
                        out=s4[:, 0:1], in_=s4[:, 0:1]),
                        r=[racc[1]], w=[rs4])

            c2_stage1(0)
            c2_fpT(0)
            rh0 = None
            for tt_ in range(TT):
                if tt_ + 1 < TT:
                    c2_stage1(tt_ + 1)
                c2_bfT(tt_)
                if tt_ + 1 < TT:
                    c2_fpT(tt_ + 1)
                c2_router(tt_)
                if tt_ == 7:
                    rh0 = router_half_ops(0)
                elif tt_ > 7:
                    for _ in range(min(5, len(rh0))):
                        rh0.pop(0)()


            def owner(b, i):
                h0 = (b * HCB + i) * 128
                if h0 < 2048:
                    return None
                return (h0 - 2048) // 256

            blk_experts = []
            for b in range(NBLK):
                es = []
                for i in range(HCB):
                    e_ = owner(b, i)
                    if e_ is not None and e_ not in es:
                        es.append(e_)
                blk_experts.append(es)

            def cb_slot(b, e_):
                return (b % 2) * 3 + blk_experts[b].index(e_)

            def cbB_dma(b):
                for e_ in blk_experts[b]:
                    s_ = cb_slot(b, e_)
                    kb.op("sync", lambda e, e_=e_, s_=s_: e.dma_start(
                        out=cbB[s_][:],
                        in_=cbt_d[e_:e_ + 1, :].to_broadcast((128, TOK))),
                        r=[rcbt_d, rRH[0], rRH[1], rLGh[0], rLGh[1]],
                        w=[rcbB[s_]], dma=True)

            gubv = [gub[p][:].rearrange("p (cc h) -> p cc h", cc=CCH)
                    for p in (0, 1)]
            dnbv = [dnb[p][:].rearrange("p (hc c) -> p hc c", hc=HCB)
                    for p in (0, 1)]

            def emit_down(b, i, lh, rlh, bpar, t0, t1):
                for tl, tile in ((0, t0), (1, t1)):
                    for nh in range(2):
                        kb.op("tensor",
                              lambda e, tl=tl, nh=nh, i=i, lh=lh,
                              bpar=bpar: e.matmul(
                                  acc_ps[tl][:, nh * 512:(nh + 1) * 512],
                                  lh[:, tl * 128:(tl + 1) * 128],
                                  dnbv[bpar][:, i, nh * 512:(nh + 1) * 512],
                                  start=(i == 0), stop=(i == HCB - 1)),
                              r=[rlh, rdnb[bpar]], w=[racc[tl]])
                if i == HCB - 1:
                    for tl, tile in ((0, t0), (1, t1)):
                        xt = X[:, tile * C:(tile + 1) * C]
                        kb.op("vector", lambda e, xt=xt, tl=tl: e.tensor_add(
                            out=xt, in0=xt, in1=acc_ps[tl][:]),
                            r=[racc[tl], rX[tile]], w=[rX[tile]])
                        if b == NBLK - 1:
                            kb.op("sync", lambda e, xt=xt, tile=tile:
                                  e.dma_start(
                                      out=out_d[tile * 128:(tile + 1) * 128, :],
                                      in_=xt),
                                  r=[rX[tile]], dma=True)

            for f in rh0:
                f()
            emit_cbtr_half(0)
            for f in router_half_ops(1):
                f()
            emit_cbtr_half(1)
            kb.op("sync", lambda e: e.dma_start(out=cbt_d, in_=cbT[:]),
                  r=[rcbT], w=[rcbt_d], dma=True)
            cbB_dma(2)
            cbB_dma(3)

            for b in range(NBLK):
                bpar = b % 2
                pend = None
                for pair in range(NPAIR):
                    t0, t1 = 2 * pair, 2 * pair + 1
                    for i in range(HCB):
                        par = i % 2
                        e_ = owner(b, i)
                        for cc in range(CCH):
                            kb.op("tensor",
                                  lambda e, cc=cc, i=i, bpar=bpar, par=par,
                                  pair=pair: e.matmul(
                                      g_ps[par][:],
                                      gubv[bpar][:, cc, i * 128:(i + 1) * 128],
                                      XTv[:, cc, pair * 256:(pair + 1) * 256],
                                      start=(cc == 0), stop=(cc == CCH - 1)),
                                  r=[rgub[bpar], rXT[t0], rXT[t1]],
                                  w=[rg[par]])
                        for cc in range(CCH):
                            kb.op("tensor",
                                  lambda e, cc=cc, i=i, bpar=bpar, par=par,
                                  pair=pair: e.matmul(
                                      u_ps[par][:],
                                      gubv[bpar][:, cc,
                                                 768 + i * 128:768 + (i + 1) * 128],
                                      XTv[:, cc, pair * 256:(pair + 1) * 256],
                                      start=(cc == 0), stop=(cc == CCH - 1)),
                                  r=[rgub[bpar], rXT[t0], rXT[t1]],
                                  w=[ru[par]])
                        kb.op("scalar", lambda e, par=par: e.activation(
                            out=sg[par][:], in_=g_ps[par][:], func=AF.Silu),
                            r=[rg[par]], w=[rsg[par]])
                        kb.op("vector", lambda e, par=par: e.tensor_mul(
                            out=hb[par][:], in0=sg[par][:], in1=u_ps[par][:]),
                            r=[rsg[par], ru[par]], w=[rhb[par]])
                        if e_ is not None:
                            s_ = cb_slot(b, e_)
                            kb.op("vector",
                                  lambda e, par=par, s_=s_, pair=pair:
                                  e.tensor_mul(
                                      out=hbb[par][:], in0=hb[par][:],
                                      in1=cbB[s_][:, pair * 256:(pair + 1) * 256]),
                                  r=[rhb[par], rcbB[s_]], w=[rhbb[par]])
                            lh, rlh = hbb[par], rhbb[par]
                        else:
                            lh, rlh = hb[par], rhb[par]
                        if pend is not None:
                            pend()
                        pend = (lambda b=b, i=i, lh=lh, rlh=rlh, bpar=bpar,
                                t0=t0, t1=t1:
                                emit_down(b, i, lh, rlh, bpar, t0, t1))
                if pend is not None:
                    pend()
                    pend = None
                if b + 2 < NBLK:
                    w_dma(b + 2)
                    if b >= 2:
                        cbB_dma(b + 2)

            kb.finalize()
            with nc.Block() as block:
                kb.emit(block, sems, dsems, final=True)
    return nc


_NC_CACHE = {}


def _pw_mats():
    W = 8

    def lin(s, k):
        w = min(s, W)
        if not (s - W + k >= 0 and s >= 1):
            return 0.0
        if w > 1:
            j = k - (W - w)
            return 0.1 + 0.9 * j / max(w - 1, 1)
        return 0.1

    PA_g = np.zeros((128, 128), np.float32)
    PB = np.zeros((7, 128), np.float32)
    for sp in range(128):
        for k in range(8):
            dp = sp + k
            val = 0.1 + 0.9 * k / 7.0
            if dp < 128:
                PA_g[dp, sp] = val
            elif dp - 128 < 7:
                PB[dp - 128, sp] = val
    PA_s = np.zeros((128, 128), np.float32)
    for sp in range(128):
        for k in range(8):
            dp = sp + k
            if dp < 128:
                PA_s[dp, sp] = lin(sp, k)
    bf = ml_dtypes.bfloat16
    return PA_s.astype(bf), PA_g.astype(bf), PB.astype(bf)


def kernel(L0_attn_embeddings, L0_attn_guidance_trajectories,
           L0_ffn_router_w, L0_ffn_correction_bias,
           L0_ffn_gate_w, L0_ffn_up_w, L0_ffn_down_w,
           L0_ffn_shared_gate_w, L0_ffn_shared_up_w, L0_ffn_shared_down_w):
    f = np.float32
    emb = np.asarray(L0_attn_embeddings, f)
    guid = np.ascontiguousarray(np.asarray(L0_attn_guidance_trajectories, f))
    guidT = np.ascontiguousarray(guid.T)
    rT = np.ascontiguousarray(np.asarray(L0_ffn_router_w, f).T)
    biasr = np.ascontiguousarray(np.asarray(L0_ffn_correction_bias, f)
                                 .reshape(1, NE))
    gw = np.asarray(L0_ffn_gate_w, f)
    uw = np.asarray(L0_ffn_up_w, f)
    dw = np.asarray(L0_ffn_down_w, f)
    sgw = np.asarray(L0_ffn_shared_gate_w, f)
    suw = np.asarray(L0_ffn_shared_up_w, f)
    sdw = np.asarray(L0_ffn_shared_down_w, f)

    gate_all = np.concatenate(
        [sgw.T, gw.transpose(1, 0, 2).reshape(C, 4096)], axis=1)
    up_all = np.concatenate(
        [suw.T, uw.transpose(1, 0, 2).reshape(C, 4096)], axis=1)
    wup = np.empty((C, 2 * HTOT), f)
    for b in range(NBLK):
        wup[:, b * 1536:b * 1536 + 768] = gate_all[:, b * 768:(b + 1) * 768]
        wup[:, b * 1536 + 768:(b + 1) * 1536] = up_all[:, b * 768:(b + 1) * 768]
    wup = np.ascontiguousarray(wup)
    wdn = np.ascontiguousarray(
        np.concatenate([sdw.T, dw.reshape(4096, C)], axis=0))

    PA_s, PA_g, PB = _pw_mats()
    ident = np.eye(128, dtype=ml_dtypes.bfloat16)
    identf = np.eye(128, dtype=f)

    in_maps = []
    for c in range(NCORES):
        bb, half = c // 2, c % 2
        start = half * 2048
        z = np.zeros((R, C), f)
        lo = start - 8
        s0, s1 = max(0, lo), min(S, start + 2169)
        z[s0 - lo:s1 - lo] = emb[bb, s0:s1]
        crow = np.zeros((1, 32), f)
        crow[0, 4:8] = NEGBIG
        crow[0, 16:32] = NEGBIG
        crow2 = np.empty((2, 256), f)
        crow2[0] = np.tile(np.asarray(L0_ffn_correction_bias, f), 16)
        crow2[1] = NEGBIG
        in_maps.append(dict(
            emb=z, pwa_s=(PA_s if half == 0 else PA_g), pwa_g=PA_g, pwb=PB,
            idb=ident, idf=identf, guid=guid, guidT=guidT, rT=rT,
            biasr=biasr, crow=crow, crow2=crow2,
            zrow=np.zeros((1, 128), f),
            wup=wup, wdn=wdn))

    if "nc" not in _NC_CACHE:
        _NC_CACHE["nc"] = build_nc()
    nc = _NC_CACHE["nc"]
    res = run_bass_kernel_spmd(nc, in_maps, core_ids=list(range(NCORES)))
    out = np.empty((B, S, C), f)
    for c in range(NCORES):
        bb, half = c // 2, c % 2
        out[bb, half * 2048:(half + 1) * 2048] = res.results[c]["out"]
    return out
